# version 06
# baseline (speedup 1.0000x reference)
import numpy as np
import concourse.bass as bass
import concourse.mybir as mybir

F32 = mybir.dt.float32
BF16 = mybir.dt.bfloat16
I8 = mybir.dt.int8
I32 = mybir.dt.int32
U32 = mybir.dt.uint32
AF = mybir.ActivationFunctionType
ALU = mybir.AluOpType

ENGINES = ("sync", "tensor", "vector", "scalar", "gpsimd")
CHAN_DEBUG = False


class Prog:
    def __init__(self, nc, channels=()):
        self.nc = nc
        self.scope = None
        self.q = {e: [] for e in ENGINES}
        self._waited = {e: {} for e in ENGINES}
        self.cnt = {e: 0 for e in ENGINES}
        self.channels = list(channels)
        for c in self.channels:
            self.cnt[c] = 0
        self.sems = None

    def add_channel(self, c):
        if c not in self.cnt:
            self.channels.append(c)
            self.cnt[c] = 0

    def wait(self, e, ev):
        if ev is None:
            return
        pe, c = ev
        w = self._waited[e]
        if w.get(pe, -1) >= c:
            return
        w[pe] = c
        def cl(h, pe=pe, c=c):
            h.wait_ge(self.sems[pe], c)
        self.q[e].append((self.scope, cl))

    def waits(self, e, evs):
        best = {}
        for ev in evs:
            if ev is None:
                continue
            pe, c = ev
            if pe not in best or c > best[pe]:
                best[pe] = c
        for pe, c in best.items():
            self.wait(e, (pe, c))

    def run(self, e, fn, inc=None, selfwait=None):
        auto = e in ("vector", "scalar", "gpsimd")
        if inc is None:
            inc = 1 if auto else 0
        if selfwait is None:
            selfwait = auto
        if selfwait and self.cnt[e] > 0:
            self.wait(e, (e, self.cnt[e]))
        ev = None
        if inc:
            self.cnt[e] += inc
            ev = (e, self.cnt[e])
        def cl(h, fn=fn, inc=inc, e=e):
            ins = fn(h)
            if inc:
                ins.then_inc(self.sems[e], inc)
        self.q[e].append((self.scope, cl))
        return ev

    def dma(self, e, out, in_, chan=None, **kw):
        if chan is None:
            chan = "dma_" + e
        self.add_channel(chan)
        self.cnt[chan] += 16
        c = self.cnt[chan]
        def cl(h, out=out, in_=in_, kw=kw, chan=chan):
            h.dma_start(out=out, in_=in_, **kw).then_inc(self.sems[chan], 16)
        self.q[e].append((self.scope, cl))
        return (chan, c)

    def cinc(self, e, fn, chan, inc=16):
        self.add_channel(chan)
        self.cnt[chan] += inc
        c = self.cnt[chan]
        def cl(h, fn=fn, chan=chan, inc=inc):
            fn(h).then_inc(self.sems[chan], inc)
        self.q[e].append((self.scope, cl))
        return (chan, c)

    def dwait(self, e, chan):
        if chan in self.cnt and self.cnt[chan] > 0:
            self.wait(e, (chan, self.cnt[chan]))

    def ev(self, e):
        return (e, self.cnt[e])

    def fence(self, e):
        self.cnt[e] += 1
        c = self.cnt[e]
        def cl(h, c=c):
            h.nop().then_inc(self.sems[e], 1)
        self.q[e].append((self.scope, cl))
        return (e, c)

    def emit(self, extra_ctx=()):
        if CHAN_DEBUG:
            print("PROG sems: %d engines + %d channels = %d total; instrs: %s"
                  % (len(ENGINES), len(self.channels), len(ENGINES) + len(self.channels),
                     {e: len(self.q[e]) for e in ENGINES}))
        from contextlib import ExitStack
        nc = self.nc
        es = ExitStack()
        for c in extra_ctx:
            es.enter_context(c)
        self.sems = {e: es.enter_context(nc.semaphore(f"qsem_{e}"))
                     for e in list(ENGINES) + self.channels}
        block = es.enter_context(nc.Block())
        for e in ENGINES:
            closures = self.q[e]
            if not closures:
                continue
            def body(h, closures=closures):
                for lab, cl in closures:
                    if lab is None:
                        cl(h)
                    else:
                        with nc.named_scope(lab):
                            cl(h)
            getattr(block, e)(body)
        es.close()


class Tiles:
    def __init__(self, nc):
        from contextlib import ExitStack
        self.nc = nc
        self.es = ExitStack()
        self.handles = {}

    def sb(self, name, shape, dtype):
        h = self.es.enter_context(self.nc.sbuf_tensor(name, list(shape), dtype))
        self.handles[name] = h
        return h

    def ps(self, name, shape, dtype=F32):
        h = self.es.enter_context(self.nc.psum_tensor(name, list(shape), dtype))
        self.handles[name] = h
        return h

    def close(self):
        self.es.close()



class PhaseTiles(Tiles):
    def __init__(self, base):
        from contextlib import ExitStack
        self.nc = base.nc
        self.es = ExitStack()
        self.handles = base.handles


def barrier(p):
    evs = [(e, p.cnt[e]) for e in ENGINES if p.cnt[e] > 0]
    evs += [(ch, p.cnt[ch]) for ch in p.channels if p.cnt[ch] > 0]
    for e in ENGINES:
        p.waits(e, evs)


class ConstCache:
    def __init__(self, p, t, prefix="cc"):
        self.p, self.t, self.prefix = p, t, prefix
        self.vals = {}
        self.frozen = False

    def freeze(self):
        self.frozen = True

    def get(self, v):
        v = float(v)
        if v not in self.vals:
            if self.frozen:
                raise RuntimeError("ConstCache miss after freeze: add %r to prewarm" % v)
            tile = self.t.sb(f"{self.prefix}_{len(self.vals)}", [128, 1], F32)
            ev = self.p.run("gpsimd", lambda h, tile=tile, v=v: h.memset(tile[:], v), inc=1)
            self.vals[v] = (tile, ev)
        return self.vals[v]

    def ap(self, v):
        return self.get(v)[0][:, :]

    def ev(self, v):
        return self.get(v)[1]


def build_consts(p, t):
    ident = t.sb("c_ident", [128, 128], BF16)
    ones1 = t.sb("c_ones1", [1, 128], F32)
    iota_r = t.sb("c_iota_r", [128, 128], F32)
    iota_c = t.sb("c_iota_c", [128, 128], F32)
    identf = t.sb("c_identf", [128, 128], F32)
    ev0 = p.run("gpsimd", lambda h: h.iota(iota_r[:], pattern=[[0, 128]], base=0,
                                           channel_multiplier=1,
                                           allow_small_or_imprecise_dtypes=True))
    ev0 = p.run("gpsimd", lambda h: h.iota(iota_c[:], pattern=[[1, 128]], base=0,
                                           channel_multiplier=0,
                                           allow_small_or_imprecise_dtypes=True), inc=1)
    ev1 = p.run("gpsimd", lambda h: h.memset(ones1[:], 1.0), inc=1)
    p.wait("vector", ev1)
    p.run("vector", lambda h: h.tensor_tensor(identf[:], iota_r[:], iota_c[:], ALU.is_equal))
    evi = p.run("vector", lambda h: h.tensor_copy(ident[:], identf[:]), inc=1)
    return ident, ones1, evi


def transpose_128(p, psum_ap, sb_in_ap, ident, deps=(), inc=1):
    p.waits("tensor", deps)
    return p.run("tensor", lambda h: h.transpose(psum_ap, sb_in_ap, ident[:]), inc=inc)


def bcast_row(p, psum_ap, row_ap, ones1, n, deps=(), inc=1):
    p.waits("tensor", deps)
    return p.run("tensor", lambda h: h.matmul(psum_ap, ones1[:, :], row_ap,
                                              start=True, stop=True), inc=inc)


def _mk_build_l0attna():

    T = 4096
    D = 1024
    HID = 2048
    IB = 512
    NT = T // 128


    def build_l0attna(p, t, cc, consts, x_in, x_out, v_scr, w, name="a0a"):
        nc = p.nc
        ident, ones1, ev_ident = consts
        nt = T // 128
        nib = T // IB
        ntib = IB // 128

        whid = t.sb(name + "_whid", [128, 8, HID], BF16)
        wout = t.sb(name + "_wout", [128, 16, D], BF16)
        wqk = t.sb(name + "_wqk", [128, 8, 128], BF16)
        QT = t.sb(name + "_QT", [128, T], BF16)
        KT = t.sb(name + "_KT", [128, T], BF16)
        attnT = t.sb(name + "_attnT", [128, nt, IB], BF16)
        gateT = t.sb(name + "_gateT", [128, 16, IB], BF16)
        bhv_rep = t.sb(name + "_bhvrep", [128, HID], F32)
        bh2col = t.sb(name + "_bh2col", [128, 16], F32)
        bout_rep = t.sb(name + "_boutrep", [128, D], F32)
        lns = t.sb(name + "_lns", [128, 8], F32)
        lnb = t.sb(name + "_lnb", [128, 8], F32)
        bqk = t.sb(name + "_bqk", [128, 1], F32)
        osw = t.sb(name + "_osw", [128, 2], F32)
        osb = t.sb(name + "_osb", [128, 2], F32)
        rstd_all = t.sb(name + "_rstdall", [128, nt], F32)
        nbias_all = t.sb(name + "_nbiasall", [128, nt], F32)

        xt2 = [t.sb(name + "_x0", [128, D], F32), t.sb(name + "_x1", [128, D], F32)]
        xsq = t.sb(name + "_xsq", [128, D], F32)
        xnbf = t.sb(name + "_xnbf", [128, D], BF16)
        xnT = t.sb(name + "_xnT", [128, 8, 128], BF16)
        xnTi = t.sb(name + "_xnTi", [128, 8, IB], BF16)
        qks = t.sb(name + "_qks", [128, 128], F32)
        vtmp = t.sb(name + "_vtmp", [128, HID], F32)
        vbf = t.sb(name + "_vbf", [128, HID], BF16)
        vt = [t.sb(name + "_vt0", [128, 1024], BF16), t.sb(name + "_vt1", [128, 1024], BF16)]
        stmp2 = [t.sb(name + "_stmp0", [128, IB], F32), t.sb(name + "_stmp1", [128, IB], F32)]
        xib = [t.sb(name + "_xib%d" % k, [128, D], F32) for k in range(ntib)]
        s1 = t.sb(name + "_s1", [128, 1], F32)
        s2 = t.sb(name + "_s2", [128, 1], F32)
        mean = t.sb(name + "_mean", [128, 1], F32)
        m2 = t.sb(name + "_m2", [128, 1], F32)
        var = t.sb(name + "_var", [128, 1], F32)

        pool = t.ps(name + "_pool", [128, 7, 512], F32)
        ptr = t.ps(name + "_ptr", [128, 1024], BF16)
        pqk = pool[:, 6, 0:128]
        pv = pool[:, 4:6, :].rearrange("p a b -> p (a b)")
        pout = pool[:, 0:4, :].rearrange("p a b -> p (a b)").rearrange(
            "p (c i) -> p c i", c=8)

        cwb = name + "cwb"
        def load_col(dst, src1d, ncol):
            for c in range(ncol):
                p.dma("sync", dst[:, c:c + 1],
                      src1d[c * 128:(c + 1) * 128].rearrange("(p o) -> p o", o=1), chan=cwb)
        load_col(lns, w["ln_scale"], 8)
        load_col(lnb, w["ln_bias"], 8)
        p.dma("sync", bqk[:, :], w["b_qk"].rearrange("(p o) -> p o", o=1), chan=cwb)
        for s in range(2):
            p.dma("sync", osw[:, s:s + 1], w["os_weight"][s, :].rearrange("(p o) -> p o", o=1), chan=cwb)
            p.dma("sync", osb[:, s:s + 1], w["os_bias"][s, :].rearrange("(p o) -> p o", o=1), chan=cwb)
        load_col(bh2col, w["b_hidden"][HID:], 16)
        brow = vtmp[0:1, 0:HID]
        p.dma("sync", brow[:, :], w["b_hidden"][0:HID].rearrange("(o d) -> o d", o=1), chan=cwb)
        borow = xsq[0:1, 0:D]
        p.dma("sync", borow[:, :], w["b_out"].rearrange("(o d) -> o d", o=1), chan=cwb)
        p.dwait("tensor", cwb)
        p.waits("tensor", [ev_ident])
        for nb in range(4):
            evb = p.run("tensor", lambda h, nb=nb: h.matmul(
                pv[:, (nb % 2) * 512:(nb % 2) * 512 + 512], ones1[:, :],
                brow[:, nb * 512:(nb + 1) * 512], start=True, stop=True), inc=1)
            p.wait("vector", evb)
            ev_bh = p.run("vector", lambda h, nb=nb: h.tensor_copy(
                bhv_rep[:, nb * 512:(nb + 1) * 512],
                pv[:, (nb % 2) * 512:(nb % 2) * 512 + 512]))
            p.wait("tensor", ev_bh)
        for nb in range(2):
            evb = p.run("tensor", lambda h, nb=nb: h.matmul(
                pv[:, nb * 512:nb * 512 + 512], ones1[:, :],
                borow[:, nb * 512:(nb + 1) * 512], start=True, stop=True), inc=1)
            p.wait("vector", evb)
            ev_bo = p.run("vector", lambda h, nb=nb: h.tensor_copy(
                bout_rep[:, nb * 512:(nb + 1) * 512], pv[:, nb * 512:nb * 512 + 512]))
            p.wait("tensor", ev_bo)

        cwh = name + "cwh"
        p.waits("sync", [evb])
        p.dma("sync", vtmp[:, :1024].rearrange("p (c q) -> p c q", c=8), w["W_qk"].rearrange("(c p) q -> p c q", p=128), chan=cwh)
        p.dwait("vector", cwh)
        evc = p.run("vector", lambda h: h.tensor_copy(
            wqk[:, :, :], vtmp[:, :1024].rearrange("p (c q) -> p c q", c=8)))
        for c in range(8):
            p.waits("sync", [evc])
            p.dma("sync", vtmp[:, :], w["W_hidden"][c * 128:(c + 1) * 128, 0:HID], chan=cwh)
            p.dwait("vector", cwh)
            evc = p.run("vector", lambda h, c=c: h.tensor_copy(whid[:, c, :], vtmp[:, :]))
        for c in range(16):
            p.waits("sync", [evc])
            p.dma("sync", vtmp[:, :1024], w["W_out"][c * 128:(c + 1) * 128, :], chan=cwh)
            p.dwait("vector", cwh)
            evc = p.run("vector", lambda h, c=c: h.tensor_copy(wout[:, c, :], vtmp[:, :1024]))
        ev_wpre = evc

        ev_xuse = [ev_wpre, ev_wpre]
        ev_ptuse = ev_wpre
        ev_tr_last = None
        ev_pqkeva = None
        ev_pveva = None
        ev_vbf_rd = None
        ev_qt = None
        for i in range(nt):
            par = i % 2
            xt = xt2[par]
            cx = name + "cxa%d" % par
            p.waits("sync", [ev_xuse[par]])
            p.dma("sync", xt[:, :], x_in[i * 128:(i + 1) * 128, :], chan=cx)
            p.dwait("scalar", cx)
            ev_sq = p.run("scalar", lambda h, xt=xt: h.square(xsq[:], xt[:]))
            p.dwait("vector", cx)
            p.run("vector", lambda h, xt=xt: h.reduce_sum(s1[:], xt[:], axis=mybir.AxisListType.X))
            p.waits("vector", [ev_sq])
            p.run("vector", lambda h: h.reduce_sum(s2[:], xsq[:], axis=mybir.AxisListType.X))
            p.run("vector", lambda h: h.tensor_scalar_mul(mean[:], s1[:], 1.0 / D))
            p.run("vector", lambda h: h.tensor_scalar_mul(s2[:], s2[:], 1.0 / D))
            p.run("vector", lambda h: h.tensor_tensor(m2[:], mean[:], mean[:], ALU.mult))
            ev_v = p.run("vector", lambda h: h.tensor_tensor(var[:], s2[:], m2[:], ALU.subtract))
            p.waits("scalar", [ev_v, cc.ev(1e-10)])
            p.run("scalar", lambda h, i=i: h.activation(out=rstd_all[:, i:i + 1], in_=var[:],
                                                        func=AF.Ln, scale=1.0 + 1e-5,
                                                        bias=cc.ap(1e-10)))
            ev_s = p.run("scalar", lambda h, i=i: h.activation(
                out=rstd_all[:, i:i + 1], in_=rstd_all[:, i:i + 1], func=AF.Exp, scale=-0.5))
            p.waits("vector", [ev_s])
            ev_nb = p.run("vector", lambda h, i=i: h.scalar_tensor_tensor(
                nbias_all[:, i:i + 1], mean[:], -1.0, rstd_all[:, i:i + 1], ALU.mult, ALU.mult))
            p.waits("scalar", [ev_nb, ev_tr_last])
            ev_xn = p.run("scalar", lambda h, xt=xt, i=i: h.activation(
                out=xnbf[:], in_=xt[:], func=AF.Identity,
                scale=rstd_all[:, i:i + 1], bias=nbias_all[:, i:i + 1]))
            ev_xuse[par] = ev_xn
            p.waits("tensor", [ev_xn, ev_ptuse])
            for dc in range(8):
                ev_tr = p.run("tensor", lambda h, dc=dc: h.transpose(
                    ptr[:, dc * 128:(dc + 1) * 128], xnbf[:, dc * 128:(dc + 1) * 128], ident[:]),
                    inc=(dc == 7))
            ev_tr_last = ev_tr
            p.waits("scalar", [ev_tr])
            for dc in range(8):
                ev_ptuse = p.run("scalar", lambda h, dc=dc: h.activation(
                    out=xnT[:, dc, :], in_=ptr[:, dc * 128:(dc + 1) * 128], func=AF.Identity,
                    scale=lns[:, dc:dc + 1], bias=lnb[:, dc:dc + 1]))
            p.waits("tensor", [ev_ptuse, ev_pqkeva])
            for dc in range(8):
                ev_mmqk = p.run("tensor", lambda h, dc=dc: h.matmul(
                    pqk, wqk[:, dc, :], xnT[:, dc, :],
                    start=(dc == 0), stop=(dc == 7)), inc=(dc == 7))
            p.waits("scalar", [ev_mmqk])
            ev_qks = p.run("scalar", lambda h: h.activation(out=qks[:], in_=pqk,
                                                            func=AF.Silu, bias=bqk[:, :]))
            ev_pqkeva = ev_qks
            p.waits("vector", [ev_qks])
            p.run("vector", lambda h, i=i: h.tensor_scalar(
                QT[:, i * 128:(i + 1) * 128], qks[:], osw[:, 0:1], osb[:, 0:1], ALU.mult, ALU.add))
            ev_qt = p.run("vector", lambda h, i=i: h.tensor_scalar(
                KT[:, i * 128:(i + 1) * 128], qks[:], osw[:, 1:2], osb[:, 1:2], ALU.mult, ALU.add))
            for nb in range(4):
                pvslice = pv[:, (nb % 2) * 512:(nb % 2) * 512 + 512]
                p.waits("tensor", [ev_pveva, ev_ptuse])
                for dc in range(8):
                    ev_mmv = p.run("tensor", lambda h, dc=dc, pvslice=pvslice, nb=nb: h.matmul(
                        pvslice, xnT[:, dc, :], whid[:, dc, nb * 512:(nb + 1) * 512],
                        start=(dc == 0), stop=(dc == 7)), inc=(dc == 7))
                p.waits("vector", [ev_mmv, ev_vbf_rd])
                ev_pveva = p.run("vector", lambda h, nb=nb, pvslice=pvslice: h.tensor_tensor(
                    vtmp[:, nb * 512:(nb + 1) * 512], pvslice,
                    bhv_rep[:, nb * 512:(nb + 1) * 512], ALU.add))
            p.waits("scalar", [ev_pveva])
            p.dwait("scalar", name + "cvs")
            ev_vbf = p.run("scalar", lambda h: h.activation(out=vbf[:], in_=vtmp[:], func=AF.Silu))
            ev_vbf_rd = ev_vbf
            p.waits("sync", [ev_vbf])
            p.dma("sync", v_scr[i * 128:(i + 1) * 128, :], vbf[:, :], chan=name + "cvs")

        p.waits("vector", [ev_qt])
        evc2 = None
        for c in range(8):
            p.waits("sync", [evc2, ev_vbf])
            p.dma("sync", vtmp[:, :], w["W_hidden"][c * 128:(c + 1) * 128, HID:], chan=cwh)
            p.dwait("vector", cwh)
            p.waits("vector", [ev_mmv])
            evc2 = p.run("vector", lambda h, c=c: h.tensor_copy(whid[:, c, :], vtmp[:, :]))
        ev_whid_g = evc2

        ev_epi = None
        ev_xnTi_use = None
        ev_gate_all = None
        ev_acc_all = None
        ev_at = None
        ev_ogt = None
        ev_vt_rd = [None, None]
        ev_rel_par = [None, None]
        ev_sq_par = [None, None]
        ev_md = None
        for ib in range(nib):
            i0 = ib * IB
            for k in range(ntib):
                it = ib * ntib + k
                cx = name + "cxb%d" % k
                p.waits("sync", [ev_epi])
                p.dma("sync", xib[k][:, :], x_in[it * 128:(it + 1) * 128, :], chan=cx)
                p.dwait("scalar", cx)
                p.waits("scalar", [ev_tr_last])
                ev_xn = p.run("scalar", lambda h, k=k, it=it: h.activation(
                    out=xnbf[:], in_=xib[k][:], func=AF.Identity,
                    scale=rstd_all[:, it:it + 1], bias=nbias_all[:, it:it + 1]))
                p.waits("tensor", [ev_xn, ev_ptuse, ev_xnTi_use])
                for dc in range(8):
                    ev_tr = p.run("tensor", lambda h, dc=dc: h.transpose(
                        ptr[:, dc * 128:(dc + 1) * 128], xnbf[:, dc * 128:(dc + 1) * 128],
                        ident[:]), inc=(dc == 7))
                ev_tr_last = ev_tr
                p.waits("scalar", [ev_tr])
                for dc in range(8):
                    ev_ptuse = p.run("scalar", lambda h, dc=dc, k=k: h.activation(
                        out=xnTi[:, dc, k * 128:(k + 1) * 128],
                        in_=ptr[:, dc * 128:(dc + 1) * 128], func=AF.Identity,
                        scale=lns[:, dc:dc + 1], bias=lnb[:, dc:dc + 1]))
            ev_xnTi = ev_ptuse
            for g in range(16):
                pvslice = pv[:, (g % 2) * 512:(g % 2) * 512 + IB]
                p.waits("tensor", [ev_xnTi, ev_gate_all, ev_epi, ev_whid_g])
                for dc in range(8):
                    ev_mg = p.run("tensor", lambda h, dc=dc, g=g, pvslice=pvslice: h.matmul(
                        pvslice, whid[:, dc, g * 128:(g + 1) * 128], xnTi[:, dc, :],
                        start=(dc == 0), stop=(dc == 7)), inc=(dc == 7))
                p.waits("scalar", [ev_mg])
                ev_gate_all = p.run("scalar", lambda h, g=g, pvslice=pvslice: h.activation(
                    out=gateT[:, g, :], in_=pvslice, func=AF.Silu, bias=bh2col[:, g:g + 1]))
            for j in range(nt):
                par = j % 2
                psim = pv[:, par * 512:par * 512 + IB]
                p.waits("tensor", [ev_qt, ev_rel_par[par],
                                   ev_gate_all if j <= 1 else None])
                ev_sim = p.run("tensor", lambda h, j=j, i0=i0, psim=psim: h.matmul(
                    psim, KT[:, j * 128:(j + 1) * 128], QT[:, i0:i0 + IB],
                    start=True, stop=True), inc=1)
                p.waits("vector", [ev_sim, ev_sq_par[par]])
                ev_rel_par[par] = p.run("vector", lambda h, psim=psim, par=par:
                                        h.tensor_scalar_max(stmp2[par][:, :], psim, 0.0))
                p.waits("scalar", [ev_rel_par[par], ev_acc_all])
                ev_at = p.run("scalar", lambda h, j=j, par=par: h.activation(
                    out=attnT[:, j, :], in_=stmp2[par][:, :], func=AF.Square, scale=1.0 / T))
                ev_sq_par[par] = ev_at
            ev_attnT = ev_at
            p.dwait("sync", name + "cvs")
            for quarter in range(4):
                for j in range(nt):
                    vpar = j % 2
                    cv = name + "cv%d" % vpar
                    p.waits("sync", [ev_vt_rd[vpar]])
                    p.dma("sync", vt[vpar][:, 0:512],
                          v_scr[j * 128:(j + 1) * 128, quarter * 512:(quarter + 1) * 512], chan=cv)
                    p.dwait("tensor", cv)
                    p.waits("tensor", [ev_attnT, ev_ogt])
                    for vc in range(4):
                        ev_acc = p.run("tensor", lambda h, j=j, vc=vc, vpar=vpar: h.matmul(
                            pool[:, vc, 0:IB], vt[vpar][:, vc * 128:(vc + 1) * 128],
                            attnT[:, j, :], start=(j == 0), stop=(j == nt - 1)),
                            inc=(vc == 3))
                    ev_vt_rd[vpar] = ev_acc
                ev_acc_all = ev_acc
                p.waits("vector", [ev_acc, ev_gate_all])
                for vc in range(4):
                    g = quarter * 4 + vc
                    ev_ogt = p.run("vector", lambda h, vc=vc, g=g: h.scalar_tensor_tensor(
                        gateT[:, g, :], pool[:, vc, 0:IB], 1.0, gateT[:, g, :], ALU.bypass, ALU.mult))
            for k in range(ntib):
                it = ib * ntib + k
                p.waits("tensor", [ev_ogt, ev_epi])
                for g in range(16):
                    for nb2 in range(2):
                        ev_md = p.run("tensor", lambda h, g=g, nb2=nb2, k=k: h.matmul(
                            pv[:, nb2 * 512:(nb2 + 1) * 512],
                            gateT[:, g, k * 128:(k + 1) * 128],
                            wout[:, g, nb2 * 512:(nb2 + 1) * 512],
                            start=(g == 0), stop=(g == 15)), inc=(g == 15 and nb2 == 1))
                p.waits("vector", [ev_md])
                p.dwait("vector", name + "cso")
                p.run("vector", lambda h: h.tensor_tensor(vtmp[:, :1024], pv, bout_rep[:, :], ALU.add))
                ev_xo = p.run("vector", lambda h, k=k: h.tensor_tensor(
                    vtmp[:, :1024], vtmp[:, :1024], xib[k][:], ALU.add))
                ev_epi = ev_xo
                p.waits("sync", [ev_xo])
                p.dma("sync", x_out[it * 128:(it + 1) * 128, :], vtmp[:, :1024], chan=name + "cso")
        p.dwait("sync", name + "cso")

    return build_l0attna

build_l0attna = _mk_build_l0attna()


def _mk_build_l0ffn():

    T = 4096
    D = 1024
    NT = T // 128
    MAGIC = 12582912.0


    def build_l0ffn(p, t, cc, x_in, x_out, wq, wscale, wbias, name="ffn", consts=None):
        nc = p.nc
        p.add_channel(name + "cst")
        ident, ones1, ev_ident = consts if consts is not None else build_consts(p, t)

        wq8 = t.sb(name + "_wq8", [128, D], I8)
        wqbf = t.sb(name + "_wqbf", [128, D], BF16)
        wqT = t.sb(name + "_wqT", [128, 8, D], BF16)
        wsrow = t.sb(name + "_wsrow", [1, D], F32)
        brow = t.sb(name + "_brow", [1, D], F32)
        wsrep = t.sb(name + "_wsrep", [128, D], F32)
        brep = t.sb(name + "_brep", [128, D], F32)

        pt = t.ps(name + "_pt", [128, 1024], BF16)
        pb = t.ps(name + "_pb", [128, 1024], F32)
        acc = t.ps(name + "_acc", [128, 1024], F32)

        p.dma("sync", wsrow[:, :], wscale.rearrange("(o d) -> o d", o=1), chan=name+"cwb")
        p.dma("sync", brow[:, :], wbias.rearrange("(o d) -> o d", o=1), chan=name+"cwb")
        p.dwait("tensor", name+"cwb")
        p.waits("tensor", [ev_ident])
        for nb in range(2):
            evt = p.run("tensor", lambda h, nb=nb: h.matmul(pb[:, nb * 512:(nb + 1) * 512],
                                                      ones1[:, :], wsrow[:, nb * 512:(nb + 1) * 512],
                                                      start=True, stop=True), inc=(nb == 1))
        p.wait("vector", evt)
        ev_wsrep = p.run("vector", lambda h: h.tensor_copy(wsrep[:], pb[:, :]), inc=1)
        p.wait("tensor", ev_wsrep)
        for nb in range(2):
            evt = p.run("tensor", lambda h, nb=nb: h.matmul(pb[:, nb * 512:(nb + 1) * 512],
                                                      ones1[:, :], brow[:, nb * 512:(nb + 1) * 512],
                                                      start=True, stop=True), inc=(nb == 1))
        p.wait("vector", evt)
        ev_brep = p.run("vector", lambda h: h.tensor_copy(brep[:], pb[:, :]), inc=1)

        ev_copy = None
        evc = None
        evt = None
        for oc in range(8):
            p.waits("sync", [evc])
            p.dma("sync", wq8[:, :], wq[oc * 128:(oc + 1) * 128, :], chan=name+"cw")
            p.dwait("vector", name+"cw")
            p.waits("vector", [evt])
            evc = p.run("vector", lambda h: h.tensor_copy(wqbf[:], wq8[:]), inc=1)
            p.waits("tensor", [evc, ev_copy, ev_brep])
            for dc in range(8):
                evt = p.run("tensor", lambda h, dc=dc: h.transpose(
                    pt[:, dc * 128:(dc + 1) * 128], wqbf[:, dc * 128:(dc + 1) * 128], ident[:]),
                    inc=(dc == 7))
            p.wait("vector", evt)
            ev_copy = p.run("vector", lambda h, oc=oc: h.tensor_copy(
                wqT[:, :, oc * 128:(oc + 1) * 128].rearrange("p c o -> p c o"),
                pt[:, :].rearrange("p (c o) -> p c o", c=8)), inc=1)

        xb = [t.sb(name + "_x0", [128, D], F32), t.sb(name + "_x1", [128, D], F32)]
        xsqp = [t.sb(name + "_xsq%d" % q, [128, D], F32) for q in range(2)]
        xnp = [t.sb(name + "_xn%d" % q, [128, D], F32) for q in range(2)]
        xqfp = [t.sb(name + "_xqf%d" % q, [128, D], F32) for q in range(2)]
        xqbfp = [t.sb(name + "_xqbf%d" % q, [128, D], BF16) for q in range(2)]
        xqTp = [t.sb(name + "_xqT%d" % q, [128, 8, 128], BF16) for q in range(2)]
        t1p = [t.sb(name + "_t1%d" % q, [128, D], F32) for q in range(2)]
        xop = [t.sb(name + "_xo%d" % q, [128, D], F32) for q in range(2)]
        stp = [t.sb(name + "_st%d" % q, [128, 10], F32) for q in range(2)]
        pt2 = t.ps(name + "_pt2", [128, 1024], BF16)
        ptq = [pt, pt2]
        accq = [acc, pb]

        ev_tile_done = [ev_copy, ev_copy]
        ev_pt_used_par = [ev_copy, ev_copy]
        ev_epi_par = [ev_brep, ev_brep]
        ev_mm_par = [None, None]

        def tile_body(i, par, xt, xsq, xn, xqf, xqbf, xqT, t1, xo, st, ptb, accb):
            nonlocal ev_tile_done, ev_pt_used_par, ev_epi_par, ev_mm_par
            s1 = st[:, 0:1]
            s2 = st[:, 1:2]
            mean = st[:, 2:3]
            m2 = st[:, 3:4]
            var = st[:, 4:5]
            rstd = st[:, 5:6]
            nbias = st[:, 6:7]
            amax = st[:, 7:8]
            xs = st[:, 8:9]
            rxs = st[:, 9:10]
            gd = ev_tile_done[par]
            p.waits("sync", [gd])
            cx = name + "cx%d" % par
            p.dma("sync", xt[:, :], x_in[i * 128:(i + 1) * 128, :], chan=cx)
            p.dwait("scalar", cx)
            p.waits("scalar", [gd])
            ev_sq = p.run("scalar", lambda h: h.square(xsq[:], xt[:]), inc=1)
            p.dwait("vector", cx)
            p.waits("vector", [gd])
            p.run("vector", lambda h: h.reduce_sum(s1, xt[:], axis=mybir.AxisListType.X))
            p.waits("vector", [ev_sq])
            ev_v = p.run("vector", lambda h: h.reduce_sum(s2, xsq[:], axis=mybir.AxisListType.X), inc=1)
            p.wait("vector", ev_v)
            p.run("vector", lambda h: h.tensor_scalar_mul(mean, s1, 1.0 / D))
            p.run("vector", lambda h: h.tensor_scalar_mul(s2, s2, 1.0 / D))
            p.run("vector", lambda h: h.tensor_tensor(m2, mean, mean, ALU.mult))
            ev_v = p.run("vector", lambda h: h.tensor_tensor(var, s2, m2, ALU.subtract), inc=1)
            p.waits("scalar", [ev_v, cc.ev(1e-5)])
            p.run("scalar", lambda h: h.activation(out=rstd, in_=var, func=AF.Ln, bias=cc.ap(1e-5)))
            ev_s = p.run("scalar", lambda h: h.activation(out=rstd, in_=rstd, func=AF.Exp, scale=-0.5), inc=1)
            p.waits("vector", [ev_s])
            ev_nb = p.run("vector", lambda h: h.scalar_tensor_tensor(
                nbias, mean, -1.0, rstd, ALU.mult, ALU.mult), inc=1)
            p.waits("scalar", [ev_nb])
            ev_xn = p.run("scalar", lambda h: h.activation(out=xn[:], in_=xt[:], func=AF.Identity,
                                                           scale=rstd, bias=nbias), inc=1)
            p.waits("vector", [ev_xn])
            ev_am = p.run("vector", lambda h: h.tensor_reduce(
                amax, xn[:], axis=mybir.AxisListType.X, op=ALU.max, apply_absolute_value=True), inc=1)
            p.wait("vector", ev_am)
            p.run("vector", lambda h: h.tensor_scalar(xs, amax, 1e-6, 1.0 / 127.0, ALU.max, ALU.mult))
            ev_rxs = p.run("vector", lambda h: h.reciprocal(rxs, xs), inc=1)
            p.waits("scalar", [ev_rxs])
            ev_q = p.run("scalar", lambda h: h.activation(out=xqf[:], in_=xn[:], func=AF.Copy,
                                                          scale=rxs), inc=1)
            p.waits("vector", [ev_q])
            p.run("vector", lambda h: h.tensor_scalar(xqf[:], xqf[:], MAGIC, MAGIC, ALU.add, ALU.subtract))
            ev_cl = p.run("vector", lambda h: h.tensor_scalar(xqbf[:], xqf[:], 127.0, -128.0, ALU.min, ALU.max), inc=1)
            p.waits("tensor", [ev_cl, ev_pt_used_par[par]])
            for dc in range(8):
                evt = p.run("tensor", lambda h, dc=dc: h.transpose(
                    ptb[:, dc * 128:(dc + 1) * 128], xqbf[:, dc * 128:(dc + 1) * 128], ident[:]),
                    inc=(dc == 7))
            p.wait("vector", evt)
            ev_xqT = p.run("vector", lambda h: h.tensor_copy(
                xqT[:, :, :], ptb[:, :].rearrange("p (c o) -> p c o", c=8)), inc=1)
            ev_pt_used_par[par] = ev_xqT
            p.waits("tensor", [ev_xqT, ev_epi_par[par]])
            for dc in range(8):
                for nb in range(2):
                    ev_mm = p.run("tensor", lambda h, nb=nb, dc=dc: h.matmul(
                        accb[:, nb * 512:(nb + 1) * 512], xqT[:, dc, :],
                        wqT[:, dc, nb * 512:(nb + 1) * 512],
                        start=(dc == 0), stop=(dc == 7)), inc=(dc == 7 and nb == 1))
            ev_mm_par[par] = ev_mm
            p.waits("scalar", [ev_mm])
            ev_epi = p.run("scalar", lambda h: h.activation(out=t1[:], in_=accb[:, :], func=AF.Copy,
                                                            scale=xs), inc=1)
            ev_epi_par[par] = ev_epi
            p.waits("vector", [ev_epi])
            p.run("vector", lambda h: h.tensor_tensor(t1[:], t1[:], wsrep[:], ALU.mult))
            p.run("vector", lambda h: h.tensor_tensor(t1[:], t1[:], brep[:], ALU.add))
            p.dwait("vector", name + "cst%d" % par)
            ev_xo = p.run("vector", lambda h: h.tensor_tensor(xo[:], t1[:], xt[:], ALU.add), inc=1)
            ev_tile_done[par] = ev_xo
            p.waits("sync", [ev_xo])
            p.dma("sync", x_out[i * 128:(i + 1) * 128, :], xo[:, :], chan=name + "cst%d" % par)

        for i in range(NT):
            par = i % 2
            tile_body(i, par, xb[par], xsqp[par], xnp[par], xqfp[par], xqbfp[par],
                      xqTp[par], t1p[par], xop[par], stp[par], ptq[par], accq[par])
        p.dwait("sync", name + "cst0")
        p.dwait("sync", name + "cst1")

    return build_l0ffn

build_l0ffn = _mk_build_l0ffn()


def _mk_build_l0attnb():

    T = 4096
    D = 1024
    NH = 16
    E = 64
    IBB = 1024


    def build_l0attnb(p, t, cc, consts, x_in, x_out, v_scr, w, name="a0b"):
        nc = p.nc
        ident, ones1, ev_ident = consts
        nt = T // 128
        ntb = T // 512
        nib = T // IBB

        xnT = t.sb(name + "_xnT", [128, 8, T], BF16)
        attT = t.sb(name + "_attT", [128, 8, T], BF16)
        QTh = t.sb(name + "_QTh", [64, T], BF16)
        KTh = t.sb(name + "_KTh", [64, T], BF16)
        PT = t.sb(name + "_PT", [128, IBB], BF16)
        PT2 = t.sb(name + "_PT2", [128, IBB], BF16)
        Vh = t.sb(name + "_Vh", [128, nt, 65], BF16)
        Vh2 = t.sb(name + "_Vh2", [128, nt, 65], BF16)
        wqkv = t.sb(name + "_wqkv", [128, 8, 128], BF16)
        wvall = t.sb(name + "_wvall", [128, 8, 1024], BF16)
        bv_rep = t.sb(name + "_bvrep", [128, 1024], BF16)
        bqcol = t.sb(name + "_bqcol", [64, NH], F32)
        bkcol = t.sb(name + "_bkcol", [64, NH], F32)
        rstd_all = t.sb(name + "_rstdall", [128, nt], F32)
        nbias_all = t.sb(name + "_nbiasall", [128, nt], F32)
        recip = t.sb(name + "_recip", [1, IBB], F32)
        pden_sb = t.sb(name + "_pdensb", [64, IBB], F32)
        vstage = t.sb(name + "_vstage", [128, NH, 65], BF16)
        xt2 = [t.sb(name + "_x0", [128, D], F32), t.sb(name + "_x1", [128, D], F32)]
        xsq = t.sb(name + "_xsq", [128, D], F32)
        xnbf = t.sb(name + "_xnbf", [128, D], BF16)
        wtmp = t.sb(name + "_wtmp", [128, 1024], F32)
        s1 = t.sb(name + "_s1", [128, 1], F32)
        s2 = t.sb(name + "_s2", [128, 1], F32)
        mean = t.sb(name + "_mean", [128, 1], F32)
        m2 = t.sb(name + "_m2", [128, 1], F32)
        var = t.sb(name + "_var", [128, 1], F32)

        pool = t.ps(name + "_pool", [128, 6, 512], F32)
        pnorm = t.ps(name + "_pnorm", [128, 512], F32)
        ptr = t.ps(name + "_ptr", [128, 1024], BF16)
        psim = [pool[:, 0:2, :].rearrange("p a b -> p (a b)"),
                pool[:, 2:4, :].rearrange("p a b -> p (a b)")]
        pO = pool[:, 4:6, :].rearrange("p a b -> p (a b)")
        pproj = pool[:, 0, :]
        pproj2 = pool[:, 1, :]
        pdelta = pool[:, 2:4, :].rearrange("p a b -> p (a b)")

        cwb = name + "cwb"
        for h in range(NH):
            p.dma("sync", bqcol[:, h:h + 1], w["bq"][h, 0, :].rearrange("(p o) -> p o", o=1), chan=cwb)
            p.dma("sync", bkcol[:, h:h + 1], w["bk"][h, 0, :].rearrange("(p o) -> p o", o=1), chan=cwb)
        borow = wtmp[0:1, 0:1024]
        bvrow = wtmp[0:1, 0:1024]
        p.dwait("tensor", cwb)
        p.waits("tensor", [ev_ident])
        p.dma("sync", bvrow, w["bv"].rearrange("h o e -> o (h e)"), chan=cwb + "V")
        p.dwait("tensor", cwb + "V")
        for nb in range(2):
            evb = p.run("tensor", lambda h, nb=nb: h.matmul(
                pproj if nb == 0 else pproj2, ones1[:, :],
                bvrow[:, nb * 512:(nb + 1) * 512], start=True, stop=True), inc=1)
            p.wait("vector", evb)
            ev_b = p.run("vector", lambda h, nb=nb: h.tensor_copy(
                bv_rep[:, nb * 512:(nb + 1) * 512], pproj if nb == 0 else pproj2))
            p.wait("tensor", ev_b)
        cwh = name + "cwh"
        evc = evb
        for c in range(8):
            p.waits("sync", [evc])
            for hq in range(4):
                p.dma("sync", wtmp[:, hq * 256:(hq + 1) * 256].rearrange("p (h e) -> p h e", h=4),
                      w["wv"][hq * 4:(hq + 1) * 4, c * 128:(c + 1) * 128, :].rearrange("h p e -> p h e"),
                      chan=cwh)
                p.dwait("sync", cwh)
            p.dwait("vector", cwh)
            p.waits("vector", [evc])
            evc = p.run("vector", lambda h, c=c: h.tensor_copy(wvall[:, c, :], wtmp[:, :1024]))
        ev_wpre = evc
        ev_tr = None

        ev_xuse = [ev_wpre, ev_wpre]
        ev_ptuse = ev_wpre
        ev_tr_last = ev_tr
        ev_pv_eva = None
        ev_vst = None
        for i in range(nt):
            par = i % 2
            xt = xt2[par]
            cx = name + "cxa%d" % par
            p.waits("sync", [ev_xuse[par]])
            p.dma("sync", xt[:, :], x_in[i * 128:(i + 1) * 128, :], chan=cx)
            p.dwait("scalar", cx)
            ev_sq = p.run("scalar", lambda h, xt=xt: h.square(xsq[:], xt[:]))
            p.dwait("vector", cx)
            p.run("vector", lambda h, xt=xt: h.reduce_sum(s1[:], xt[:], axis=mybir.AxisListType.X))
            p.waits("vector", [ev_sq])
            p.run("vector", lambda h: h.reduce_sum(s2[:], xsq[:], axis=mybir.AxisListType.X))
            p.run("vector", lambda h: h.tensor_scalar_mul(mean[:], s1[:], 1.0 / D))
            p.run("vector", lambda h: h.tensor_scalar_mul(s2[:], s2[:], 1.0 / D))
            p.run("vector", lambda h: h.tensor_tensor(m2[:], mean[:], mean[:], ALU.mult))
            ev_v = p.run("vector", lambda h: h.tensor_tensor(var[:], s2[:], m2[:], ALU.subtract))
            p.waits("scalar", [ev_v, cc.ev(1e-5)])
            p.run("scalar", lambda h, i=i: h.activation(out=rstd_all[:, i:i + 1], in_=var[:],
                                                        func=AF.Ln, bias=cc.ap(1e-5)))
            ev_s = p.run("scalar", lambda h, i=i: h.activation(
                out=rstd_all[:, i:i + 1], in_=rstd_all[:, i:i + 1], func=AF.Exp, scale=-0.5))
            p.waits("vector", [ev_s])
            ev_nb = p.run("vector", lambda h, i=i: h.scalar_tensor_tensor(
                nbias_all[:, i:i + 1], mean[:], -1.0, rstd_all[:, i:i + 1], ALU.mult, ALU.mult))
            p.waits("scalar", [ev_nb, ev_tr_last])
            ev_xn = p.run("scalar", lambda h, xt=xt, i=i: h.activation(
                out=xnbf[:], in_=xt[:], func=AF.Identity,
                scale=rstd_all[:, i:i + 1], bias=nbias_all[:, i:i + 1]))
            ev_xuse[par] = ev_xn
            p.waits("tensor", [ev_xn, ev_ptuse])
            for dc in range(8):
                ev_tr = p.run("tensor", lambda h, dc=dc: h.transpose(
                    ptr[:, dc * 128:(dc + 1) * 128], xnbf[:, dc * 128:(dc + 1) * 128], ident[:]),
                    inc=(dc == 7))
            ev_tr_last = ev_tr
            p.waits("scalar", [ev_tr])
            for dc in range(8):
                ev_ptuse = p.run("scalar", lambda h, dc=dc, i=i: h.activation(
                    out=xnT[:, dc, i * 128:(i + 1) * 128], in_=ptr[:, dc * 128:(dc + 1) * 128],
                    func=AF.Copy))
            for nb in range(2):
                pp = pproj if nb == 0 else pproj2
                p.waits("tensor", [ev_ptuse, ev_pv_eva])
                for dc in range(8):
                    ev_mv = p.run("tensor", lambda h, dc=dc, nb=nb, pp=pp, i=i: h.matmul(
                        pp, xnT[:, dc, i * 128:(i + 1) * 128],
                        wvall[:, dc, nb * 512:(nb + 1) * 512],
                        start=(dc == 0), stop=(dc == 7)), inc=(dc == 7))
                p.waits("vector", [ev_mv])
                p.dwait("vector", name + "cvs")
                ev_pv_eva = p.run("vector", lambda h, nb=nb, pp=pp: h.tensor_tensor(
                    vstage[:, nb * 8:(nb + 1) * 8, 0:64],
                    pp.rearrange("p (g e) -> p g e", e=64),
                    bv_rep[:, nb * 512:(nb + 1) * 512].rearrange("p (g e) -> p g e", e=64),
                    ALU.add))
            if i == 0:
                ev_ones = p.run("gpsimd", lambda h: h.memset(vstage[:, :, 64:65], 1.0))
                p.waits("sync", [ev_ones])
            ev_vst = ev_pv_eva
            p.waits("sync", [ev_pv_eva])
            p.dwait("sync", name + "cvs")
            for hh in range(NH):
                p.dma("sync", v_scr[hh, i * 128:(i + 1) * 128, :], vstage[:, hh, :],
                      chan=name + "cvs")

        ev_attT_use = None
        ev_mq = None
        ev_qk_use = None
        ev_at_w = None
        ev_exp_par = [ev_vst, ev_vst]
        ev_acc_par = [None, None]
        ev_vh_use = [None, None]
        ev_pn_ev = None
        PTb = [PT, PT2]
        Vhb = [Vh, Vh2]
        p.dwait("sync", name + "cvs")
        for hh in range(NH):
            cvh = name + "cvh%d" % (hh % 2)
            Vhh = Vhb[hh % 2]
            p.waits("sync", [ev_vh_use[hh % 2]])
            for q4 in range(0, nt, 8):
                qn = min(8, nt - q4)
                p.dma("sync", Vhh[:, q4:q4 + qn, :],
                      v_scr[hh, q4 * 128:(q4 + qn) * 128, :].rearrange("(jt p) e -> p jt e", p=128),
                      chan=cvh)
            ev_qke = None
            for dst, wsrc, bcol in ((QTh, "wq", bqcol), (KTh, "wk", bkcol)):
                p.waits("sync", [p.ev("vector")])
                p.dma("sync", wtmp[:, 0:512].rearrange("p (c e) -> p c e", c=8),
                      w[wsrc][hh, :, :].rearrange("(c p) e -> p c e", p=128), chan=cwh)
                p.dwait("vector", cwh)
                p.waits("vector", [ev_mq])
                ev_wc = p.run("vector", lambda h: h.tensor_copy(
                    wqkv[:, :, 0:64], wtmp[:, 0:512].rearrange("p (c e) -> p c e", c=8)))
                for tb in range(ntb):
                    p.waits("tensor", [ev_wc, ev_qk_use, ev_exp_par[0], ev_qke])
                    for dc in range(8):
                        ev_mq = p.run("tensor", lambda h, dc=dc, tb=tb: h.matmul(
                            pproj[0:64, :], wqkv[:, dc, 0:64],
                            xnT[:, dc, tb * 512:(tb + 1) * 512],
                            start=(dc == 0), stop=(dc == 7)), inc=(dc == 7))
                    p.waits("scalar", [ev_mq])
                    ev_qke = p.run("scalar", lambda h, dst=dst, tb=tb, hh=hh, bcol=bcol:
                                   h.activation(out=dst[:, tb * 512:(tb + 1) * 512],
                                                in_=pproj[0:64, :], func=AF.Identity,
                                                bias=bcol[:, hh:hh + 1]))
            ev_qk = ev_qke
            for ib in range(nib):
                i0 = ib * IBB
                ev_sm_j = [None] * nt
                ev_pT_j = [None] * nt
                nhalf = IBB // 512
                for j in range(nt + 1):
                    if j < nt:
                        ps = psim[j % 2]
                        p.waits("tensor", [ev_qk, ev_exp_par[j % 2]])
                        for half in range(nhalf):
                            ev_sm = p.run("tensor", lambda h, j=j, i0=i0, ps=ps, half=half: h.matmul(
                                ps[:, half * 512:(half + 1) * 512], KTh[:, j * 128:(j + 1) * 128],
                                QTh[:, i0 + half * 512:i0 + (half + 1) * 512],
                                start=True, stop=True), inc=(half == nhalf - 1))
                        ev_sm_j[j] = ev_sm
                        p.waits("scalar", [ev_sm, cc.ev(-15.0), ev_acc_par[j % 2]])
                        ev_pT_j[j] = p.run("scalar", lambda h, ps=ps, j=j: h.activation(
                            out=PTb[j % 2][:, 0:IBB], in_=ps[:, 0:IBB], func=AF.Exp,
                            scale=0.125, bias=cc.ap(-15.0)))
                        ev_exp_par[j % 2] = ev_pT_j[j]
                    if j >= 1:
                        jj = j - 1
                        if jj == 0:
                            p.dwait("tensor", cvh)
                        p.waits("tensor", [ev_pT_j[jj], ev_at_w if jj == 0 else None])
                        for half in range(nhalf):
                            ev_acc = p.run("tensor", lambda h, jj=jj, half=half, Vhh=Vhh: h.matmul(
                                pO[0:65, half * 512:(half + 1) * 512], Vhh[:, jj, :],
                                PTb[jj % 2][:, half * 512:(half + 1) * 512],
                                start=(jj == 0), stop=(jj == nt - 1)), inc=(half == nhalf - 1))
                        ev_acc_par[jj % 2] = ev_acc
                ev_vh_use[hh % 2] = ev_acc
                ev_qk_use = ev_sm_j[nt - 1]
                p.waits("vector", [ev_acc])
                ev_rc = p.run("vector", lambda h: h.reciprocal(recip[:, 0:IBB], pO[64:65, 0:IBB]))
                for nb in range(nhalf):
                    p.waits("tensor", [ev_rc, ev_pn_ev])
                    ev_bc = p.run("tensor", lambda h, nb=nb: h.matmul(
                        pnorm[0:64, :], ones1[:, 0:64],
                        recip[:, nb * 512:(nb + 1) * 512], start=True, stop=True), inc=1)
                    p.waits("scalar", [ev_bc])
                    ev_pn_ev = p.run("scalar", lambda h, nb=nb: h.copy(
                        out=pden_sb[:, nb * 512:(nb + 1) * 512], in_=pnorm[0:64, :]))
                p.waits("vector", [ev_pn_ev, ev_attT_use])
                ev_at_w = p.run("vector", lambda h, hh=hh, i0=i0: h.scalar_tensor_tensor(
                    attT[(hh % 2) * 64:(hh % 2) * 64 + 64, hh // 2, i0:i0 + IBB],
                    pO[0:64, 0:IBB], 1.0, pden_sb[:, 0:IBB], ALU.bypass, ALU.mult))

        p.waits("sync", [p.ev("vector")])
        p.dma("sync", borow, w["bo"].rearrange("(o d) -> o d", o=1), chan=cwb + "L")
        p.dwait("tensor", cwb + "L")
        p.waits("tensor", [ev_at_w])
        for nb in range(2):
            evb = p.run("tensor", lambda h, nb=nb: h.matmul(
                pproj if nb == 0 else pproj2, ones1[:, :],
                borow[:, nb * 512:(nb + 1) * 512], start=True, stop=True), inc=1)
            p.wait("vector", evb)
            ev_b = p.run("vector", lambda h, nb=nb: h.tensor_copy(
                bv_rep[:, nb * 512:(nb + 1) * 512], pproj if nb == 0 else pproj2))
            p.wait("tensor", ev_b)
        evc = ev_b
        for c in range(8):
            p.waits("sync", [evc])
            p.dma("sync", wtmp[:, :1024], w["wo"][c * 128:(c + 1) * 128, :], chan=cwh)
            p.dwait("vector", cwh)
            evc = p.run("vector", lambda h: h.tensor_copy(xnbf[:], wtmp[:, :1024]))
            p.waits("tensor", [evc, ev_ptuse])
            for dc in range(8):
                ev_tr = p.run("tensor", lambda h, dc=dc: h.transpose(
                    ptr[:, dc * 128:(dc + 1) * 128], xnbf[:, dc * 128:(dc + 1) * 128], ident[:]),
                    inc=(dc == 7))
            ev_ptuse = ev_tr
            p.waits("vector", [ev_tr])
            evc = p.run("vector", lambda h, c=c: h.tensor_copy(
                wvall[:, :, c * 128:(c + 1) * 128],
                ptr[:, :].rearrange("p (a b) -> p a b", a=8)))
        ev_wo_ready = evc

        ev_epi = None
        ev_md = None
        for i in range(nt):
            par = i % 2
            xt = xt2[par]
            cx = name + "cxb%d" % par
            p.waits("sync", [ev_epi])
            p.dma("sync", xt[:, :], x_in[i * 128:(i + 1) * 128, :], chan=cx)
            p.waits("tensor", [ev_at_w, ev_epi, ev_md, ev_wo_ready])
            for dc in range(8):
                for nb in range(2):
                    ev_md = p.run("tensor", lambda h, dc=dc, nb=nb, i=i: h.matmul(
                        pdelta[:, nb * 512:(nb + 1) * 512], attT[:, dc, i * 128:(i + 1) * 128],
                        wvall[:, dc, nb * 512:(nb + 1) * 512],
                        start=(dc == 0), stop=(dc == 7)), inc=(dc == 7 and nb == 1))
            p.dwait("vector", cx)
            p.waits("vector", [ev_md])
            p.dwait("vector", name + "cso")
            p.run("vector", lambda h: h.tensor_tensor(xsq[:, :], pdelta, bv_rep[:, :], ALU.add))
            ev_epi = p.run("vector", lambda h, xt=xt: h.tensor_tensor(xsq[:, :], xsq[:, :], xt[:], ALU.add))
            p.waits("sync", [ev_epi])
            p.dma("sync", x_out[i * 128:(i + 1) * 128, :], xsq[:, :], chan=name + "cso")
        p.dwait("sync", name + "cso")

    return build_l0attnb

build_l0attnb = _mk_build_l0attnb()


def _mk_build_l1attna():

    T = 4096
    D = 1024
    N = 1024
    TB = 512
    NSQ = 11


    def build_l1attna(p, t, cc, consts, x_in, x_out, w, name="a1a"):
        nc = p.nc
        ident, ones1, ev_ident = consts
        nt = T // 128
        ntb = T // TB
        identf = t.handles["c_identf"]

        xnT = t.sb(name + "_xnT", [128, 8, T], BF16)
        big = t.sb(name + "_big", [128, 24576], BF16)
        WinT = t.sb(name + "_WinT", [128, 8, N], BF16)
        WoutT = t.sb(name + "_WoutT", [128, 8, D], BF16)
        yhT = t.sb(name + "_yhT", [128, 8, TB], BF16)
        rstd_all = t.sb(name + "_rstdall", [128, nt], F32)
        nbias_all = t.sb(name + "_nbiasall", [128, nt], F32)
        bpcol = t.sb(name + "_bpcol", [128, 24], F32)
        dbcol = t.sb(name + "_dbcol", [128, 8], F32)
        nalpha = t.sb(name + "_nalpha", [128, 8], F32)
        bpd = t.sb(name + "_bpd", [128, 8], F32)
        carry = t.sb(name + "_carry", [128, 8], F32)
        carryprev = t.sb(name + "_cprev", [128, 1], F32)
        ing = t.sb(name + "_ing", [128, 1], F32)
        iso = t.sb(name + "_iso", [128, 1], F32)
        onecol = t.sb(name + "_onecol", [128, 1], F32)
        sc1 = t.sb(name + "_sc1", [1, 4], F32)
        wtmp = t.sb(name + "_wtmp", [128, 1024], F32)
        xnbf = t.sb(name + "_xnbf", [128, D], BF16)
        xt2 = [t.sb(name + "_x0", [128, D], F32), t.sb(name + "_x1", [128, D], F32)]
        xsq = t.sb(name + "_xsq", [128, D], F32)
        s1 = t.sb(name + "_s1", [128, 1], F32)
        s2 = t.sb(name + "_s2", [128, 1], F32)
        mean = t.sb(name + "_mean", [128, 1], F32)
        m2 = t.sb(name + "_m2", [128, 1], F32)
        var = t.sb(name + "_var", [128, 1], F32)
        dtr = t.sb(name + "_dtr", [128, 1], F32)
        dtmp = t.sb(name + "_dtmp", [128, 128], F32)
        uTp = [t.sb(name + "_uT%d" % q, [128, TB], F32) for q in range(2)]
        txp = [t.sb(name + "_tx%d" % q, [128, TB], F32) for q in range(2)]
        taxp = [t.sb(name + "_tax%d" % q, [128, TB], F32) for q in range(2)]
        tdelta = t.sb(name + "_tdelta", [128, TB], F32)
        tap = [t.sb(name + "_ta%d" % q, [128, TB], F32) for q in range(2)]
        tbp_ = [t.sb(name + "_tb%d" % q, [128, TB], F32) for q in range(2)]
        tcp_ = [t.sb(name + "_tc%d" % q, [128, TB], F32) for q in range(2)]
        tpp = [t.sb(name + "_tp%d" % q, [128, TB], F32) for q in range(2)]
        tr_ = xsq[:, 0:TB]
        tdp = [t.sb(name + "_td%d" % q, [128, TB], F32) for q in range(2)]
        trs = xsq[:, TB:2 * TB]
        zT = tdelta
        epi = wtmp[:, 0:D]

        bigv = big.rearrange("p (a c q) -> p a c q", a=3, c=8)
        Wnat = bigv[:, 0, :, :]
        Bm = bigv[:, 1, :, :]
        B2m = bigv[:, 2, :, :]
        WpT = big.rearrange("p (c k) -> p c k", c=8)

        pool = t.ps(name + "_pool", [128, 7, 512], F32)
        ptr = t.ps(name + "_ptr", [128, 1024], BF16)
        pa = pool[:, 0, :]
        pb = pool[:, 1, :]
        praw = [pool[:, 2, :], pool[:, 3, :], pool[:, 4, :]]
        pu = pool[:, 5, :]
        ps1 = pool[:, 6, 0:128]

        cwb = name + "cwb"

        def load_col(dst, src1d, ncol):
            for c in range(ncol):
                p.dma("sync", dst[:, c:c + 1],
                      src1d[c * 128:(c + 1) * 128].rearrange("(p o) -> p o", o=1), chan=cwb)
        load_col(bpcol, w["bp"], 24)
        load_col(dbcol, w["delta_bias"], 8)
        load_col(nalpha, w["alpha_log"], 8)
        p.dma("sync", sc1[0:1, 0:1], w["log_gamma"].rearrange("(o k) -> o k", o=1), chan=cwb)
        p.dwait("vector", cwb)
        p.dwait("scalar", cwb)
        p.dwait("gpsimd", cwb)
        ev_cz = p.run("gpsimd", lambda h: h.memset(carry[:, :], 0.0))
        ev_oc = p.run("gpsimd", lambda h: h.memset(onecol[:, :], 1.0))
        p.run("vector", lambda h: h.tensor_tensor(bpd[:, :], bpcol[:, 0:8], dbcol[:, :], ALU.add))
        p.run("scalar", lambda h: h.activation(out=txp[0][:, 0:8], in_=nalpha[:, :], func=AF.Abs))
        p.run("scalar", lambda h: h.activation(out=txp[0][:, 8:16], in_=txp[0][:, 0:8], func=AF.Exp, scale=-1.0))
        p.waits("scalar", [cc.ev(1.0), cc.ev(1e-12)])
        ev_s = p.run("scalar", lambda h: h.activation(out=txp[0][:, 8:16], in_=txp[0][:, 8:16], func=AF.Ln,
                                                      bias=cc.ap(1.0)))
        p.waits("vector", [ev_s])
        p.run("vector", lambda h: h.tensor_scalar_max(txp[0][:, 0:8], nalpha[:, :], 0.0))
        p.run("vector", lambda h: h.tensor_tensor(txp[0][:, 0:8], txp[0][:, 0:8], txp[0][:, 8:16], ALU.add))
        ev_na = p.run("vector", lambda h: h.tensor_scalar_mul(nalpha[:, :], txp[0][:, 0:8], -1.0))

        def sigma_phase(wap, transposed_dst, inv_dst, tag, extra_scale_ev=None, extra_scale=None):
            cw = name + "cw" + tag
            evc = p.ev("vector")
            ev_tr = None
            for c in range(8):
                p.waits("sync", [evc])
                p.dma("sync", wtmp[:, :], wap[c * 128:(c + 1) * 128, :], chan=cw)
                p.dwait("vector", cw)
                evc = p.run("vector", lambda h, c=c: h.tensor_copy(Wnat[:, c, :], wtmp[:, :]))
            for c in range(8):
                p.waits("tensor", [evc])
                for dc in range(8):
                    ev_tr = p.run("tensor", lambda h, c=c, dc=dc: h.transpose(
                        ptr[:, dc * 128:(dc + 1) * 128], Wnat[:, c, dc * 128:(dc + 1) * 128],
                        ident[:]), inc=(dc == 7))
                p.waits("vector", [ev_tr])
                evc = p.run("vector", lambda h, c=c: h.tensor_copy(
                    transposed_dst[:, :, c * 128:(c + 1) * 128],
                    ptr[:, :].rearrange("p (a b) -> p a b", a=8)))
            ev_ev = evc
            for mt in range(8):
                for nb2 in range(2):
                    pp = pa if nb2 == 0 else pb
                    p.waits("tensor", [ev_ev])
                    for k in range(8):
                        ev_mm = p.run("tensor", lambda h, mt=mt, nb2=nb2, k=k, pp=pp: h.matmul(
                            pp, Wnat[:, k, mt * 128:(mt + 1) * 128],
                            Wnat[:, k, nb2 * 512:(nb2 + 1) * 512],
                            start=(k == 0), stop=(k == 7)), inc=(k == 7))
                    p.waits("scalar", [ev_mm])
                    ev_ev = p.run("scalar", lambda h, mt=mt, nb2=nb2, pp=pp: h.copy(
                        out=Bm[:, mt, nb2 * 512:(nb2 + 1) * 512], in_=pp))
            src, dst = Bm, B2m
            for it in range(NSQ):
                for mt in range(8):
                    for nb2 in range(2):
                        pp = pa if nb2 == 0 else pb
                        p.waits("tensor", [ev_ev])
                        for k in range(8):
                            ev_mm = p.run("tensor", lambda h, mt=mt, nb2=nb2, k=k, pp=pp,
                                          src=src: h.matmul(
                                pp, src[:, k, mt * 128:(mt + 1) * 128],
                                src[:, k, nb2 * 512:(nb2 + 1) * 512],
                                start=(k == 0), stop=(k == 7)), inc=(k == 7))
                        p.waits("scalar", [ev_mm])
                        ev_ev = p.run("scalar", lambda h, mt=mt, nb2=nb2, pp=pp, dst=dst: h.copy(
                            out=dst[:, mt, nb2 * 512:(nb2 + 1) * 512], in_=pp))
                p.waits("vector", [ev_ev, ev_oc])
                p.run("vector", lambda h: h.memset(dtr[:, :], 0.0))
                for c in range(8):
                    p.run("vector", lambda h, c=c, dst=dst: h.tensor_tensor(
                        dtmp[:, :], dst[:, c, c * 128:(c + 1) * 128], identf[:, :], ALU.mult))
                    p.run("vector", lambda h: h.reduce_sum(s1[:], dtmp[:, :], axis=mybir.AxisListType.X))
                    p.run("vector", lambda h: h.tensor_tensor(dtr[:, :], dtr[:, :], s1[:], ALU.add))
                ev_dt = p.ev("vector")
                p.waits("tensor", [ev_dt])
                ev_tm = p.run("tensor", lambda h: h.matmul(ps1[0:1, 0:1], dtr[:, 0:1], onecol[:, 0:1],
                                                           start=True, stop=True), inc=1)
                p.waits("vector", [ev_tm])
                ev_rc = p.run("vector", lambda h: h.reciprocal(sc1[0:1, 1:2], ps1[0:1, 0:1]))
                p.waits("tensor", [ev_rc])
                ev_bc = p.run("tensor", lambda h: h.matmul(ps1[:, 0:1], ones1[:, :], sc1[0:1, 1:2],
                                                           start=True, stop=True), inc=1)
                p.waits("scalar", [ev_bc])
                ev_sc = p.run("scalar", lambda h: h.copy(out=s2[:, :], in_=ps1[:, 0:1]))
                p.waits("vector", [ev_sc])
                for c in range(8):
                    ev_ev = p.run("vector", lambda h, c=c, dst=dst: h.tensor_scalar_mul(
                        dst[:, c, :], dst[:, c, :], s2[:, 0:1]))
                src, dst = dst, src
            Bfin = src
            p.waits("vector", [ev_ev])
            p.run("vector", lambda h: h.memset(dtr[:, :], 0.0))
            ev_z = p.run("vector", lambda h: h.memset(s2[:, :], 0.0))
            ev_ev2 = ev_z
            for mt in range(8):
                for nb2 in range(2):
                    pp = pa if nb2 == 0 else pb
                    p.waits("tensor", [ev_ev2, ev_ev])
                    for k in range(8):
                        ev_mm = p.run("tensor", lambda h, mt=mt, nb2=nb2, k=k, pp=pp, Bfin=Bfin:
                                      h.matmul(
                            pp, transposed_dst[:, k, mt * 128:(mt + 1) * 128],
                            Bfin[:, k, nb2 * 512:(nb2 + 1) * 512],
                            start=(k == 0), stop=(k == 7)), inc=(k == 7))
                    p.waits("vector", [ev_mm])
                    p.run("vector", lambda h, mt=mt, nb2=nb2, pp=pp: h.tensor_tensor(
                        uTp[0][:, :], pp, Wnat[:, mt, nb2 * 512:(nb2 + 1) * 512], ALU.mult))
                    p.run("vector", lambda h: h.reduce_sum(s1[:], uTp[0][:, 0:512], axis=mybir.AxisListType.X))
                    ev_ev2 = p.run("vector", lambda h: h.tensor_tensor(dtr[:, :], dtr[:, :], s1[:], ALU.add))
            p.waits("vector", [ev_ev2])
            p.run("vector", lambda h: h.memset(uTp[0][:, 0:1], 0.0))
            for c in range(8):
                p.run("vector", lambda h, c=c, Bfin=Bfin: h.tensor_tensor(
                    dtmp[:, :], Bfin[:, c, c * 128:(c + 1) * 128], identf[:, :], ALU.mult))
                p.run("vector", lambda h: h.reduce_sum(s1[:], dtmp[:, :], axis=mybir.AxisListType.X))
                p.run("vector", lambda h: h.tensor_tensor(uTp[0][:, 0:1], uTp[0][:, 0:1], s1[:], ALU.add))
            ev_tb = p.ev("vector")
            p.waits("tensor", [ev_tb])
            p.run("tensor", lambda h: h.matmul(ps1[0:1, 1:2], dtr[:, 0:1], onecol[:, 0:1],
                                               start=True, stop=True))
            ev_tm = p.run("tensor", lambda h: h.matmul(ps1[0:1, 2:3], uTp[0][:, 0:1], onecol[:, 0:1],
                                                       start=True, stop=True), inc=1)
            p.waits("vector", [ev_tm])
            p.run("vector", lambda h: h.reciprocal(sc1[0:1, 3:4], ps1[0:1, 2:3]))
            ev_l = p.run("vector", lambda h: h.tensor_tensor(sc1[0:1, 0:1] if False else sc1[0:1, 1:2],
                                                             ps1[0:1, 1:2], sc1[0:1, 3:4], ALU.mult))
            p.waits("scalar", [ev_l])
            ev_sg = p.run("scalar", lambda h: h.activation(out=sc1[0:1, 1:2], in_=sc1[0:1, 1:2],
                                                           func=AF.Sqrt))
            p.waits("vector", [ev_sg, extra_scale_ev])
            p.run("vector", lambda h: h.tensor_scalar_max(sc1[0:1, 1:2], sc1[0:1, 1:2], 1.0))
            ev_iv = p.run("vector", lambda h: h.reciprocal(sc1[0:1, 1:2], sc1[0:1, 1:2]))
            if extra_scale is not None:
                ev_iv = p.run("vector", lambda h: h.tensor_tensor(
                    sc1[0:1, 1:2], sc1[0:1, 1:2], extra_scale, ALU.mult))
            p.waits("tensor", [ev_iv])
            ev_bc = p.run("tensor", lambda h: h.matmul(ps1[:, 0:1], ones1[:, :], sc1[0:1, 1:2],
                                                       start=True, stop=True), inc=1)
            p.waits("scalar", [ev_bc])
            ev_fin = p.run("scalar", lambda h: h.copy(out=inv_dst[:, :], in_=ps1[:, 0:1]))
            return ev_fin

        ev_eg = p.run("scalar", lambda h: h.activation(out=sc1[0:1, 0:1], in_=sc1[0:1, 0:1],
                                                       func=AF.Exp))
        ev_si = sigma_phase(w["W_in"], WinT, ing, "i", extra_scale_ev=ev_eg, extra_scale=sc1[0:1, 0:1])
        ev_so = sigma_phase(w["W_out"], WoutT, iso, "o")

        cwp = name + "cwp"
        evc = p.ev("vector")
        for r in range(24):
            p.waits("sync", [evc])
            p.dma("sync", wtmp[:, :], w["Wp"][r * 128:(r + 1) * 128, :], chan=cwp)
            p.dwait("vector", cwp)
            evc = p.run("vector", lambda h: h.tensor_copy(xnbf[:], wtmp[:, :]))
            p.waits("tensor", [evc, ev_si, ev_so])
            for dc in range(8):
                ev_tr = p.run("tensor", lambda h, dc=dc: h.transpose(
                    ptr[:, dc * 128:(dc + 1) * 128], xnbf[:, dc * 128:(dc + 1) * 128], ident[:]),
                    inc=(dc == 7))
            p.waits("vector", [ev_tr])
            evc = p.run("vector", lambda h, r=r: h.tensor_copy(
                WpT[:, :, r * 128:(r + 1) * 128],
                ptr[:, :].rearrange("p (a b) -> p a b", a=8)))
        ev_wp = evc

        ev_xuse = [ev_wp, ev_wp]
        ev_ptuse = ev_wp
        ev_tr_last = ev_tr
        for i in range(nt):
            par = i % 2
            xt = xt2[par]
            cx = name + "cxa%d" % par
            p.waits("sync", [ev_xuse[par]])
            p.dma("sync", xt[:, :], x_in[i * 128:(i + 1) * 128, :], chan=cx)
            p.dwait("scalar", cx)
            ev_sq = p.run("scalar", lambda h, xt=xt: h.square(xsq[:], xt[:]))
            p.dwait("vector", cx)
            p.run("vector", lambda h, xt=xt: h.reduce_sum(s1[:], xt[:], axis=mybir.AxisListType.X))
            p.waits("vector", [ev_sq])
            p.run("vector", lambda h: h.reduce_sum(s2[:], xsq[:], axis=mybir.AxisListType.X))
            p.run("vector", lambda h: h.tensor_scalar_mul(mean[:], s1[:], 1.0 / D))
            p.run("vector", lambda h: h.tensor_scalar_mul(s2[:], s2[:], 1.0 / D))
            p.run("vector", lambda h: h.tensor_tensor(m2[:], mean[:], mean[:], ALU.mult))
            ev_v = p.run("vector", lambda h: h.tensor_tensor(var[:], s2[:], m2[:], ALU.subtract))
            p.waits("scalar", [ev_v, cc.ev(1e-5)])
            p.run("scalar", lambda h, i=i: h.activation(out=rstd_all[:, i:i + 1], in_=var[:],
                                                        func=AF.Ln, bias=cc.ap(1e-5)))
            ev_s = p.run("scalar", lambda h, i=i: h.activation(
                out=rstd_all[:, i:i + 1], in_=rstd_all[:, i:i + 1], func=AF.Exp, scale=-0.5))
            p.waits("vector", [ev_s])
            ev_nb = p.run("vector", lambda h, i=i: h.scalar_tensor_tensor(
                nbias_all[:, i:i + 1], mean[:], -1.0, rstd_all[:, i:i + 1], ALU.mult, ALU.mult))
            p.waits("scalar", [ev_nb, ev_tr_last])
            ev_xn = p.run("scalar", lambda h, xt=xt, i=i: h.activation(
                out=xnbf[:], in_=xt[:], func=AF.Identity,
                scale=rstd_all[:, i:i + 1], bias=nbias_all[:, i:i + 1]))
            ev_xuse[par] = ev_xn
            p.waits("tensor", [ev_xn, ev_ptuse])
            for dc in range(8):
                ev_tr = p.run("tensor", lambda h, dc=dc: h.transpose(
                    ptr[:, dc * 128:(dc + 1) * 128], xnbf[:, dc * 128:(dc + 1) * 128], ident[:]),
                    inc=(dc == 7))
            ev_tr_last = ev_tr
            p.waits("scalar", [ev_tr])
            for dc in range(8):
                ev_ptuse = p.run("scalar", lambda h, dc=dc, i=i: h.activation(
                    out=xnT[:, dc, i * 128:(i + 1) * 128], in_=ptr[:, dc * 128:(dc + 1) * 128],
                    func=AF.Copy))

        ev_praw_used = None
        ev_pu_used = None
        ev_yh_used = None
        ev_yh = None
        ev_sa_prev = None
        ev_chain_v = [None, None]
        ev_epi = None
        ev_md = None
        for tbi in range(ntb):
            t0 = tbi * TB
            def g_body(g, t0, par, tx, tax, ta, tb_, tc_, tp, td, uT):
                nonlocal ev_praw_used, ev_pu_used, ev_yh, ev_sa_prev
                p.waits("tensor", [ev_praw_used, ev_ptuse, ev_wp])
                for r3 in range(3):
                    kk = r3 * 8 + g
                    for dc in range(8):
                        ev_mm = p.run("tensor", lambda h, r3=r3, kk=kk, dc=dc, t0=t0: h.matmul(
                            praw[r3], WpT[:, dc, kk * 128:(kk + 1) * 128],
                            xnT[:, dc, t0:t0 + TB],
                            start=(dc == 0), stop=(dc == 7)), inc=(dc == 7 and r3 == 2))
                p.waits("tensor", [ev_pu_used, ev_si])
                for dc in range(8):
                    ev_mu = p.run("tensor", lambda h, g=g, dc=dc, t0=t0: h.matmul(
                        pu, WinT[:, dc, g * 128:(g + 1) * 128], xnT[:, dc, t0:t0 + TB],
                        start=(dc == 0), stop=(dc == 7)), inc=(dc == 7))
                p.waits("scalar", [ev_mu])
                ev_ue = p.run("scalar", lambda h: h.activation(out=uT[:, :], in_=pu,
                                                               func=AF.Identity, scale=ing[:, :]))
                ev_pu_used = ev_ue
                p.waits("scalar", [ev_mm, ev_na])
                p.run("scalar", lambda h, g=g: h.activation(out=tx[:, :], in_=praw[0],
                                                            func=AF.Identity, bias=bpd[:, g:g + 1]))
                ev_praw_used = p.run("scalar", lambda h, g=g: h.activation(
                    out=tb_[:, :], in_=praw[1], func=AF.Tanh, bias=bpcol[:, 8 + g:9 + g]))
                ev_praw_used = p.run("scalar", lambda h, g=g: h.activation(
                    out=tc_[:, :], in_=praw[2], func=AF.Tanh, bias=bpcol[:, 16 + g:17 + g]))
                p.run("scalar", lambda h: h.activation(out=tax[:, :], in_=tx[:, :], func=AF.Abs))
                p.run("scalar", lambda h: h.activation(out=tax[:, :], in_=tax[:, :], func=AF.Exp,
                                                       scale=-1.0))
                ev_sp = p.run("scalar", lambda h: h.activation(out=tax[:, :], in_=tax[:, :],
                                                               func=AF.Ln, bias=cc.ap(1.0)))
                p.waits("vector", [ev_sp, ev_sa_prev])
                p.run("vector", lambda h: h.tensor_scalar_max(tdelta[:, :], tx[:, :], 0.0))
                ev_vd = p.run("vector", lambda h: h.tensor_tensor(tdelta[:, :], tdelta[:, :],
                                                                  tax[:, :], ALU.add))
                p.waits("scalar", [ev_vd])
                ev_sa = p.run("scalar", lambda h, g=g: h.activation(
                    out=ta[:, :], in_=tdelta[:, :], func=AF.Exp, scale=nalpha[:, g:g + 1]))
                ev_sa_prev = ev_sa
                p.waits("vector", [ev_sa])
                p.run("vector", lambda h: h.tensor_scalar_min(ta[:, :], ta[:, :], 1.0 - 1e-4))
                p.run("vector", lambda h: h.tensor_tensor(tp[:, :], ta[:, :], ta[:, :], ALU.mult))
                p.run("vector", lambda h: h.tensor_tensor(trs[:, :], tc_[:, :], tc_[:, :], ALU.mult))
                p.run("vector", lambda h: h.tensor_tensor(tp[:, :], tp[:, :], trs[:, :], ALU.add))
                p.run("vector", lambda h: h.tensor_tensor(tr_[:, :], tb_[:, :], tb_[:, :], ALU.mult))
                p.run("vector", lambda h: h.tensor_tensor(tax[:, :], ta[:, :], tb_[:, :], ALU.mult))
                p.run("vector", lambda h: h.tensor_tensor(td[:, :], tp[:, :], tr_[:, :], ALU.subtract))
                p.run("vector", lambda h: h.tensor_tensor(td[:, :], td[:, :], td[:, :], ALU.mult))
                p.run("vector", lambda h: h.scalar_tensor_tensor(tax[:, :], tax[:, :], 4.0, tax[:, :],
                                                                 ALU.mult, ALU.mult))
                ev_vq = p.run("vector", lambda h: h.tensor_tensor(td[:, :], td[:, :], tax[:, :], ALU.add))
                p.waits("scalar", [ev_vq])
                ev_sd = p.run("scalar", lambda h: h.activation(out=td[:, :], in_=td[:, :],
                                                               func=AF.Sqrt, bias=cc.ap(1e-12)))
                p.waits("vector", [ev_sd])
                p.run("vector", lambda h: h.tensor_tensor(tp[:, :], tp[:, :], tr_[:, :], ALU.add))
                ev_vl = p.run("vector", lambda h: h.tensor_tensor(tp[:, :], tp[:, :], td[:, :], ALU.add))
                p.waits("scalar", [ev_vl])
                ev_ss = p.run("scalar", lambda h: h.activation(out=tp[:, :], in_=tp[:, :],
                                                               func=AF.Sqrt, scale=0.5,
                                                               bias=cc.ap(1e-12)))
                p.waits("vector", [ev_ss, ev_ue])
                p.run("vector", lambda h: h.tensor_scalar_max(tp[:, :], tp[:, :], 1.0))
                p.run("vector", lambda h: h.reciprocal(trs[:, :], tp[:, :]))
                p.run("vector", lambda h: h.tensor_tensor(ta[:, :], ta[:, :], trs[:, :], ALU.mult))
                p.run("vector", lambda h: h.tensor_tensor(tc_[:, :], tc_[:, :], trs[:, :], ALU.mult))
                p.run("vector", lambda h: h.tensor_tensor(tb_[:, :], tb_[:, :], trs[:, :], ALU.mult))
                p.run("vector", lambda h: h.tensor_tensor(tr_[:, :], tb_[:, :], uT[:, :], ALU.mult))
                p.run("vector", lambda h, g=g: h.tensor_copy(carryprev[:, :], carry[:, g:g + 1]))
                p.run("vector", lambda h, g=g: h.tensor_tensor_scan(
                    zT[:, :], ta[:, :], tr_[:, :], carry[:, g:g + 1], ALU.mult, ALU.add))
                p.run("vector", lambda h, g=g: h.tensor_copy(carry[:, g:g + 1], zT[:, TB - 1:TB]))
                p.waits("vector", [ev_yh_used])
                p.run("vector", lambda h, g=g: h.tensor_tensor(
                    yhT[:, g, 1:TB], tc_[:, 1:TB], zT[:, 0:TB - 1], ALU.mult))
                ev_yh = p.run("vector", lambda h, g=g: h.tensor_tensor(
                    yhT[:, g, 0:1], tc_[:, 0:1], carryprev[:, :], ALU.mult))
            for g in range(8):
                par = g % 2
                p.waits("scalar", [ev_chain_v[par]])
                g_body(g, t0, par, txp[par], taxp[par], tap[par],
                       tbp_[par], tcp_[par], tpp[par], tdp[par], uTp[par])
                ev_chain_v[par] = ev_yh
            for k in range(TB // 128):
                it = tbi * (TB // 128) + k
                par = it % 2
                xt = xt2[par]
                cx = name + "cxb%d" % par
                p.waits("sync", [ev_epi, ev_xuse[par]])
                p.dma("sync", xt[:, :], x_in[it * 128:(it + 1) * 128, :], chan=cx)
                p.waits("tensor", [ev_yh, ev_md, ev_epi, ev_so])
                for g in range(8):
                    for nb2 in range(2):
                        ev_md = p.run("tensor", lambda h, g=g, nb2=nb2, k=k: h.matmul(
                            pool[:, nb2, :], yhT[:, g, k * 128:(k + 1) * 128],
                            WoutT[:, g, nb2 * 512:(nb2 + 1) * 512],
                            start=(g == 0), stop=(g == 7)), inc=(g == 7 and nb2 == 1))
                ev_yh_used = ev_md
                p.waits("scalar", [ev_md, ev_so])
                p.dwait("scalar", name + "cso")
                ev_ep = p.run("scalar", lambda h: h.activation(
                    out=epi[:, :], in_=pool[:, 0:2, :].rearrange("p a b -> p (a b)"),
                    func=AF.Identity, scale=iso[:, :]))
                p.dwait("vector", cx)
                p.dwait("vector", name + "cso")
                p.waits("vector", [ev_ep])
                ev_epi = p.run("vector", lambda h, xt=xt: h.tensor_tensor(
                    epi[:, :], epi[:, :], xt[:], ALU.add))
                ev_xuse[par] = ev_epi
                p.waits("sync", [ev_epi])
                p.dma("sync", x_out[it * 128:(it + 1) * 128, :], epi[:, :], chan=name + "cso")
        p.dwait("sync", name + "cso")

    return build_l1attna

build_l1attna = _mk_build_l1attna()


def _mk_build_l1ffn():
    from concourse.bass_core_rust import IndirectOffsetOnAxis

    T = 4096
    D = 1024
    NHE = 4
    DQ = 256
    NSUB = 128
    TOPK = 8
    KC = 16
    NE = 16384
    DL = 256
    DH = 512
    NTOK_GLOBAL = 32768.0
    NCORES = 8


    def build_l1ffn(p, t, cc, consts, x_in, x_out, h_scr, q_scr, xl_scr, cc_in, cc_out, w, name="a1f",
                    h_ready=None, skip_hall=False):
        nc = p.nc
        ident, ones1, ev_ident = consts
        nt = T // 128
        ntb = T // 512

        xnT = t.sb(name + "_xnT", [128, 8, T], BF16)
        qtmp = t.sb(name + "_qtmp", [128, 512], F32)
        qf = t.sb(name + "_qf", [128, 8, 128], F32)
        hg = t.sb(name + "_hg", [128, 32, DH], BF16)
        hg2 = t.sb(name + "_hg2", [128, 32, DH], BF16)
        wqT = hg2[:, 0:16, :].rearrange("p a b -> p (a b)").rearrange(
            "p (c q) -> p c q", c=8)
        xnf = t.sb(name + "_xnf", [128, 1024], F32)
        xlbf = hg2[:, 20:22, :].rearrange("p a b -> p (a b)")
        xlT = hg[:, 18:20, :].rearrange("p a (b c) -> p (a b) c", b=4)
        xlblk = t.sb(name + "_xlblk", [128, 8, 512], BF16)
        w2aT = t.sb(name + "_w2aT", [128, 8, 512], BF16)
        wv_sb = t.sb(name + "_wv", [128, 4, 1024], BF16)
        w1_sb = hg[:, 16:18, :]
        sk1T = t.sb(name + "_sk1T", [128, 128], F32)
        sk2T = t.sb(name + "_sk2T", [128, 128], F32)
        bqcol = t.sb(name + "_bqcol", [128, 8], F32)
        gamcol = t.sb(name + "_gamcol", [128, 8], F32)
        betcol = t.sb(name + "_betcol", [128, 8], F32)
        qsum = t.sb(name + "_qsum", [128, 16], F32)
        qsc = t.sb(name + "_qsc", [128, 8], F32)
        qbs = t.sb(name + "_qbs", [128, 8], F32)
        iota16 = t.sb(name + "_iota16", [128, 16], F32)
        rstd_all = t.sb(name + "_rstdall", [128, nt], F32)
        nbias_all = t.sb(name + "_nbiasall", [128, nt], F32)
        wqTl = hg[:, 0:16, :].rearrange("p a b -> p (a b)").rearrange("p (c q) -> p c q", c=8)
        wtmp = t.sb(name + "_wtmp", [128, 1024], F32)
        xnbf = t.sb(name + "_xnbf", [128, D], BF16)
        xt2 = [t.sb(name + "_x0", [128, D], F32), t.sb(name + "_x1", [128, D], F32)]
        xsq = t.sb(name + "_xsq", [128, D], F32)
        s1t = t.sb(name + "_s1t", [128, 1], F32)
        s2t = t.sb(name + "_s2t", [128, 1], F32)
        mean = t.sb(name + "_mean", [128, 1], F32)
        m2 = t.sb(name + "_m2", [128, 1], F32)
        var = t.sb(name + "_var", [128, 1], F32)
        qnT = t.sb(name + "_qnT", [128, 8, 128], F32)
        ss = t.sb(name + "_ss", [128, 8, 128], F32)
        ts1 = t.sb(name + "_ts1", [128, 4, 16], F32)
        ts2 = t.sb(name + "_ts2", [128, 4, 16], F32)
        ti1 = t.sb(name + "_ti1", [128, 4, 16], U32)
        ti2 = t.sb(name + "_ti2", [128, 4, 16], U32)
        ti1f = t.sb(name + "_ti1f", [128, 4, 16], F32)
        ti2f = t.sb(name + "_ti2f", [128, 4, 16], F32)
        comb = t.sb(name + "_comb", [128, 4, 256], F32)
        fs = t.sb(name + "_fs", [128, 4, 8], F32)
        tci = t.sb(name + "_tci", [128, 4, 8], U32)
        tcif = t.sb(name + "_tcif", [128, 4, 8], F32)
        tmod = t.sb(name + "_tmod", [128, 4, 8], F32)
        idiv = t.sb(name + "_idiv", [128, 4, 8], F32)
        eoh = comb[:, :, 0:128].rearrange("p h (a b) -> p h a b", a=8)
        i1v = t.sb(name + "_i1v", [128, 4, 8], F32)
        i2v = t.sb(name + "_i2v", [128, 4, 8], F32)
        probs = t.sb(name + "_probs", [128, 4, 8], F32)
        probs2 = t.sb(name + "_probs2", [128, 4, 8], F32)
        psum8 = t.sb(name + "_psum8", [128, 4, 1], F32)
        eidx = t.sb(name + "_eidx", [128, 32], I32)
        xp = t.sb(name + "_xp", [128, DH], BF16)
        xp2 = t.sb(name + "_xp2", [128, DH], BF16)
        dots = t.sb(name + "_dots", [128, 32], F32)
        act = t.sb(name + "_act", [128, 32], F32)
        dump = t.sb(name + "_dump", [128, DH], BF16)
        accb = t.sb(name + "_accb", [128, DH], F32)
        accbf = t.sb(name + "_accbf", [128, DH], BF16)
        accT = t.sb(name + "_accT", [128, 4, 128], BF16)
        lat_t = t.sb(name + "_latt", [128, 2, 128], BF16)
        hge = t.sb(name + "_hge", [128, DH], BF16)

        pool = t.ps(name + "_pool", [128, 7, 512], F32)
        ptr = t.ps(name + "_ptr", [128, 1024], BF16)
        pa = pool[:, 0, :]
        pb = pool[:, 1, :]
        psc = [pool[:, 2, :], pool[:, 3, :]]
        pxp = pool[:, 4, :]
        pfin = pool[:, 5:7, :].rearrange("p a b -> p (a b)")

        cwb = name + "cwb"

        def load_col(dst, src1d, ncol):
            for c in range(ncol):
                p.dma("sync", dst[:, c:c + 1],
                      src1d[c * 128:(c + 1) * 128].rearrange("(p o) -> p o", o=1), chan=cwb)
        load_col(bqcol, w["bq"].rearrange("h q -> (h q)"), 8)
        load_col(gamcol, w["gamma"].rearrange("h q -> (h q)"), 8)
        load_col(betcol, w["beta"].rearrange("h q -> (h q)"), 8)
        p.dwait("vector", cwb)
        p.dwait("gpsimd", cwb)
        ev_io = p.run("gpsimd", lambda h: h.iota(iota16[:, :], pattern=[[1, 16]], base=0,
                                                 channel_multiplier=0,
                                                 allow_small_or_imprecise_dtypes=True))

        cwh = name + "cwh"
        evc = p.ev("vector")
        p.waits("sync", [evc])
        p.dma("sync", wtmp[:, 0:1024].rearrange("p (c f) -> p c f", c=2),
              w["W1"].rearrange("(c p) f -> p c f", p=128), chan=cwh)
        p.dwait("vector", cwh)
        evc = p.run("vector", lambda h: h.tensor_copy(
            w1_sb[:, :, :], wtmp[:, 0:1024].rearrange("p (c f) -> p c f", c=2)))
        for c in range(4):
            p.waits("sync", [evc])
            p.dma("sync", wtmp[:, :], w["W2"][c * 128:(c + 1) * 128, D:], chan=cwh)
            p.dwait("vector", cwh)
            evc = p.run("vector", lambda h, c=c: h.tensor_copy(wv_sb[:, c, :], wtmp[:, :]))
        for c in range(4):
            p.waits("sync", [evc])
            p.dma("sync", wtmp[:, :], w["W2"][c * 128:(c + 1) * 128, 0:D], chan=cwh)
            p.dwait("vector", cwh)
            evc = p.run("vector", lambda h: h.tensor_copy(xnbf[:], wtmp[:, :]))
            p.waits("tensor", [evc])
            for dc in range(8):
                ev_tr = p.run("tensor", lambda h, dc=dc: h.transpose(
                    ptr[:, dc * 128:(dc + 1) * 128], xnbf[:, dc * 128:(dc + 1) * 128], ident[:]),
                    inc=(dc == 7))
            p.waits("vector", [ev_tr])
            evc = p.run("vector", lambda h, c=c: h.tensor_copy(
                w2aT[:, :, c * 128:(c + 1) * 128],
                ptr[:, :].rearrange("p (a b) -> p a b", a=8)))
        identf = t.handles["c_identf"]
        pq32 = pool[:, 2:4, :].rearrange("p a b -> p (a b)")
        for hh in range(NHE):
            for qc in range(2):
                p.waits("sync", [evc])
                p.dma("sync", wtmp[:, :], w["Wq"][hh, qc * 128:(qc + 1) * 128, :], chan=cwh)
                p.dwait("tensor", cwh)
                p.waits("tensor", [evc])
                for dc in range(8):
                    ev_tr = p.run("tensor", lambda h, dc=dc: h.transpose(
                        pq32[:, dc * 128:(dc + 1) * 128], wtmp[:, dc * 128:(dc + 1) * 128],
                        identf[:, :]), inc=(dc == 7))
                p.waits("vector", [ev_tr])
                col = slice(hh * 256 + qc * 128, hh * 256 + (qc + 1) * 128)
                p.run("vector", lambda h, col=col: h.tensor_copy(
                    wqT[:, :, col], pq32.rearrange("p (a b) -> p a b", a=8)))
                evc = p.run("vector", lambda h, col=col: h.tensor_tensor(
                    wqTl[:, :, col], pq32.rearrange("p (a b) -> p a b", a=8),
                    wqT[:, :, col], ALU.subtract))
        for skap, dst in ((w["sk1"], sk1T), (w["sk2"], sk2T)):
            p.waits("sync", [evc])
            p.dma("sync", wtmp[:, 0:128], skap[:, :], chan=cwh)
            p.dwait("tensor", cwh)
            p.waits("tensor", [evc])
            ev_tr = p.run("tensor", lambda h: h.transpose(pool[:, 2, 0:128], wtmp[:, 0:128],
                                                          identf[:, :]), inc=1)
            p.waits("vector", [ev_tr])
            evc = p.run("vector", lambda h, dst=dst: h.tensor_copy(dst[:, :], pool[:, 2, 0:128]))
        ev_wpre = evc

        if not skip_hall:
            ev_ge = None
            for et in range(NE // 128):
                p.waits("sync", [evc])
                p.dma("sync", wtmp[:, 0:256], w["latents"][et * 128:(et + 1) * 128, :], chan=cwh)
                p.dwait("vector", cwh)
                evc = p.run("vector", lambda h: h.tensor_copy(xnbf[:, 0:256], wtmp[:, 0:256]))
                p.waits("tensor", [evc])
                for dc in range(2):
                    ev_tr = p.run("tensor", lambda h, dc=dc: h.transpose(
                        ptr[:, dc * 128:(dc + 1) * 128], xnbf[:, dc * 128:(dc + 1) * 128],
                        ident[:]), inc=(dc == 1))
                p.waits("vector", [ev_tr])
                evc2 = p.run("vector", lambda h: h.tensor_copy(
                    lat_t[:, :, :], ptr[:, 0:256].rearrange("p (a b) -> p a b", a=2)))
                p.waits("tensor", [evc2, ev_ge])
                for k in range(2):
                    ev_hm = p.run("tensor", lambda h, k=k: h.matmul(
                        pa, lat_t[:, k, :], w1_sb[:, k, :],
                        start=(k == 0), stop=(k == 1)), inc=(k == 1))
                p.waits("scalar", [ev_hm])
                p.dwait("scalar", name + "chs")
                ev_ge = p.run("scalar", lambda h: h.activation(out=hge[:, :], in_=pa, func=AF.Gelu))
                p.waits("sync", [ev_ge])
                p.dma("sync", h_scr[et * 128:(et + 1) * 128, :], hge[:, :], chan=name + "chs")
            ev_hall = ev_ge
        else:
            ev_hall = h_ready

        ev_xuse = [ev_wpre, ev_wpre]
        ev_ptuse = ev_wpre
        ev_tr_last = ev_tr
        for i in range(nt):
            par = i % 2
            xt = xt2[par]
            cx = name + "cxa%d" % par
            p.waits("sync", [ev_xuse[par]])
            p.dma("sync", xt[:, :], x_in[i * 128:(i + 1) * 128, :], chan=cx)
            p.dwait("scalar", cx)
            ev_sq = p.run("scalar", lambda h, xt=xt: h.square(xsq[:], xt[:]))
            p.dwait("vector", cx)
            p.run("vector", lambda h, xt=xt: h.reduce_sum(s1t[:], xt[:], axis=mybir.AxisListType.X))
            p.waits("vector", [ev_sq])
            p.run("vector", lambda h: h.reduce_sum(s2t[:], xsq[:], axis=mybir.AxisListType.X))
            p.run("vector", lambda h: h.tensor_scalar_mul(mean[:], s1t[:], 1.0 / D))
            p.run("vector", lambda h: h.tensor_scalar_mul(s2t[:], s2t[:], 1.0 / D))
            p.run("vector", lambda h: h.tensor_tensor(m2[:], mean[:], mean[:], ALU.mult))
            ev_v = p.run("vector", lambda h: h.tensor_tensor(var[:], s2t[:], m2[:], ALU.subtract))
            p.waits("scalar", [ev_v, cc.ev(1e-5)])
            p.run("scalar", lambda h, i=i: h.activation(out=rstd_all[:, i:i + 1], in_=var[:],
                                                        func=AF.Ln, bias=cc.ap(1e-5)))
            ev_s = p.run("scalar", lambda h, i=i: h.activation(
                out=rstd_all[:, i:i + 1], in_=rstd_all[:, i:i + 1], func=AF.Exp, scale=-0.5))
            p.waits("vector", [ev_s])
            ev_nb = p.run("vector", lambda h, i=i: h.scalar_tensor_tensor(
                nbias_all[:, i:i + 1], mean[:], -1.0, rstd_all[:, i:i + 1], ALU.mult, ALU.mult))
            p.waits("scalar", [ev_nb, ev_tr_last])
            ev_xn = p.run("scalar", lambda h, xt=xt, i=i: h.activation(
                out=xnf[:], in_=xt[:], func=AF.Identity,
                scale=rstd_all[:, i:i + 1], bias=nbias_all[:, i:i + 1]))
            ev_xuse[par] = ev_xn
            p.waits("vector", [ev_xn])
            ev_xh = p.run("vector", lambda h: h.tensor_copy(xnbf[:], xnf[:]))
            ev_xl = p.run("vector", lambda h: h.tensor_tensor(xlbf[:], xnf[:], xnbf[:], ALU.subtract))
            p.waits("tensor", [ev_xh, ev_ptuse])
            for dc in range(8):
                ev_tr = p.run("tensor", lambda h, dc=dc: h.transpose(
                    ptr[:, dc * 128:(dc + 1) * 128], xnbf[:, dc * 128:(dc + 1) * 128], ident[:]),
                    inc=(dc == 7))
            p.waits("scalar", [ev_tr])
            for dc in range(8):
                ev_ptuse = p.run("scalar", lambda h, dc=dc, i=i: h.activation(
                    out=xnT[:, dc, i * 128:(i + 1) * 128], in_=ptr[:, dc * 128:(dc + 1) * 128],
                    func=AF.Copy))
            p.waits("tensor", [ev_xl, ev_ptuse])
            for dc in range(8):
                ev_tr = p.run("tensor", lambda h, dc=dc: h.transpose(
                    ptr[:, dc * 128:(dc + 1) * 128], xlbf[:, dc * 128:(dc + 1) * 128], ident[:]),
                    inc=(dc == 7))
            ev_tr_last = ev_tr
            p.dwait("scalar", name + "cxl")
            p.waits("scalar", [ev_tr])
            for dc in range(8):
                ev_ptuse = p.run("scalar", lambda h, dc=dc: h.activation(
                    out=xlT[:, dc, :], in_=ptr[:, dc * 128:(dc + 1) * 128], func=AF.Copy))
            p.waits("sync", [ev_ptuse])
            for c2 in range(8):
                p.dma("sync", xl_scr[c2, :, i * 128:(i + 1) * 128], xlT[:, c2, :],
                      chan=name + "cxl")

        ev_z = p.run("vector", lambda h: h.memset(qsum[:, :], 0.0))
        ev_qe = None
        ev_xlb = None
        for tbi in range(ntb):
            p.waits("sync", [ev_xlb])
            p.dwait("sync", name + "cxl")
            for c2 in range(8):
                p.dma("sync", xlblk[:, c2, :], xl_scr[c2, :, tbi * 512:(tbi + 1) * 512],
                      chan=name + "cxb")
            for c in range(8):
                pp = psc[c % 2]
                p.waits("tensor", [ev_ptuse, ev_qe])
                p.dwait("tensor", name + "cxb")
                for dc in range(8):
                    p.run("tensor", lambda h, c=c, tbi=tbi, dc=dc, pp=pp: h.matmul(
                        pp, wqT[:, dc, c * 128:(c + 1) * 128],
                        xnT[:, dc, tbi * 512:(tbi + 1) * 512],
                        start=(dc == 0), stop=False))
                for dc in range(8):
                    p.run("tensor", lambda h, c=c, dc=dc, pp=pp: h.matmul(
                        pp, wqT[:, dc, c * 128:(c + 1) * 128], xlblk[:, dc, :],
                        start=False, stop=False))
                for dc in range(8):
                    ev_qm = p.run("tensor", lambda h, c=c, tbi=tbi, dc=dc, pp=pp: h.matmul(
                        pp, wqTl[:, dc, c * 128:(c + 1) * 128],
                        xnT[:, dc, tbi * 512:(tbi + 1) * 512],
                        start=False, stop=(dc == 7)), inc=(dc == 7))
                ev_xlb = ev_qm
                p.waits("scalar", [ev_qm, ev_z])
                p.dwait("scalar", name + "cqs")
                ev_qe = p.run("scalar", lambda h, c=c, tbi=tbi, pp=pp: h.activation(
                    out=qtmp[:, :], in_=pp, func=AF.Identity, bias=bqcol[:, c:c + 1]))
                p.waits("sync", [ev_qe])
                p.dma("sync", q_scr[c, :, tbi * 512:(tbi + 1) * 512], qtmp[:, :],
                      chan=name + "cqs")
                p.waits("vector", [ev_qe, ev_z])
                p.run("vector", lambda h: h.reduce_sum(s1t[:], qtmp[:, :], axis=mybir.AxisListType.X))
                p.run("vector", lambda h, c=c: h.tensor_tensor(
                    qsum[:, c:c + 1], qsum[:, c:c + 1], s1t[:], ALU.add))
                p.waits("scalar", [ev_z])
                ev_sqs = p.run("scalar", lambda h: h.square(xsq[:, 0:512], qtmp[:, :]))
                p.waits("vector", [ev_sqs])
                p.run("vector", lambda h: h.reduce_sum(s2t[:], xsq[:, 0:512], axis=mybir.AxisListType.X))
                ev_z = p.run("vector", lambda h, c=c: h.tensor_tensor(
                    qsum[:, 8 + c:9 + c], qsum[:, 8 + c:9 + c], s2t[:], ALU.add))

        p.waits("gpsimd", [ev_z])
        ev_st = p.cinc("gpsimd", lambda h: h.dma_start(out=cc_in[:, :], in_=qsum[:, :]),
                       name + "ccs")
        p.wait("gpsimd", ev_st)
        ev_cc = p.cinc("gpsimd", lambda h: h.collective_compute(
            "AllReduce", ALU.add, replica_groups=[list(range(NCORES))],
            ins=[cc_in.opt()], outs=[cc_out.opt()]), name + "ccol", inc=1)
        p.wait("gpsimd", ev_cc)
        ev_ld = p.cinc("gpsimd", lambda h: h.dma_start(out=qsum[:, :], in_=cc_out[:, :]),
                       name + "ccl")
        p.waits("vector", [ev_ld])
        p.run("vector", lambda h: h.tensor_scalar_mul(qsum[:, 0:8], qsum[:, 0:8], 1.0 / NTOK_GLOBAL))
        ev_vv = p.run("vector", lambda h: h.tensor_scalar_mul(qsum[:, 8:16], qsum[:, 8:16], 1.0 / NTOK_GLOBAL))
        p.run("vector", lambda h: h.tensor_tensor(qsc[:, :], qsum[:, 0:8], qsum[:, 0:8], ALU.mult))
        ev_vv = p.run("vector", lambda h: h.tensor_tensor(qsc[:, :], qsum[:, 8:16], qsc[:, :], ALU.subtract))
        p.waits("scalar", [ev_vv, cc.ev(1e-5)])
        p.run("scalar", lambda h: h.activation(out=qbs[:, :], in_=qsc[:, :], func=AF.Ln, bias=cc.ap(1e-5)))
        ev_ss2 = p.run("scalar", lambda h: h.activation(out=qbs[:, :], in_=qbs[:, :], func=AF.Exp, scale=-0.5))
        p.waits("vector", [ev_ss2])
        p.run("vector", lambda h: h.tensor_tensor(qsc[:, :], qbs[:, :], gamcol[:, :], ALU.mult))
        p.run("vector", lambda h: h.tensor_tensor(qbs[:, :], qsum[:, 0:8], qsc[:, :], ALU.mult))
        ev_bn = p.run("vector", lambda h: h.tensor_tensor(qbs[:, :], betcol[:, :], qbs[:, :], ALU.subtract))

        ev_qnT_use = None
        ev_ss_use = None
        ev_epi = None
        ev_fm = None
        ev_ei = None
        ev_hg_use_par = [None, None]
        ev_xp_par = [None, None]
        ev_pr_par = [None, None]
        hgb = [hg, hg2]
        xpb = [xp, xp2]
        probsb = [probs, probs2]

        def partA(i):
            nonlocal ev_qnT_use, ev_ss_use, ev_ei
            par = i % 2
            xt = xt2[par]
            cx = name + "cxb%d" % par
            probs = probsb[par]
            p.waits("sync", [ev_epi, ev_xuse[par]])
            p.dma("sync", xt[:, :], x_in[i * 128:(i + 1) * 128, :], chan=cx)
            cq = name + "cq%d" % (i % 2)
            p.waits("sync", [ev_qnT_use])
            p.dwait("sync", name + "cqs")
            for c in range(8):
                p.dma("sync", qf[:, c, :], q_scr[c, :, i * 128:(i + 1) * 128], chan=cq)
            p.dwait("scalar", cq)
            p.waits("scalar", [ev_bn, ev_qnT_use])
            for c in range(8):
                ev_qn = p.run("scalar", lambda h, c=c: h.activation(
                    out=qnT[:, c, :], in_=qf[:, c, :], func=AF.Identity,
                    scale=qsc[:, c:c + 1], bias=qbs[:, c:c + 1]))
            p.waits("tensor", [ev_qn])
            for hh in range(NHE):
                for side in range(2):
                    p.waits("tensor", [ev_ss_use])
                    ev_sm = p.run("tensor", lambda h, hh=hh, side=side: h.matmul(
                        pool[:, 2 + side, 0:128],
                        qnT[:, 2 * hh + side, :], sk1T[:, :] if side == 0 else sk2T[:, :],
                        start=True, stop=True), inc=1)
                    p.waits("vector", [ev_sm])
                    ev_ss_use = p.run("vector", lambda h, hh=hh, side=side: h.tensor_copy(
                        ss[:, (0 if side == 0 else 4) + hh, :], pool[:, 2 + side, 0:128]))
            ev_qnT_use = ev_sm
            for side, tsv, tiv, tivf in ((0, ts1, ti1, ti1f), (1, ts2, ti2, ti2f)):
                for hh in range(NHE):
                    src = ss[:, (0 if side == 0 else 4) + hh, :]
                    p.run("vector", lambda h, src=src, tsv=tsv, hh=hh: h.max(
                        out=tsv[:, hh, 0:8], in_=src))
                    p.run("vector", lambda h, src=src, tsv=tsv, hh=hh: h.match_replace(
                        out=xsq[:, 0:128], in_to_replace=tsv[:, hh, 0:8], in_values=src,
                        imm_value=-1e30))
                    p.run("vector", lambda h, tsv=tsv, hh=hh: h.max(
                        out=tsv[:, hh, 8:16], in_=xsq[:, 0:128]))
                    p.run("vector", lambda h, src=src, tsv=tsv, tiv=tiv, hh=hh: h.max_index(
                        out=tiv[:, hh, 0:8], in_max=tsv[:, hh, 0:8], in_values=src))
                    ev_tk = p.run("vector", lambda h, src=src, tsv=tsv, tiv=tiv, hh=hh: h.max_index(
                        out=tiv[:, hh, 8:16], in_max=tsv[:, hh, 8:16], in_values=src))
            for hh in range(NHE):
                p.run("vector", lambda h, hh=hh: h.tensor_tensor(
                    comb[:, hh, :].rearrange("p (a b) -> p a b", a=16),
                    ts1[:, hh, :].to_broadcast([128, 16, 16]),
                    ts2[:, hh, :].rearrange("p (o k) -> p o k", o=1).to_broadcast([128, 16, 16]),
                    ALU.add))
                p.run("vector", lambda h, hh=hh: h.max(out=fs[:, hh, :], in_=comb[:, hh, :]))
                ev_tk = p.run("vector", lambda h, hh=hh: h.max_index(
                    out=tci[:, hh, :], in_max=fs[:, hh, :], in_values=comb[:, hh, :]))
            p.waits("scalar", [ev_tk, ev_pr_par[par]])
            ev_ex = p.run("scalar", lambda h, probs=probs: h.activation(
                out=probs[:, :, :], in_=fs[:, :, :], func=AF.Exp))
            p.waits("vector", [ev_ex])
            p.run("vector", lambda h: h.tensor_tensor(eoh[:, :, 0:4, 0], probs[:, :, 0:4],
                                                      probs[:, :, 4:8], ALU.add))
            p.run("vector", lambda h: h.tensor_tensor(eoh[:, :, 0:2, 0], eoh[:, :, 0:2, 0],
                                                      eoh[:, :, 2:4, 0], ALU.add))
            p.run("vector", lambda h: h.tensor_tensor(psum8[:, :, 0], eoh[:, :, 0:1, 0],
                                                      eoh[:, :, 1:2, 0], ALU.add))
            p.run("vector", lambda h: h.reciprocal(psum8[:, :, :], psum8[:, :, :]))
            p.run("vector", lambda h, probs=probs: h.tensor_tensor(
                probs[:, :, :], probs[:, :, :],
                psum8[:, :, 0:1].to_broadcast([128, 4, 8]), ALU.mult))
            p.dwait("vector", name + "cg%d" % (1 - par))
            p.run("vector", lambda h: h.tensor_copy(tcif[:, :, :], tci[:, :, :]))
            p.run("vector", lambda h: h.tensor_copy(ti1f[:, :, :], ti1[:, :, :]))
            p.run("vector", lambda h: h.tensor_copy(ti2f[:, :, :], ti2[:, :, :]))
            MAGIC = 12582912.0
            p.run("vector", lambda h: h.tensor_scalar(idiv[:, :, :], tcif[:, :, :], -7.5, 1.0 / 16.0,
                                                      ALU.add, ALU.mult))
            p.run("vector", lambda h: h.tensor_scalar(idiv[:, :, :], idiv[:, :, :], MAGIC, MAGIC,
                                                      ALU.add, ALU.subtract))
            p.run("vector", lambda h: h.scalar_tensor_tensor(tmod[:, :, :], idiv[:, :, :], -16.0,
                                                             tcif[:, :, :], ALU.mult, ALU.add))
            for (idxv, tivf, outv) in ((idiv, ti1f, i1v), (tmod, ti2f, i2v)):
                p.run("vector", lambda h, idxv=idxv: h.tensor_tensor(
                    eoh[:, :, :, :],
                    idxv[:, :, :].to_broadcast([128, 4, 8, 16]),
                    iota16[:, :].rearrange("p (a b k) -> p a b k", a=1, b=1).to_broadcast([128, 4, 8, 16]),
                    ALU.is_equal))
                p.run("vector", lambda h, tivf=tivf: h.tensor_tensor(
                    eoh[:, :, :, :], eoh[:, :, :, :],
                    tivf[:, :, :].rearrange("p h (o k) -> p h o k", o=1).to_broadcast([128, 4, 8, 16]),
                    ALU.mult))
                p.run("vector", lambda h: h.tensor_tensor(eoh[:, :, :, 0:8], eoh[:, :, :, 0:8],
                                                          eoh[:, :, :, 8:16], ALU.add))
                p.run("vector", lambda h: h.tensor_tensor(eoh[:, :, :, 0:4], eoh[:, :, :, 0:4],
                                                          eoh[:, :, :, 4:8], ALU.add))
                p.run("vector", lambda h: h.tensor_tensor(eoh[:, :, :, 0:2], eoh[:, :, :, 0:2],
                                                          eoh[:, :, :, 2:4], ALU.add))
                p.run("vector", lambda h, outv=outv: h.tensor_tensor(
                    outv[:, :, :], eoh[:, :, :, 0], eoh[:, :, :, 1], ALU.add))
            p.run("vector", lambda h: h.scalar_tensor_tensor(
                i1v[:, :, :], i1v[:, :, :], 128.0, i2v[:, :, :], ALU.mult, ALU.add))
            ev_ei = p.run("vector", lambda h: h.tensor_copy(
                eidx[:, :], i1v[:, :, :].rearrange("p h k -> p (h k)")))
            p.dwait("gpsimd", name + "chs")
            p.waits("gpsimd", [ev_ei, ev_hg_use_par[par], ev_hall])
            for k in range(32):
                p.cinc("gpsimd", lambda h, k=k, par=par: h.indirect_dma_start(
                    out=hgb[par][:, k, :], out_offset=None,
                    in_=h_scr.rearrange("(p g) f -> p g f", p=128),
                    in_offset=IndirectOffsetOnAxis(eidx[:, k:k + 1], 1)), name + "cg%d" % par)
            p.waits("tensor", [ev_xp_par[par], ev_ptuse])
            for dc in range(8):
                ev_xm = p.run("tensor", lambda h, dc=dc, i=i: h.matmul(
                    pxp, xnT[:, dc, i * 128:(i + 1) * 128], w2aT[:, dc, :],
                    start=(dc == 0), stop=(dc == 7)), inc=(dc == 7))
            p.waits("vector", [ev_xm])
            p.run("vector", lambda h, par=par: h.tensor_copy(xpb[par][:, :], pxp))

        def partB(i):
            nonlocal ev_epi, ev_fm
            par = i % 2
            xt = xt2[par]
            cx = name + "cxb%d" % par
            hg = hgb[par]
            probs = probsb[par]
            p.dwait("vector", name + "cg%d" % par)
            for k in range(32):
                ev_dot = p.run("vector", lambda h, k=k, hg=hg, par=par: h.scalar_tensor_tensor(
                    dump[:, :], hg[:, k, :], 1.0, xpb[par][:, :], ALU.bypass, ALU.mult,
                    accum_out=dots[:, k:k + 1]))
            ev_xp_par[par] = ev_dot
            p.waits("scalar", [ev_dot])
            ev_gl = p.run("scalar", lambda h: h.activation(out=act[:, :], in_=dots[:, :], func=AF.Gelu))
            p.waits("vector", [ev_gl])
            ev_act = p.run("vector", lambda h, probs=probs: h.tensor_tensor(
                act[:, :], act[:, :], probs[:, :, :].rearrange("p h k -> p (h k)"), ALU.mult))
            ev_pr_par[par] = ev_act
            ev_acc = p.run("vector", lambda h, hg=hg: h.tensor_scalar(
                accb[:, :], hg[:, 0, :], act[:, 0:1], None, ALU.mult))
            for k in range(1, 32):
                ev_acc = p.run("vector", lambda h, k=k, hg=hg: h.scalar_tensor_tensor(
                    accb[:, :], hg[:, k, :], act[:, k:k + 1], accb[:, :], ALU.mult, ALU.add))
            ev_hg_use_par[par] = ev_acc
            ev_accbf = p.run("vector", lambda h: h.tensor_copy(accbf[:, :], accb[:, :]))
            p.waits("tensor", [ev_accbf, ev_fm])
            for dc in range(4):
                ev_tra = p.run("tensor", lambda h, dc=dc: h.transpose(
                    ptr[:, dc * 128:(dc + 1) * 128], accbf[:, dc * 128:(dc + 1) * 128], ident[:]),
                    inc=(dc == 3))
            p.waits("vector", [ev_tra])
            ev_accT = p.run("vector", lambda h: h.tensor_copy(
                accT[:, :, :], ptr[:, 0:512].rearrange("p (a b) -> p a b", a=4)))
            p.waits("tensor", [ev_accT, ev_epi])
            for fc in range(4):
                for nb2 in range(2):
                    ev_fm = p.run("tensor", lambda h, fc=fc, nb2=nb2: h.matmul(
                        pfin[:, nb2 * 512:(nb2 + 1) * 512], accT[:, fc, :],
                        wv_sb[:, fc, nb2 * 512:(nb2 + 1) * 512],
                        start=(fc == 0), stop=(fc == 3)), inc=(fc == 3 and nb2 == 1))
            p.waits("scalar", [ev_fm])
            p.dwait("scalar", name + "cso")
            ev_ep = p.run("scalar", lambda h: h.activation(out=xnf[:, :], in_=pfin,
                                                           func=AF.Identity, scale=0.25))
            p.dwait("vector", cx)
            p.waits("vector", [ev_ep])
            ev_epi = p.run("vector", lambda h, xt=xt: h.tensor_tensor(
                xnf[:, :], xnf[:, :], xt[:], ALU.add))
            ev_xuse[par] = ev_epi
            p.waits("sync", [ev_epi])
            p.dma("sync", x_out[i * 128:(i + 1) * 128, :], xnf[:, :], chan=name + "cso")

        for i in range(nt + 1):
            if i < nt:
                partA(i)
            if i >= 1:
                partB(i - 1)
        p.dwait("sync", name + "cso")
        return ev_hall

    return build_l1ffn

build_l1ffn = _mk_build_l1ffn()


def _mk_build_l1attnb():

    T = 4096
    D = 1024
    NH = 8
    HD = 64
    SEG = 1024
    C = 32
    CLAMP = 35.0
    MAG = 12582912.0


    def build_l1attnb(p, t, cc, consts, x_in, x_out, w, name="a1b"):
        STOP = 0
        nc = p.nc
        ident, ones1, ev_ident = consts
        identf = t.handles["c_identf"]
        iota_r = t.handles["c_iota_r"]
        iota_c = t.handles["c_iota_c"]
        nt = T // 128
        nseg = T // SEG
        ntt = SEG // 128
        nck = SEG // C

        xnT = t.sb(name + "_xnT", [128, 8, SEG], BF16)
        wsel = t.sb(name + "_wsel", [128, 8, 160], BF16)
        dtw = t.sb(name + "_dtw", [32, 512], BF16)
        wk4 = t.sb(name + "_wk4", [128, 8, 2048], BF16)
        wo_sb = t.sb(name + "_wo", [128, 4, 1024], BF16)
        dbrep = t.sb(name + "_dbrep", [128, 512], F32)
        bdrep = t.sb(name + "_bdrep", [128, 512], F32)
        tfrep = t.sb(name + "_tfrep", [128, 512], F32)
        rstd_all = t.sb(name + "_rstdall", [128, nt], F32)
        nbias_all = t.sb(name + "_nbiasall", [128, nt], F32)
        TRI4 = t.sb(name + "_TRI4", [128, 32], F32)
        ONES32 = t.sb(name + "_ONES32", [128, 32], F32)
        mask2r = t.sb(name + "_mask2r", [128, 32], F32)
        IDR32 = t.sb(name + "_IDR32", [128, 32], F32)
        S = t.sb(name + "_S", [128, 4, HD], F32)
        kT = t.sb(name + "_kT", [128, 4, SEG], BF16)
        knat = t.sb(name + "_knat", [128, ntt, 512], BF16)
        vnat = t.sb(name + "_vnat", [128, ntt, 512], BF16)
        bvnat = t.sb(name + "_bvnat", [128, ntt, 512], BF16)
        rsig = t.sb(name + "_rsig", [128, ntt, 512], BF16)
        gsig = t.sb(name + "_gsig", [128, ntt, 512], BF16)
        ysn = t.sb(name + "_ysn", [128, ntt, 512], BF16)
        ysT = t.sb(name + "_ysT", [128, 4, SEG], BF16)
        logdn = t.sb(name + "_logdn", [128, ntt, 512], F32)
        dtrawT = t.sb(name + "_dtrawT", [32, SEG], BF16)
        bcol = t.sb(name + "_bcol", [128, ntt], F32)
        Lcm = t.sb(name + "_Lcm", [128, 4, C], F32)
        Ltok = t.sb(name + "_Ltok", [128, 512], F32)
        Qt = t.sb(name + "_Qt", [128, 4, C], F32)
        Q2 = t.sb(name + "_Q2", [128, 4, C], BF16)
        K2 = t.sb(name + "_K2", [128, 4, C], BF16)
        Q3 = t.sb(name + "_Q3", [128, 4, 16], BF16)
        K3 = t.sb(name + "_K3", [128, 4, 16], BF16)
        khat = t.sb(name + "_khat", [128, 512], BF16)
        Asb = t.sb(name + "_Asb", [128, 8, C], BF16)
        Dend = t.sb(name + "_Dend", [128, 4], F32)
        e2 = t.sb(name + "_e2", [128, 4, C], F32)
        e3 = t.sb(name + "_e3", [128, 4, C], F32)
        st1 = t.sb(name + "_st1", [128, 512], F32)
        wtmp = t.sb(name + "_wtmp", [128, 2048], F32)
        xnbf = t.sb(name + "_xnbf", [128, D], BF16)
        xt2 = [t.sb(name + "_x0", [128, D], F32), t.sb(name + "_x1", [128, D], F32)]
        xsq = t.sb(name + "_xsq", [128, D], F32)
        s1 = t.sb(name + "_s1", [128, 1], F32)
        s2 = t.sb(name + "_s2", [128, 1], F32)
        mean = t.sb(name + "_mean", [128, 1], F32)
        m2 = t.sb(name + "_m2", [128, 1], F32)
        var = t.sb(name + "_var", [128, 1], F32)

        pool = t.ps(name + "_pool", [128, 7, 512], F32)
        ptr = t.ps(name + "_ptr", [128, 1024], BF16)
        pW = pool[:, 0, :]
        pU = pool[:, 1, :]
        pA = pool[:, 2, :]
        pL = pool[:, 3, :]
        pS2 = pool[:, 4, :]
        pTr = pool[:, 5, :]
        pP = pool[:, 6, :]
        pP2 = pool[:, 5, :]

        cwb = name + "cwb"

        rowtmp = wtmp[0:1, 0:512]
        ev_rd = None
        for idx, (src, dst) in enumerate(((w["dt_b"], dbrep),
                                          (w["base_decay"].rearrange("h k -> (h k)"), bdrep),
                                          (w["time_first"].rearrange("h k -> (h k)"), tfrep))):
            ch = cwb + str(idx)
            p.waits("sync", [ev_rd])
            p.dma("sync", rowtmp, src.rearrange("(o k) -> o k", o=1), chan=ch)
            p.dwait("tensor", ch)
            p.waits("tensor", [ev_ident, ev_rd])
            ev_b = p.run("tensor", lambda h: h.matmul(pP, ones1[:, :], rowtmp,
                                                      start=True, stop=True), inc=1)
            p.waits("vector", [ev_b])
            ev_rd = p.run("vector", lambda h, dst=dst: h.tensor_copy(dst[:, :], pP))

        ev_g1 = p.run("gpsimd", lambda h: h.memset(ONES32[:, :], 1.0))
        ev_sz = p.run("gpsimd", lambda h: h.memset(S[:, :, :], 0.0))
        p.waits("vector", [ev_g1, ev_rd])
        p.run("vector", lambda h: h.tensor_scalar(st1[:, 0:1], iota_r[:, 0:1], -15.5, 1.0 / 32.0,
                                                  ALU.add, ALU.mult))
        p.run("vector", lambda h: h.tensor_scalar(st1[:, 0:1], st1[:, 0:1], MAG, MAG,
                                                  ALU.add, ALU.subtract))
        p.run("vector", lambda h: h.scalar_tensor_tensor(st1[:, 1:2], st1[:, 0:1], -32.0,
                                                         iota_r[:, 0:1], ALU.mult, ALU.add))
        ev_tri = p.run("vector", lambda h: h.tensor_tensor(
            TRI4[:, :], st1[:, 1:2].to_broadcast([128, 32]), iota_c[:, 0:32], ALU.is_le))
        p.run("vector", lambda h: h.tensor_scalar(st1[:, 2:3], st1[:, 1:2], -7.5, 1.0 / 16.0,
                                                  ALU.add, ALU.mult))
        p.run("vector", lambda h: h.tensor_scalar(st1[:, 2:3], st1[:, 2:3], MAG, MAG,
                                                  ALU.add, ALU.subtract))
        p.run("vector", lambda h: h.tensor_scalar(st1[:, 3:35], iota_c[:, 0:32], -7.5, 1.0 / 16.0,
                                                  ALU.add, ALU.mult))
        p.run("vector", lambda h: h.tensor_scalar(st1[:, 3:35], st1[:, 3:35], MAG, MAG,
                                                  ALU.add, ALU.subtract))
        p.run("vector", lambda h: h.tensor_tensor(
            st1[:, 3:35], st1[:, 3:35], st1[:, 2:3].to_broadcast([128, 32]), ALU.is_equal))
        p.run("vector", lambda h: h.tensor_tensor(
            mask2r[:, :], st1[:, 1:2].to_broadcast([128, 32]), iota_c[:, 0:32], ALU.is_le))
        p.run("vector", lambda h: h.tensor_tensor(
            mask2r[:, :], mask2r[:, :], st1[:, 3:35], ALU.mult))
        ev_msk = p.run("vector", lambda h: h.tensor_tensor(
            IDR32[:, :], st1[:, 1:2].to_broadcast([128, 32]), iota_c[:, 0:32], ALU.is_equal))

        cwh = name + "cwh"
        evc = p.ev("vector")
        p.waits("sync", [evc])
        p.dma("sync", wtmp[0:32, 0:512], w["dt_W"][:, :], chan=cwh)
        p.dwait("vector", cwh)
        evc = p.run("vector", lambda h: h.tensor_copy(dtw[:, :], wtmp[0:32, 0:512]))
        for c in range(8):
            p.waits("sync", [evc])
            p.dma("sync", wtmp[:, 0:160], w["W_sel"][c * 128:(c + 1) * 128, :], chan=cwh)
            p.dwait("vector", cwh)
            evc = p.run("vector", lambda h, c=c: h.tensor_copy(wsel[:, c, :], wtmp[:, 0:160]))
        for c in range(8):
            for j, wn in enumerate(("Wk", "Wv", "Wr", "Wg")):
                p.waits("sync", [evc])
                p.dma("sync", wtmp[:, j * 512:(j + 1) * 512], w[wn][c * 128:(c + 1) * 128, :],
                      chan=cwh)
                p.dwait("vector", cwh)
                evc = p.run("vector", lambda h, c=c, j=j: h.tensor_copy(
                    wk4[:, c, j * 512:(j + 1) * 512], wtmp[:, j * 512:(j + 1) * 512]))
        for c in range(4):
            p.waits("sync", [evc])
            p.dma("sync", wtmp[:, 0:1024], w["Wo"][c * 128:(c + 1) * 128, :], chan=cwh)
            p.dwait("vector", cwh)
            evc = p.run("vector", lambda h, c=c: h.tensor_copy(wo_sb[:, c, :], wtmp[:, 0:1024]))
        ev_wpre = evc

        ev_xuse = [ev_wpre, ev_wpre]
        ev_ptuse = ev_wpre
        ev_tr_last = None
        ev_pP = None
        ev_pP2 = ev_msk
        ev_ys_ev = None
        ev_S_upd = ev_sz
        ev_U_mm = None
        ev_intra = None
        ev_Aev = ev_msk
        ev_Lev = None
        ev_kh = None
        ev_fac = None
        ev_Lcm_ev = None
        ev_epi = None
        ev_md = None
        ev_ysT_ev = None
        for sg in range(nseg):
            tg0 = sg * SEG
            p.waits("tensor", [ev_ys_ev, ev_Aev, ev_kh, ev_fac, ev_S_upd, ev_md, ev_epi])
            for tt0 in range(ntt):
                i = tg0 // 128 + tt0
                par = i % 2
                xt = xt2[par]
                cx = name + "cxa%d" % par
                p.waits("sync", [ev_xuse[par]])
                p.dma("sync", xt[:, :], x_in[i * 128:(i + 1) * 128, :], chan=cx)
                p.dwait("scalar", cx)
                ev_sq = p.run("scalar", lambda h, xt=xt: h.square(xsq[:], xt[:]))
                p.dwait("vector", cx)
                p.run("vector", lambda h, xt=xt: h.reduce_sum(s1[:], xt[:], axis=mybir.AxisListType.X))
                p.waits("vector", [ev_sq])
                p.run("vector", lambda h: h.reduce_sum(s2[:], xsq[:], axis=mybir.AxisListType.X))
                p.run("vector", lambda h: h.tensor_scalar_mul(mean[:], s1[:], 1.0 / D))
                p.run("vector", lambda h: h.tensor_scalar_mul(s2[:], s2[:], 1.0 / D))
                p.run("vector", lambda h: h.tensor_tensor(m2[:], mean[:], mean[:], ALU.mult))
                ev_v = p.run("vector", lambda h: h.tensor_tensor(var[:], s2[:], m2[:], ALU.subtract))
                p.waits("scalar", [ev_v, cc.ev(1e-5)])
                p.run("scalar", lambda h, i=i: h.activation(out=rstd_all[:, i:i + 1], in_=var[:],
                                                            func=AF.Ln, bias=cc.ap(1e-5)))
                ev_s = p.run("scalar", lambda h, i=i: h.activation(
                    out=rstd_all[:, i:i + 1], in_=rstd_all[:, i:i + 1], func=AF.Exp, scale=-0.5))
                p.waits("vector", [ev_s])
                ev_nb = p.run("vector", lambda h, i=i: h.scalar_tensor_tensor(
                    nbias_all[:, i:i + 1], mean[:], -1.0, rstd_all[:, i:i + 1], ALU.mult, ALU.mult))
                p.waits("scalar", [ev_nb, ev_tr_last])
                ev_xn = p.run("scalar", lambda h, xt=xt, i=i: h.activation(
                    out=xnbf[:], in_=xt[:], func=AF.Identity,
                    scale=rstd_all[:, i:i + 1], bias=nbias_all[:, i:i + 1]))
                ev_xuse[par] = ev_xn
                p.waits("tensor", [ev_xn, ev_ptuse])
                for dc in range(8):
                    ev_tr = p.run("tensor", lambda h, dc=dc: h.transpose(
                        ptr[:, dc * 128:(dc + 1) * 128], xnbf[:, dc * 128:(dc + 1) * 128], ident[:]),
                        inc=(dc == 7))
                ev_tr_last = ev_tr
                p.waits("scalar", [ev_tr])
                for dc in range(8):
                    ev_ptuse = p.run("scalar", lambda h, dc=dc, tt0=tt0: h.activation(
                        out=xnT[:, dc, tt0 * 128:(tt0 + 1) * 128], in_=ptr[:, dc * 128:(dc + 1) * 128],
                        func=AF.Copy))


            for g in range(4):
                for tb2 in range(SEG // 512):
                    p.waits("tensor", [ev_pP, ev_ptuse])
                    for dc in range(8):
                        ev_m = p.run("tensor", lambda h, g=g, tb2=tb2, dc=dc, tg0=tg0: h.matmul(
                            pP, wk4[:, dc, g * 128:(g + 1) * 128],
                            xnT[:, dc, tb2 * 512:(tb2 + 1) * 512],
                            start=(dc == 0), stop=(dc == 7)), inc=(dc == 7))
                    p.waits("scalar", [ev_m])
                    ev_pP = p.run("scalar", lambda h, g=g, tb2=tb2: h.activation(
                        out=kT[:, g, tb2 * 512:(tb2 + 1) * 512], in_=pP, func=AF.Copy))
            for tt in range(ntt):
                it = tg0 // 128 + tt
                p.waits("tensor", [ev_pP, ev_ptuse])
                for dc in range(8):
                    ev_m = p.run("tensor", lambda h, dc=dc, tt=tt: h.matmul(
                        pP[:, 0:64], xnT[:, dc, tt * 128:(tt + 1) * 128], wsel[:, dc, 32:96],
                        start=(dc == 0), stop=(dc == 7)), inc=(dc == 7))
                p.waits("vector", [ev_m])
                p.run("vector", lambda h: h.reduce_sum(s1[:], pP[:, 0:64], axis=mybir.AxisListType.X))
                ev_pP = p.run("vector", lambda h, tt=tt: h.tensor_scalar_mul(
                    bcol[:, tt:tt + 1], s1[:], 1.0 / 64.0))
                p.waits("tensor", [ev_pP])
                for dc in range(8):
                    ev_m = p.run("tensor", lambda h, dc=dc, tt=tt: h.matmul(
                        pP[0:32, 0:128], wsel[:, dc, 0:32], xnT[:, dc, tt * 128:(tt + 1) * 128],
                        start=(dc == 0), stop=(dc == 7)), inc=(dc == 7))
                p.waits("scalar", [ev_m])
                ev_pP = p.run("scalar", lambda h, tt=tt: h.copy(
                    out=dtrawT[:, tt * 128:(tt + 1) * 128], in_=pP[0:32, 0:128]))
                for j in range(4):
                    p.waits("tensor", [ev_pP, ev_pP2])
                    for dc in range(8):
                        ev_m = p.run("tensor", lambda h, j=j, dc=dc, tt=tt: h.matmul(
                            pP, xnT[:, dc, tt * 128:(tt + 1) * 128],
                            wk4[:, dc, j * 512:(j + 1) * 512],
                            start=(dc == 0), stop=(dc == 7)), inc=(dc == 7))
                    p.waits("scalar", [ev_m])
                    if j == 0:
                        ev_pP = p.run("scalar", lambda h, tt=tt: h.copy(out=knat[:, tt, :], in_=pP))
                    elif j == 1:
                        p.run("scalar", lambda h, tt=tt: h.copy(out=vnat[:, tt, :], in_=pP))
                        ev_pP = p.run("scalar", lambda h, tt=tt: h.activation(
                            out=bvnat[:, tt, :], in_=pP, func=AF.Identity,
                            scale=bcol[:, tt:tt + 1]))
                    elif j == 2:
                        ev_pP = p.run("scalar", lambda h, tt=tt: h.activation(
                            out=rsig[:, tt, :], in_=pP, func=AF.Sigmoid))
                    else:
                        ev_pP = p.run("scalar", lambda h, tt=tt: h.activation(
                            out=gsig[:, tt, :], in_=pP, func=AF.Sigmoid))
                p.waits("tensor", [ev_pP, ev_pP2])
                ev_m = p.run("tensor", lambda h, tt=tt: h.matmul(
                    pP2, dtrawT[:, tt * 128:(tt + 1) * 128], dtw[:, :],
                    start=True, stop=True), inc=1)
                p.waits("vector", [ev_m])
                ev_vv = p.run("vector", lambda h: h.tensor_tensor(st1[:, :], pP2, dbrep[:, :], ALU.add))
                p.waits("scalar", [ev_vv])
                p.dwait("scalar", name + "cso")
                p.run("scalar", lambda h: h.activation(out=wtmp[:, 0:512], in_=st1[:, :], func=AF.Abs))
                p.run("scalar", lambda h: h.activation(out=wtmp[:, 0:512], in_=wtmp[:, 0:512],
                                                       func=AF.Exp, scale=-1.0))
                p.waits("scalar", [cc.ev(1.0)])
                ev_sp = p.run("scalar", lambda h: h.activation(out=wtmp[:, 0:512], in_=wtmp[:, 0:512],
                                                               func=AF.Ln, bias=cc.ap(1.0)))
                p.waits("vector", [ev_sp])
                p.run("vector", lambda h: h.tensor_scalar_max(st1[:, :], st1[:, :], 0.0))
                p.run("vector", lambda h: h.tensor_tensor(st1[:, :], st1[:, :], wtmp[:, 0:512], ALU.add))
                ev_pP2 = p.run("vector", lambda h, tt=tt: h.scalar_tensor_tensor(
                    logdn[:, tt, :], st1[:, :], -1.0, bdrep[:, :], ALU.mult, ALU.mult))

            for ck in range(nck if STOP != 1 else 0):
                tt = ck // 4
                pb = (ck % 4) * 32
                t0 = ck * C
                rows = slice(pb, pb + 32)
                p.waits("tensor", [ev_Lev, ev_kh, ev_tri, ev_pP2, ev_Aev])
                ev_mL = p.run("tensor", lambda h, rows=rows, tt=tt, pb=pb: h.matmul(
                    pL[rows, :], TRI4[rows, :], logdn[rows, tt, :],
                    start=True, stop=True, tile_position=(pb, pb)), inc=1)
                ev_mS = p.run("tensor", lambda h, rows=rows, tt=tt, pb=pb: h.matmul(
                    pS2[rows, :], ONES32[rows, :], logdn[rows, tt, :],
                    start=True, stop=True, tile_position=(pb, pb)), inc=1)
                p.waits("scalar", [ev_mL])
                ev_Lev = p.run("scalar", lambda h, rows=rows: h.copy(out=Ltok[rows, :],
                                                                     in_=pL[rows, :]))
                p.waits("vector", [ev_Lev, ev_mS, ev_U_mm])
                ev_k1 = p.run("vector", lambda h, rows=rows: h.tensor_tensor(
                    st1[rows, :], pS2[rows, :], Ltok[rows, :], ALU.subtract))
                p.waits("scalar", [ev_k1])
                ev_k2 = p.run("scalar", lambda h, rows=rows: h.activation(
                    out=st1[rows, :], in_=st1[rows, :], func=AF.Exp))
                p.waits("vector", [ev_k2])
                ev_kh = p.run("vector", lambda h, rows=rows, tt=tt: h.tensor_tensor(
                    khat[rows, :], st1[rows, :], knat[rows, tt, :], ALU.mult))
                p.waits("tensor", [ev_Lcm_ev, ev_Lev, ev_msk])
                for blk in range(4):
                    ev_tb = p.run("tensor", lambda h, blk=blk, rows=rows, pb=pb: h.matmul(
                        pTr[:, blk * 32:(blk + 1) * 32], Ltok[rows, blk * 128:(blk + 1) * 128],
                        IDR32[rows, :], start=True, stop=True, tile_position=(pb, 0)),
                        inc=(blk == 3))
                p.waits("scalar", [ev_tb, ev_fac])
                ev_Lcm_ev = p.run("scalar", lambda h: h.copy(
                    out=Lcm[:, :, :], in_=pTr[:, 0:128].rearrange("p (a b) -> p a b", a=4)))
                p.waits("scalar", [ev_Lcm_ev])
                ev_de = p.run("scalar", lambda h: h.activation(out=Dend[:, :],
                                                               in_=Lcm[:, :, C - 1], func=AF.Exp))
                ev_e1 = p.run("scalar", lambda h: h.activation(
                    out=e2[:, :, :], in_=Lcm[:, :, :], func=AF.Exp))
                p.waits("vector", [ev_e1, ev_U_mm, ev_intra])
                ev_qt = p.run("vector", lambda h, t0=t0: h.tensor_tensor(
                    Qt[:, :, :], e2[:, :, :], kT[:, :, t0:t0 + C], ALU.mult))
                ev_q2 = None
                for sub in range(2):
                    mid = sub * 16 + 7
                    ssl = slice(sub * 16, (sub + 1) * 16)
                    p.waits("vector", [ev_q2] if ev_q2 else [])
                    p.run("vector", lambda h, ssl=ssl, mid=mid: h.tensor_tensor(
                        e2[:, :, ssl], Lcm[:, :, ssl],
                        Lcm[:, :, mid:mid + 1].to_broadcast([128, 4, 16]), ALU.subtract))
                    p.run("vector", lambda h, ssl=ssl: h.tensor_scalar(
                        e2[:, :, ssl], e2[:, :, ssl], CLAMP, -CLAMP, ALU.min, ALU.max))
                    ev_q2 = p.run("vector", lambda h, ssl=ssl: h.tensor_scalar_mul(
                        e3[:, :, ssl], e2[:, :, ssl], -1.0))
                p.waits("scalar", [ev_q2])
                p.run("scalar", lambda h: h.activation(out=e2[:, :, :], in_=e2[:, :, :], func=AF.Exp))
                ev_se = p.run("scalar", lambda h: h.activation(out=e3[:, :, :], in_=e3[:, :, :],
                                                               func=AF.Exp))
                p.waits("vector", [ev_se])
                p.run("vector", lambda h, t0=t0: h.tensor_tensor(
                    Q2[:, :, :], e2[:, :, :], kT[:, :, t0:t0 + C], ALU.mult))
                ev_f1 = p.run("vector", lambda h, t0=t0: h.tensor_tensor(
                    K2[:, :, :], e3[:, :, :], kT[:, :, t0:t0 + C], ALU.mult))
                p.run("vector", lambda h: h.tensor_tensor(
                    e2[:, :, 0:16], Lcm[:, :, 16:32],
                    Lcm[:, :, 15:16].to_broadcast([128, 4, 16]), ALU.subtract))
                ev_f2 = p.run("vector", lambda h: h.tensor_tensor(
                    e2[:, :, 16:32], Lcm[:, :, 15:16].to_broadcast([128, 4, 16]),
                    Lcm[:, :, 0:16], ALU.subtract))
                p.waits("scalar", [ev_f2])
                ev_f3 = p.run("scalar", lambda h: h.activation(out=e2[:, :, :], in_=e2[:, :, :],
                                                               func=AF.Exp))
                p.waits("vector", [ev_f3])
                p.run("vector", lambda h, t0=t0: h.tensor_tensor(
                    Q3[:, :, :], e2[:, :, 0:16], kT[:, :, t0 + 16:t0 + C], ALU.mult))
                ev_fac = p.run("vector", lambda h, t0=t0: h.tensor_tensor(
                    K3[:, :, :], e2[:, :, 16:32], kT[:, :, t0:t0 + 16], ALU.mult))
                p.waits("tensor", [ev_qt, ev_ys_ev, ev_S_upd, ev_fac, ev_pP])
                for hh in range(NH if STOP != 4 else 0):
                    rb = (hh % 2) * 64
                    bank = pW if rb == 0 else pP
                    p.run("tensor", lambda h, hh=hh, rb=rb, rows=rows, pb=pb, bank=bank: h.matmul(
                        bank[rows, hh * 64:(hh + 1) * 64],
                        Qt[rb:rb + 64, hh // 2, :], S[rb:rb + 64, hh // 2, :],
                        start=(hh < 2), stop=(hh >= NH - 2), tile_position=(rb, pb)))
                for hh in range(NH if STOP != 2 else 0):
                    rb = (hh % 2) * 64
                    p.waits("tensor", [ev_Aev, ev_ys_ev])
                    p.run("tensor", lambda h, hh=hh, rb=rb, rows=rows, pb=pb: h.matmul(
                        pA[rows, 0:32], K2[rb:rb + 64, hh // 2, :], Q2[rb:rb + 64, hh // 2, :],
                        start=True, stop=True, tile_position=(rb, pb)))
                    ev_am = p.run("tensor", lambda h, hh=hh, rb=rb, pb=pb: h.matmul(
                        pL[pb:pb + 16, 448:464], K3[rb:rb + 64, hh // 2, :],
                        Q3[rb:rb + 64, hh // 2, :],
                        start=True, stop=True, tile_position=(rb, pb)), inc=1)
                    p.waits("vector", [ev_am])
                    p.run("vector", lambda h, hh=hh, rows=rows: h.scalar_tensor_tensor(
                        Asb[rows, hh, :], pA[rows, 0:32], 1.0, mask2r[rows, :],
                        ALU.bypass, ALU.mult))
                    ev_Aev = p.run("vector", lambda h, hh=hh, pb=pb: h.tensor_copy(
                        Asb[pb:pb + 16, hh, 16:32], pL[pb:pb + 16, 448:464]))
                p.waits("tensor", [ev_Aev])
                for hh in range(NH if STOP not in (2, 5) else 0):
                    ev_intra = p.run("tensor", lambda h, hh=hh, rows=rows, tt=tt, pb=pb: h.matmul(
                        pA[rows, hh * 64:(hh + 1) * 64], Asb[rows, hh, :],
                        bvnat[rows, tt, hh * 64:(hh + 1) * 64],
                        start=(hh == 0), stop=(hh == NH - 1), tile_position=(pb, pb)),
                        inc=(hh == NH - 1))
                p.waits("tensor", [ev_kh, ev_S_upd])
                for hh in range(NH if STOP != 3 else 0):
                    rb = (hh % 2) * 64
                    ev_U_mm = p.run("tensor", lambda h, hh=hh, rb=rb, rows=rows, tt=tt, pb=pb:
                                    h.matmul(
                        pU[rb:rb + 64, (hh // 2) * 64:(hh // 2) * 64 + 64],
                        khat[rows, hh * 64:(hh + 1) * 64],
                        bvnat[rows, tt, hh * 64:(hh + 1) * 64],
                        start=True, stop=True, tile_position=(pb, rb)),
                        inc=(hh == NH - 1))
                p.waits("vector", [ev_U_mm, ev_de])
                for g in range(4):
                    ev_S_upd = p.run("vector", lambda h, g=g: h.scalar_tensor_tensor(
                        S[:, g, :], S[:, g, :], Dend[:, g:g + 1], pU[:, g * 64:(g + 1) * 64],
                        ALU.mult, ALU.add))
                p.waits("vector", [ev_intra, ev_ysT_ev])
                p.run("vector", lambda h, rows=rows, tt=tt: h.tensor_tensor(
                    st1[rows, :], knat[rows, tt, :], vnat[rows, tt, :], ALU.mult))
                p.run("vector", lambda h, rows=rows: h.tensor_tensor(
                    st1[rows, :], st1[rows, :], tfrep[rows, :], ALU.mult))
                st1v = st1[rows, :].rearrange("p (a b) -> p a b", a=4)
                pWv = pW[rows, :].rearrange("p (a b) -> p a b", a=4)
                pPv = pP[rows, :].rearrange("p (a b) -> p a b", a=4)
                p.run("vector", lambda h, st1v=st1v, pWv=pWv: h.tensor_tensor(
                    st1v[:, :, 0:64], pWv[:, :, 0:64], st1v[:, :, 0:64], ALU.add))
                p.run("vector", lambda h, st1v=st1v, pPv=pPv: h.tensor_tensor(
                    st1v[:, :, 64:128], pPv[:, :, 64:128], st1v[:, :, 64:128], ALU.add))
                p.run("vector", lambda h, rows=rows: h.tensor_tensor(
                    st1[rows, :], pA[rows, :], st1[rows, :], ALU.add))
                p.run("vector", lambda h, rows=rows, tt=tt: h.tensor_tensor(
                    st1[rows, :], st1[rows, :], rsig[rows, tt, :], ALU.mult))
                ev_ys_ev = p.run("vector", lambda h, rows=rows, tt=tt: h.tensor_tensor(
                    ysn[rows, tt, :], st1[rows, :], gsig[rows, tt, :], ALU.mult))

            for tt in range(ntt):
                it = tg0 // 128 + tt
                p.waits("tensor", [ev_ys_ev, ev_ysT_ev, ev_ptuse])
                for vb in range(4):
                    ev_tb = p.run("tensor", lambda h, tt=tt, vb=vb: h.transpose(
                        ptr[:, vb * 128:(vb + 1) * 128], ysn[:, tt, vb * 128:(vb + 1) * 128],
                        ident[:]), inc=(vb == 3))
                p.waits("scalar", [ev_tb])
                ev_ysT_ev = p.run("scalar", lambda h, tt=tt, tg0=tg0: h.copy(
                    out=ysT[:, :, tg0 % SEG + tt * 128:tg0 % SEG + (tt + 1) * 128]
                    if False else ysT[:, :, tt * 128:(tt + 1) * 128],
                    in_=ptr[:, 0:512].rearrange("p (a b) -> p a b", a=4)))
            for tt in range(ntt):
                it = tg0 // 128 + tt
                par = tt % 2
                xt = xt2[par]
                cx = name + "cxb%d" % par
                p.waits("sync", [ev_epi, ev_xuse[par]])
                p.dma("sync", xt[:, :], x_in[it * 128:(it + 1) * 128, :], chan=cx)
                p.waits("tensor", [ev_ysT_ev, ev_md, ev_epi])
                for kc in range(4):
                    for nb2 in range(2):
                        ev_md = p.run("tensor", lambda h, kc=kc, nb2=nb2, tt=tt: h.matmul(
                            pP if nb2 == 0 else pP2, ysT[:, kc, tt * 128:(tt + 1) * 128],
                            wo_sb[:, kc, nb2 * 512:(nb2 + 1) * 512],
                            start=(kc == 0), stop=(kc == 3)), inc=(kc == 3 and nb2 == 1))
                p.dwait("scalar", name + "cso")
                p.dwait("scalar", cx)
                p.waits("scalar", [ev_md])
                p.run("scalar", lambda h: h.copy(out=wtmp[:, 0:512], in_=pP))
                ev_ep = p.run("scalar", lambda h: h.copy(out=wtmp[:, 512:1024], in_=pP2))
                p.waits("vector", [ev_ep])
                ev_epi = p.run("vector", lambda h, xt=xt: h.tensor_tensor(
                    wtmp[:, 0:1024], wtmp[:, 0:1024], xt[:], ALU.add))
                ev_xuse[par] = ev_epi
                p.waits("sync", [ev_epi])
                p.dma("sync", x_out[it * 128:(it + 1) * 128, :], wtmp[:, 0:1024], chan=name + "cso")
            ev_pP = ev_ep
            ev_pP2 = ev_epi
        p.dwait("sync", name + "cso")

    return build_l1attnb

build_l1attnb = _mk_build_l1attnb()


from concourse.bass_utils import run_bass_kernel_spmd
from concourse.replica_groups import maybe_share_collective_output_space

_T = 4096
_D = 1024
_NC = 8


def _prog_l0attna():
    nc = bass.Bass("TRN2", target_bir_lowering=False)
    shapes = {"ln_scale": [_D], "ln_bias": [_D], "W_hidden": [_D, 4096], "b_hidden": [4096],
              "W_qk": [_D, 128], "b_qk": [128], "os_weight": [2, 128], "os_bias": [2, 128],
              "W_out": [2048, _D], "b_out": [_D]}
    x_in = nc.dram_tensor("x_in", [_T, _D], F32, kind="ExternalInput").ap()
    waps = {k: nc.dram_tensor(k, s, F32, kind="ExternalInput").ap() for k, s in shapes.items()}
    x_out = nc.dram_tensor("x_out", [_T, _D], F32, kind="ExternalOutput").ap()
    v_scr = nc.dram_tensor("v_scr", [_T, 2048], BF16, kind="Internal").ap()
    p = Prog(nc)
    t = Tiles(nc)
    cc = ConstCache(p, t)
    consts = build_consts(p, t)
    build_l0attna(p, t, cc, consts, x_in, x_out, v_scr, waps)
    p.emit()
    t.close()
    return nc


def _prog_l0ffn():
    nc = bass.Bass("TRN2", target_bir_lowering=False)
    x_in = nc.dram_tensor("x_in", [_T, _D], F32, kind="ExternalInput").ap()
    wq = nc.dram_tensor("wq", [_D, _D], I8, kind="ExternalInput").ap()
    wscale = nc.dram_tensor("wscale", [_D], F32, kind="ExternalInput").ap()
    wbias = nc.dram_tensor("wbias", [_D], F32, kind="ExternalInput").ap()
    x_out = nc.dram_tensor("x_out", [_T, _D], F32, kind="ExternalOutput").ap()
    p = Prog(nc)
    t = Tiles(nc)
    cc = ConstCache(p, t)
    build_l0ffn(p, t, cc, x_in, x_out, wq, wscale, wbias)
    p.emit()
    t.close()
    return nc


def _prog_l0attnb():
    nc = bass.Bass("TRN2", target_bir_lowering=False)
    shapes = {"wq": [16, _D, 64], "wk": [16, _D, 64], "wv": [16, _D, 64],
              "bq": [16, 1, 64], "bk": [16, 1, 64], "bv": [16, 1, 64],
              "wo": [_D, _D], "bo": [_D]}
    x_in = nc.dram_tensor("x_in", [_T, _D], F32, kind="ExternalInput").ap()
    waps = {k: nc.dram_tensor(k, s, F32, kind="ExternalInput").ap() for k, s in shapes.items()}
    x_out = nc.dram_tensor("x_out", [_T, _D], F32, kind="ExternalOutput").ap()
    v_scr = nc.dram_tensor("v_scr", [16, _T, 65], BF16, kind="Internal").ap()
    p = Prog(nc)
    t = Tiles(nc)
    cc = ConstCache(p, t)
    consts = build_consts(p, t)
    build_l0attnb(p, t, cc, consts, x_in, x_out, v_scr, waps)
    p.emit()
    t.close()
    return nc


def _prog_l1attna():
    nc = bass.Bass("TRN2", target_bir_lowering=False)
    shapes = {"W_in": [_D, _D], "W_out": [_D, _D], "Wp": [3 * _D, _D], "bp": [3 * _D],
              "alpha_log": [_D], "delta_bias": [_D], "log_gamma": [1]}
    x_in = nc.dram_tensor("x_in", [_T, _D], F32, kind="ExternalInput").ap()
    waps = {k: nc.dram_tensor(k, s, F32, kind="ExternalInput").ap() for k, s in shapes.items()}
    x_out = nc.dram_tensor("x_out", [_T, _D], F32, kind="ExternalOutput").ap()
    p = Prog(nc)
    t = Tiles(nc)
    cc = ConstCache(p, t)
    consts = build_consts(p, t)
    build_l1attna(p, t, cc, consts, x_in, x_out, waps)
    p.emit()
    t.close()
    return nc


def _prog_l1ffn():
    nc = bass.Bass("TRN2", target_bir_lowering=False, num_devices=_NC)
    shapes = {"Wq": [4, 256, _D], "bq": [4, 256], "gamma": [4, 256], "beta": [4, 256],
              "sk1": [128, 128], "sk2": [128, 128], "latents": [16384, 256],
              "W1": [256, 512], "W2": [512, 2048]}
    x_in = nc.dram_tensor("x_in", [_T, _D], F32, kind="ExternalInput").ap()
    waps = {k: nc.dram_tensor(k, s, F32, kind="ExternalInput").ap() for k, s in shapes.items()}
    x_out = nc.dram_tensor("x_out", [_T, _D], F32, kind="ExternalOutput").ap()
    h_scr = nc.dram_tensor("h_scr", [16384, 512], BF16, kind="Internal").ap()
    q_scr = nc.dram_tensor("q_scr", [8, 128, _T], F32, kind="Internal").ap()
    xl_scr = nc.dram_tensor("xl_scr", [8, 128, _T], BF16, kind="Internal").ap()
    cc_in = nc.dram_tensor("cc_in", [128, 16], F32, kind="Internal").ap()
    aspace = maybe_share_collective_output_space("AllReduce", [list(range(_NC))])
    cc_out = nc.dram_tensor("cc_out", [128, 16], F32, kind="Internal",
                            addr_space=aspace).ap()
    p = Prog(nc)
    t = Tiles(nc)
    cc = ConstCache(p, t)
    consts = build_consts(p, t)
    build_l1ffn(p, t, cc, consts, x_in, x_out, h_scr, q_scr, xl_scr, cc_in, cc_out, waps)
    p.emit()
    t.close()
    return nc


def _prog_l1attnb():
    nc = bass.Bass("TRN2", target_bir_lowering=False)
    shapes = {"W_sel": [_D, 160], "dt_W": [32, 512], "dt_b": [512],
              "Wk": [_D, 512], "Wv": [_D, 512], "Wr": [_D, 512], "Wg": [_D, 512],
              "Wo": [512, _D], "base_decay": [8, 64], "time_first": [8, 64]}
    x_in = nc.dram_tensor("x_in", [_T, _D], F32, kind="ExternalInput").ap()
    waps = {k: nc.dram_tensor(k, s, F32, kind="ExternalInput").ap() for k, s in shapes.items()}
    x_out = nc.dram_tensor("x_out", [_T, _D], F32, kind="ExternalOutput").ap()
    p = Prog(nc)
    t = Tiles(nc)
    cc = ConstCache(p, t)
    consts = build_consts(p, t)
    build_l1attnb(p, t, cc, consts, x_in, x_out, waps)
    p.emit()
    t.close()
    return nc


_A0A_SHAPES = {"ln_scale": [_D], "ln_bias": [_D], "W_hidden": [_D, 4096], "b_hidden": [4096],
               "W_qk": [_D, 128], "b_qk": [128], "os_weight": [2, 128], "os_bias": [2, 128],
               "W_out": [2048, _D], "b_out": [_D]}
_A0B_SHAPES = {"wq": [16, _D, 64], "wk": [16, _D, 64], "wv": [16, _D, 64],
               "bq": [16, 1, 64], "bk": [16, 1, 64], "bv": [16, 1, 64],
               "wo": [_D, _D], "bo": [_D]}
_A1A_SHAPES = {"W_in": [_D, _D], "W_out": [_D, _D], "Wp": [3 * _D, _D], "bp": [3 * _D],
               "alpha_log": [_D], "delta_bias": [_D], "log_gamma": [1]}
_F1_SHAPES = {"Wq": [4, 256, _D], "bq": [4, 256], "gamma": [4, 256], "beta": [4, 256],
              "sk1": [128, 128], "sk2": [128, 128], "latents": [16384, 256],
              "W1": [256, 512], "W2": [512, 2048]}
_A1B_SHAPES = {"W_sel": [_D, 160], "dt_W": [32, 512], "dt_b": [512],
               "Wk": [_D, 512], "Wv": [_D, 512], "Wr": [_D, 512], "Wg": [_D, 512],
               "Wo": [512, _D], "base_decay": [8, 64], "time_first": [8, 64]}


def _build_fused():
    nc = bass.Bass("TRN2", target_bir_lowering=False, num_devices=_NC)

    def dr(n, s, dt=F32):
        return nc.dram_tensor(n, s, dt, kind="ExternalInput").ap()

    x_in = dr("x_in", [_T, _D])
    x_out = nc.dram_tensor("x_out", [_T, _D], F32, kind="ExternalOutput").ap()
    chain = [x_in] + [nc.dram_tensor("xc%d" % i, [_T, _D], F32, kind="Internal").ap()
                      for i in range(7)] + [x_out]
    wa0a = {k: dr("a0a_" + k, s) for k, s in _A0A_SHAPES.items()}
    f0_wq = dr("f0_wq", [_D, _D], I8)
    f0_ws = dr("f0_wscale", [_D])
    f0_wb = dr("f0_wbias", [_D])
    wa0b = {k: dr("a0b_" + k, s) for k, s in _A0B_SHAPES.items()}
    wa1a = {k: dr("a1a_" + k, s) for k, s in _A1A_SHAPES.items()}
    wf1 = {k: dr("f1_" + k, s) for k, s in _F1_SHAPES.items()}
    wa1b = {k: dr("a1b_" + k, s) for k, s in _A1B_SHAPES.items()}

    v_scr_a = nc.dram_tensor("v_scr_a", [_T, 2048], BF16, kind="Internal").ap()
    v_scr_b = nc.dram_tensor("v_scr_b", [16, _T, 65], BF16, kind="Internal").ap()
    h_scr = nc.dram_tensor("h_scr", [16384, 512], BF16, kind="Internal").ap()
    q_scr = nc.dram_tensor("q_scr", [8, 128, _T], F32, kind="Internal").ap()
    xl_scr = nc.dram_tensor("xl_scr", [8, 128, _T], BF16, kind="Internal").ap()
    cc_in = nc.dram_tensor("cc_in", [128, 16], F32, kind="Internal").ap()
    aspace = maybe_share_collective_output_space("AllReduce", [list(range(_NC))])
    cc_out = nc.dram_tensor("cc_out", [128, 16], F32, kind="Internal",
                            addr_space=aspace).ap()
    cc_in2 = nc.dram_tensor("cc_in2", [128, 16], F32, kind="Internal").ap()
    cc_out2 = nc.dram_tensor("cc_out2", [128, 16], F32, kind="Internal",
                             addr_space=aspace).ap()

    p = Prog(nc)
    t = Tiles(nc)
    cc = ConstCache(p, t)
    consts = build_consts(p, t)
    for v in (-15.0, 1.0, 1e-10, 1e-12, 1e-5):
        cc.get(v)
    cc.freeze()

    phases = [
        lambda t2, xi, xo: build_l0attna(p, t2, cc, consts, xi, xo, v_scr_a, wa0a,
                                         name="a0a"),
        lambda t2, xi, xo: build_l0ffn(p, t2, cc, xi, xo, f0_wq, f0_ws, f0_wb,
                                       name="f0a", consts=consts),
        lambda t2, xi, xo: build_l0attnb(p, t2, cc, consts, xi, xo, v_scr_b, wa0b,
                                         name="a0b"),
        lambda t2, xi, xo: build_l0ffn(p, t2, cc, xi, xo, f0_wq, f0_ws, f0_wb,
                                       name="f0b", consts=consts),
        lambda t2, xi, xo: build_l1attna(p, t2, cc, consts, xi, xo, wa1a, name="a1a"),
        lambda t2, xi, xo: build_l1ffn(p, t2, cc, consts, xi, xo, h_scr, q_scr, xl_scr,
                                       cc_in, cc_out, wf1, name="f1a"),
        lambda t2, xi, xo: build_l1attnb(p, t2, cc, consts, xi, xo, wa1b, name="a1b"),
        lambda t2, xi, xo: build_l1ffn(p, t2, cc, consts, xi, xo, h_scr, q_scr, xl_scr,
                                       cc_in2, cc_out2, wf1, name="f1b",
                                       skip_hall=True),
    ]
    for i, ph in enumerate(phases):
        t2 = PhaseTiles(t)
        ph(t2, chain[i], chain[i + 1])
        barrier(p)
        t2.close()
    p.emit()
    t.close()
    return nc


_FUSED = None


def _get_fused():
    global _FUSED
    if _FUSED is None:
        _FUSED = _build_fused()
    return _FUSED


_PROGS = None


def _get_progs():
    global _PROGS
    if _PROGS is None:
        _PROGS = {
            "a0a": _prog_l0attna(),
            "f0": _prog_l0ffn(),
            "a0b": _prog_l0attnb(),
            "a1a": _prog_l1attna(),
            "f1": _prog_l1ffn(),
            "a1b": _prog_l1attnb(),
        }
    return _PROGS


def _f32(a):
    return np.ascontiguousarray(np.asarray(a, dtype=np.float32))


def kernel(**inputs):
    progs = _get_fused()
    x = _f32(inputs["L0_attn_a_x"])
    xs = [np.ascontiguousarray(x[c]) for c in range(_NC)]

    w_a0a = {
        "ln_scale": _f32(inputs["L0_attn_a_ln_scale"]),
        "ln_bias": _f32(inputs["L0_attn_a_ln_bias"]),
        "W_hidden": _f32(inputs["L0_attn_a_W_hidden"]),
        "b_hidden": _f32(inputs["L0_attn_a_b_hidden"]),
        "W_qk": _f32(inputs["L0_attn_a_W_qk"]),
        "b_qk": _f32(inputs["L0_attn_a_b_qk"]),
        "os_weight": _f32(inputs["L0_attn_a_os_weight"]),
        "os_bias": _f32(inputs["L0_attn_a_os_bias"]),
        "W_out": _f32(inputs["L0_attn_a_W_out"]),
        "b_out": _f32(inputs["L0_attn_a_b_out"]),
    }
    w_f0 = {
        "wq": np.ascontiguousarray(np.asarray(inputs["L0_ffn_weight_q"], dtype=np.int8)),
        "wscale": _f32(inputs["L0_ffn_weight_scale"]),
        "wbias": _f32(inputs["L0_ffn_bias"]),
    }
    w_a0b = {
        "wq": _f32(inputs["L0_attn_b_wq"]),
        "bq": _f32(inputs["L0_attn_b_bq"]),
        "wk": _f32(inputs["L0_attn_b_wk"]),
        "bk": _f32(inputs["L0_attn_b_bk"]),
        "wv": _f32(inputs["L0_attn_b_wv"]),
        "bv": _f32(inputs["L0_attn_b_bv"]),
        "wo": _f32(inputs["L0_attn_b_wo"]),
        "bo": _f32(inputs["L0_attn_b_bo"]),
    }
    w_a1a = {
        "W_in": _f32(inputs["L1_attn_a_W_in"]),
        "W_out": _f32(inputs["L1_attn_a_W_out"]),
        "Wp": _f32(inputs["L1_attn_a_Wp"]),
        "bp": _f32(inputs["L1_attn_a_bp"]),
        "alpha_log": _f32(inputs["L1_attn_a_alpha_log"]),
        "delta_bias": _f32(inputs["L1_attn_a_delta_bias"]),
        "log_gamma": _f32(inputs["L1_attn_a_log_gamma"]).reshape(1),
    }
    w_f1 = {
        "Wq": _f32(inputs["L1_ffn_Wq"]),
        "bq": _f32(inputs["L1_ffn_bq"]),
        "gamma": _f32(inputs["L1_ffn_gamma"]),
        "beta": _f32(inputs["L1_ffn_beta"]),
        "sk1": _f32(inputs["L1_ffn_sk1"]),
        "sk2": _f32(inputs["L1_ffn_sk2"]),
        "latents": _f32(inputs["L1_ffn_latents"]),
        "W1": _f32(inputs["L1_ffn_W1"]),
        "W2": _f32(inputs["L1_ffn_W2"]),
    }
    w_a1b = {
        "W_sel": _f32(inputs["L1_attn_b_W_sel"]),
        "dt_W": _f32(inputs["L1_attn_b_dt_W"]),
        "dt_b": _f32(inputs["L1_attn_b_dt_b"]),
        "Wk": _f32(inputs["L1_attn_b_Wk"]),
        "Wv": _f32(inputs["L1_attn_b_Wv"]),
        "Wr": _f32(inputs["L1_attn_b_Wr"]),
        "Wg": _f32(inputs["L1_attn_b_Wg"]),
        "Wo": _f32(inputs["L1_attn_b_Wo"]),
        "base_decay": _f32(inputs["L1_attn_b_base_decay"]),
        "time_first": _f32(inputs["L1_attn_b_time_first"]),
    }

    wall = {}
    for pre, wmap in (("a0a_", w_a0a), ("f0_", {"wq": w_f0["wq"], "wscale": w_f0["wscale"],
                                                "wbias": w_f0["wbias"]}),
                      ("a0b_", w_a0b), ("a1a_", w_a1a), ("f1_", w_f1), ("a1b_", w_a1b)):
        for k, v in wmap.items():
            wall[pre + k] = v
    in_maps = [dict(wall, x_in=xs[c]) for c in range(_NC)]
    res = run_bass_kernel_spmd(progs, in_maps, core_ids=list(range(_NC)))
    return np.stack([np.asarray(res.results[c]["x_out"], np.float32)
                     for c in range(_NC)], 0)


# version 07 (final)
# speedup vs baseline: 1.0307x; 1.0307x over previous
import numpy as np
import concourse.bass as bass
import concourse.mybir as mybir

F32 = mybir.dt.float32
BF16 = mybir.dt.bfloat16
I8 = mybir.dt.int8
I32 = mybir.dt.int32
U32 = mybir.dt.uint32
AF = mybir.ActivationFunctionType
ALU = mybir.AluOpType

ENGINES = ("sync", "tensor", "vector", "scalar", "gpsimd")
CHAN_DEBUG = False


class Prog:
    def __init__(self, nc, channels=()):
        self.nc = nc
        self.scope = None
        self.q = {e: [] for e in ENGINES}
        self._waited = {e: {} for e in ENGINES}
        self.cnt = {e: 0 for e in ENGINES}
        self.channels = list(channels)
        for c in self.channels:
            self.cnt[c] = 0
        self.sems = None

    def add_channel(self, c):
        if c not in self.cnt:
            self.channels.append(c)
            self.cnt[c] = 0

    def wait(self, e, ev):
        if ev is None:
            return
        pe, c = ev
        w = self._waited[e]
        if w.get(pe, -1) >= c:
            return
        w[pe] = c
        def cl(h, pe=pe, c=c):
            h.wait_ge(self.sems[pe], c)
        self.q[e].append((self.scope, cl))

    def waits(self, e, evs):
        best = {}
        for ev in evs:
            if ev is None:
                continue
            pe, c = ev
            if pe not in best or c > best[pe]:
                best[pe] = c
        for pe, c in best.items():
            self.wait(e, (pe, c))

    def run(self, e, fn, inc=None, selfwait=None):
        auto = e in ("vector", "scalar", "gpsimd")
        if inc is None:
            inc = 1 if auto else 0
        if selfwait is None:
            selfwait = auto
        if selfwait and self.cnt[e] > 0:
            self.wait(e, (e, self.cnt[e]))
        ev = None
        if inc:
            self.cnt[e] += inc
            ev = (e, self.cnt[e])
        def cl(h, fn=fn, inc=inc, e=e):
            ins = fn(h)
            if inc:
                ins.then_inc(self.sems[e], inc)
        self.q[e].append((self.scope, cl))
        return ev

    def dma(self, e, out, in_, chan=None, **kw):
        if chan is None:
            chan = "dma_" + e
        self.add_channel(chan)
        self.cnt[chan] += 16
        c = self.cnt[chan]
        def cl(h, out=out, in_=in_, kw=kw, chan=chan):
            h.dma_start(out=out, in_=in_, **kw).then_inc(self.sems[chan], 16)
        self.q[e].append((self.scope, cl))
        return (chan, c)

    def cinc(self, e, fn, chan, inc=16):
        self.add_channel(chan)
        self.cnt[chan] += inc
        c = self.cnt[chan]
        def cl(h, fn=fn, chan=chan, inc=inc):
            fn(h).then_inc(self.sems[chan], inc)
        self.q[e].append((self.scope, cl))
        return (chan, c)

    def dwait(self, e, chan):
        if chan in self.cnt and self.cnt[chan] > 0:
            self.wait(e, (chan, self.cnt[chan]))

    def ev(self, e):
        return (e, self.cnt[e])

    def fence(self, e):
        self.cnt[e] += 1
        c = self.cnt[e]
        def cl(h, c=c):
            h.nop().then_inc(self.sems[e], 1)
        self.q[e].append((self.scope, cl))
        return (e, c)

    def emit(self, extra_ctx=()):
        if CHAN_DEBUG:
            print("PROG sems: %d engines + %d channels = %d total; instrs: %s"
                  % (len(ENGINES), len(self.channels), len(ENGINES) + len(self.channels),
                     {e: len(self.q[e]) for e in ENGINES}))
        from contextlib import ExitStack
        nc = self.nc
        es = ExitStack()
        for c in extra_ctx:
            es.enter_context(c)
        self.sems = {e: es.enter_context(nc.semaphore(f"qsem_{e}"))
                     for e in list(ENGINES) + self.channels}
        block = es.enter_context(nc.Block())
        for e in ENGINES:
            closures = self.q[e]
            if not closures:
                continue
            def body(h, closures=closures):
                for lab, cl in closures:
                    if lab is None:
                        cl(h)
                    else:
                        with nc.named_scope(lab):
                            cl(h)
            getattr(block, e)(body)
        es.close()


class Tiles:
    def __init__(self, nc):
        from contextlib import ExitStack
        self.nc = nc
        self.es = ExitStack()
        self.handles = {}

    def sb(self, name, shape, dtype):
        h = self.es.enter_context(self.nc.sbuf_tensor(name, list(shape), dtype))
        self.handles[name] = h
        return h

    def ps(self, name, shape, dtype=F32):
        h = self.es.enter_context(self.nc.psum_tensor(name, list(shape), dtype))
        self.handles[name] = h
        return h

    def close(self):
        self.es.close()



class PhaseTiles(Tiles):
    def __init__(self, base):
        from contextlib import ExitStack
        self.nc = base.nc
        self.es = ExitStack()
        self.handles = base.handles


def barrier(p):
    evs = [(e, p.cnt[e]) for e in ENGINES if p.cnt[e] > 0]
    evs += [(ch, p.cnt[ch]) for ch in p.channels if p.cnt[ch] > 0]
    for e in ENGINES:
        p.waits(e, evs)


class ConstCache:
    def __init__(self, p, t, prefix="cc"):
        self.p, self.t, self.prefix = p, t, prefix
        self.vals = {}
        self.frozen = False

    def freeze(self):
        self.frozen = True

    def get(self, v):
        v = float(v)
        if v not in self.vals:
            if self.frozen:
                raise RuntimeError("ConstCache miss after freeze: add %r to prewarm" % v)
            tile = self.t.sb(f"{self.prefix}_{len(self.vals)}", [128, 1], F32)
            ev = self.p.run("gpsimd", lambda h, tile=tile, v=v: h.memset(tile[:], v), inc=1)
            self.vals[v] = (tile, ev)
        return self.vals[v]

    def ap(self, v):
        return self.get(v)[0][:, :]

    def ev(self, v):
        return self.get(v)[1]


def build_consts(p, t):
    ident = t.sb("c_ident", [128, 128], BF16)
    ones1 = t.sb("c_ones1", [1, 128], F32)
    iota_r = t.sb("c_iota_r", [128, 128], F32)
    iota_c = t.sb("c_iota_c", [128, 128], F32)
    identf = t.sb("c_identf", [128, 128], F32)
    ev0 = p.run("gpsimd", lambda h: h.iota(iota_r[:], pattern=[[0, 128]], base=0,
                                           channel_multiplier=1,
                                           allow_small_or_imprecise_dtypes=True))
    ev0 = p.run("gpsimd", lambda h: h.iota(iota_c[:], pattern=[[1, 128]], base=0,
                                           channel_multiplier=0,
                                           allow_small_or_imprecise_dtypes=True), inc=1)
    ev1 = p.run("gpsimd", lambda h: h.memset(ones1[:], 1.0), inc=1)
    p.wait("vector", ev1)
    p.run("vector", lambda h: h.tensor_tensor(identf[:], iota_r[:], iota_c[:], ALU.is_equal))
    evi = p.run("vector", lambda h: h.tensor_copy(ident[:], identf[:]), inc=1)
    return ident, ones1, evi


def transpose_128(p, psum_ap, sb_in_ap, ident, deps=(), inc=1):
    p.waits("tensor", deps)
    return p.run("tensor", lambda h: h.transpose(psum_ap, sb_in_ap, ident[:]), inc=inc)


def bcast_row(p, psum_ap, row_ap, ones1, n, deps=(), inc=1):
    p.waits("tensor", deps)
    return p.run("tensor", lambda h: h.matmul(psum_ap, ones1[:, :], row_ap,
                                              start=True, stop=True), inc=inc)


def _mk_build_l0attna():

    T = 4096
    D = 1024
    HID = 2048
    IB = 512
    NT = T // 128


    def build_l0attna(p, t, cc, consts, x_in, x_out, v_scr, w, name="a0a"):
        nc = p.nc
        ident, ones1, ev_ident = consts
        nt = T // 128
        nib = T // IB
        ntib = IB // 128

        whid = t.sb(name + "_whid", [128, 8, HID], BF16)
        wout = t.sb(name + "_wout", [128, 16, D], BF16)
        wqk = t.sb(name + "_wqk", [128, 8, 128], BF16)
        QT = t.sb(name + "_QT", [128, T], BF16)
        KT = t.sb(name + "_KT", [128, T], BF16)
        attnT = t.sb(name + "_attnT", [128, nt, IB], BF16)
        gateT = t.sb(name + "_gateT", [128, 16, IB], BF16)
        bhv_rep = t.sb(name + "_bhvrep", [128, HID], F32)
        bh2col = t.sb(name + "_bh2col", [128, 16], F32)
        bout_rep = t.sb(name + "_boutrep", [128, D], F32)
        lns = t.sb(name + "_lns", [128, 8], F32)
        lnb = t.sb(name + "_lnb", [128, 8], F32)
        bqk = t.sb(name + "_bqk", [128, 1], F32)
        osw = t.sb(name + "_osw", [128, 2], F32)
        osb = t.sb(name + "_osb", [128, 2], F32)
        rstd_all = t.sb(name + "_rstdall", [128, nt], F32)
        nbias_all = t.sb(name + "_nbiasall", [128, nt], F32)

        xt2 = [t.sb(name + "_x0", [128, D], F32), t.sb(name + "_x1", [128, D], F32)]
        xsq = t.sb(name + "_xsq", [128, D], F32)
        xnbf = t.sb(name + "_xnbf", [128, D], BF16)
        xnT = t.sb(name + "_xnT", [128, 8, 128], BF16)
        xnTi = t.sb(name + "_xnTi", [128, 8, IB], BF16)
        qks = t.sb(name + "_qks", [128, 128], F32)
        vtmp = t.sb(name + "_vtmp", [128, HID], F32)
        vbf = t.sb(name + "_vbf", [128, HID], BF16)
        vt = [t.sb(name + "_vt0", [128, 1024], BF16), t.sb(name + "_vt1", [128, 1024], BF16)]
        stmp2 = [t.sb(name + "_stmp0", [128, IB], F32), t.sb(name + "_stmp1", [128, IB], F32)]
        xib = [t.sb(name + "_xib%d" % k, [128, D], F32) for k in range(ntib)]
        s1 = t.sb(name + "_s1", [128, 1], F32)
        s2 = t.sb(name + "_s2", [128, 1], F32)
        mean = t.sb(name + "_mean", [128, 1], F32)
        m2 = t.sb(name + "_m2", [128, 1], F32)
        var = t.sb(name + "_var", [128, 1], F32)

        pool = t.ps(name + "_pool", [128, 7, 512], F32)
        ptr = t.ps(name + "_ptr", [128, 1024], BF16)
        pqk = pool[:, 6, 0:128]
        pv = pool[:, 4:6, :].rearrange("p a b -> p (a b)")
        pout = pool[:, 0:4, :].rearrange("p a b -> p (a b)").rearrange(
            "p (c i) -> p c i", c=8)

        cwb = name + "cwb"
        def load_col(dst, src1d, ncol):
            for c in range(ncol):
                p.dma("sync", dst[:, c:c + 1],
                      src1d[c * 128:(c + 1) * 128].rearrange("(p o) -> p o", o=1), chan=cwb)
        load_col(lns, w["ln_scale"], 8)
        load_col(lnb, w["ln_bias"], 8)
        p.dma("sync", bqk[:, :], w["b_qk"].rearrange("(p o) -> p o", o=1), chan=cwb)
        for s in range(2):
            p.dma("sync", osw[:, s:s + 1], w["os_weight"][s, :].rearrange("(p o) -> p o", o=1), chan=cwb)
            p.dma("sync", osb[:, s:s + 1], w["os_bias"][s, :].rearrange("(p o) -> p o", o=1), chan=cwb)
        load_col(bh2col, w["b_hidden"][HID:], 16)
        brow = vtmp[0:1, 0:HID]
        p.dma("sync", brow[:, :], w["b_hidden"][0:HID].rearrange("(o d) -> o d", o=1), chan=cwb)
        borow = xsq[0:1, 0:D]
        p.dma("sync", borow[:, :], w["b_out"].rearrange("(o d) -> o d", o=1), chan=cwb)
        p.dwait("tensor", cwb)
        p.waits("tensor", [ev_ident])
        for nb in range(4):
            evb = p.run("tensor", lambda h, nb=nb: h.matmul(
                pv[:, (nb % 2) * 512:(nb % 2) * 512 + 512], ones1[:, :],
                brow[:, nb * 512:(nb + 1) * 512], start=True, stop=True), inc=1)
            p.wait("vector", evb)
            ev_bh = p.run("vector", lambda h, nb=nb: h.tensor_copy(
                bhv_rep[:, nb * 512:(nb + 1) * 512],
                pv[:, (nb % 2) * 512:(nb % 2) * 512 + 512]))
            p.wait("tensor", ev_bh)
        for nb in range(2):
            evb = p.run("tensor", lambda h, nb=nb: h.matmul(
                pv[:, nb * 512:nb * 512 + 512], ones1[:, :],
                borow[:, nb * 512:(nb + 1) * 512], start=True, stop=True), inc=1)
            p.wait("vector", evb)
            ev_bo = p.run("vector", lambda h, nb=nb: h.tensor_copy(
                bout_rep[:, nb * 512:(nb + 1) * 512], pv[:, nb * 512:nb * 512 + 512]))
            p.wait("tensor", ev_bo)

        cwh = name + "cwh"
        p.waits("sync", [evb])
        p.dma("sync", vtmp[:, :1024].rearrange("p (c q) -> p c q", c=8), w["W_qk"].rearrange("(c p) q -> p c q", p=128), chan=cwh)
        p.dwait("vector", cwh)
        evc = p.run("vector", lambda h: h.tensor_copy(
            wqk[:, :, :], vtmp[:, :1024].rearrange("p (c q) -> p c q", c=8)))
        for c in range(8):
            p.waits("sync", [evc])
            p.dma("sync", vtmp[:, :], w["W_hidden"][c * 128:(c + 1) * 128, 0:HID], chan=cwh)
            p.dwait("vector", cwh)
            evc = p.run("vector", lambda h, c=c: h.tensor_copy(whid[:, c, :], vtmp[:, :]))
        for c in range(16):
            p.waits("sync", [evc])
            p.dma("sync", vtmp[:, :1024], w["W_out"][c * 128:(c + 1) * 128, :], chan=cwh)
            p.dwait("vector", cwh)
            evc = p.run("vector", lambda h, c=c: h.tensor_copy(wout[:, c, :], vtmp[:, :1024]))
        ev_wpre = evc

        ev_xuse = [ev_wpre, ev_wpre]
        ev_ptuse = ev_wpre
        ev_tr_last = None
        ev_pqkeva = None
        ev_pveva = None
        ev_vbf_rd = None
        ev_qt = None
        for i in range(nt):
            par = i % 2
            xt = xt2[par]
            cx = name + "cxa%d" % par
            p.waits("sync", [ev_xuse[par]])
            p.dma("sync", xt[:, :], x_in[i * 128:(i + 1) * 128, :], chan=cx)
            p.dwait("scalar", cx)
            ev_sq = p.run("scalar", lambda h, xt=xt: h.square(xsq[:], xt[:]))
            p.dwait("vector", cx)
            p.run("vector", lambda h, xt=xt: h.reduce_sum(s1[:], xt[:], axis=mybir.AxisListType.X))
            p.waits("vector", [ev_sq])
            p.run("vector", lambda h: h.reduce_sum(s2[:], xsq[:], axis=mybir.AxisListType.X))
            p.run("vector", lambda h: h.tensor_scalar_mul(mean[:], s1[:], 1.0 / D))
            p.run("vector", lambda h: h.tensor_scalar_mul(s2[:], s2[:], 1.0 / D))
            p.run("vector", lambda h: h.tensor_tensor(m2[:], mean[:], mean[:], ALU.mult))
            ev_v = p.run("vector", lambda h: h.tensor_tensor(var[:], s2[:], m2[:], ALU.subtract))
            p.waits("scalar", [ev_v, cc.ev(1e-10)])
            p.run("scalar", lambda h, i=i: h.activation(out=rstd_all[:, i:i + 1], in_=var[:],
                                                        func=AF.Ln, scale=1.0 + 1e-5,
                                                        bias=cc.ap(1e-10)))
            ev_s = p.run("scalar", lambda h, i=i: h.activation(
                out=rstd_all[:, i:i + 1], in_=rstd_all[:, i:i + 1], func=AF.Exp, scale=-0.5))
            p.waits("vector", [ev_s])
            ev_nb = p.run("vector", lambda h, i=i: h.scalar_tensor_tensor(
                nbias_all[:, i:i + 1], mean[:], -1.0, rstd_all[:, i:i + 1], ALU.mult, ALU.mult))
            p.waits("scalar", [ev_nb, ev_tr_last])
            ev_xn = p.run("scalar", lambda h, xt=xt, i=i: h.activation(
                out=xnbf[:], in_=xt[:], func=AF.Identity,
                scale=rstd_all[:, i:i + 1], bias=nbias_all[:, i:i + 1]))
            ev_xuse[par] = ev_xn
            p.waits("tensor", [ev_xn, ev_ptuse])
            for dc in range(8):
                ev_tr = p.run("tensor", lambda h, dc=dc: h.transpose(
                    ptr[:, dc * 128:(dc + 1) * 128], xnbf[:, dc * 128:(dc + 1) * 128], ident[:]),
                    inc=(dc == 7))
            ev_tr_last = ev_tr
            p.waits("scalar", [ev_tr])
            for dc in range(8):
                ev_ptuse = p.run("scalar", lambda h, dc=dc: h.activation(
                    out=xnT[:, dc, :], in_=ptr[:, dc * 128:(dc + 1) * 128], func=AF.Identity,
                    scale=lns[:, dc:dc + 1], bias=lnb[:, dc:dc + 1]))
            p.waits("tensor", [ev_ptuse, ev_pqkeva])
            for dc in range(8):
                ev_mmqk = p.run("tensor", lambda h, dc=dc: h.matmul(
                    pqk, wqk[:, dc, :], xnT[:, dc, :],
                    start=(dc == 0), stop=(dc == 7)), inc=(dc == 7))
            p.waits("scalar", [ev_mmqk])
            ev_qks = p.run("scalar", lambda h: h.activation(out=qks[:], in_=pqk,
                                                            func=AF.Silu, bias=bqk[:, :]))
            ev_pqkeva = ev_qks
            p.waits("vector", [ev_qks])
            p.run("vector", lambda h, i=i: h.tensor_scalar(
                QT[:, i * 128:(i + 1) * 128], qks[:], osw[:, 0:1], osb[:, 0:1], ALU.mult, ALU.add))
            ev_qt = p.run("vector", lambda h, i=i: h.tensor_scalar(
                KT[:, i * 128:(i + 1) * 128], qks[:], osw[:, 1:2], osb[:, 1:2], ALU.mult, ALU.add))
            for nb in range(4):
                pvslice = pv[:, (nb % 2) * 512:(nb % 2) * 512 + 512]
                p.waits("tensor", [ev_pveva, ev_ptuse])
                for dc in range(8):
                    ev_mmv = p.run("tensor", lambda h, dc=dc, pvslice=pvslice, nb=nb: h.matmul(
                        pvslice, xnT[:, dc, :], whid[:, dc, nb * 512:(nb + 1) * 512],
                        start=(dc == 0), stop=(dc == 7)), inc=(dc == 7))
                p.waits("vector", [ev_mmv, ev_vbf_rd])
                ev_pveva = p.run("vector", lambda h, nb=nb, pvslice=pvslice: h.tensor_tensor(
                    vtmp[:, nb * 512:(nb + 1) * 512], pvslice,
                    bhv_rep[:, nb * 512:(nb + 1) * 512], ALU.add))
            p.waits("scalar", [ev_pveva])
            p.dwait("scalar", name + "cvs")
            ev_vbf = p.run("scalar", lambda h: h.activation(out=vbf[:], in_=vtmp[:], func=AF.Silu))
            ev_vbf_rd = ev_vbf
            p.waits("sync", [ev_vbf])
            p.dma("sync", v_scr[i * 128:(i + 1) * 128, :], vbf[:, :], chan=name + "cvs")

        p.waits("vector", [ev_qt])
        evc2 = None
        for c in range(8):
            p.waits("sync", [evc2, ev_vbf])
            p.dma("sync", vtmp[:, :], w["W_hidden"][c * 128:(c + 1) * 128, HID:], chan=cwh)
            p.dwait("vector", cwh)
            p.waits("vector", [ev_mmv])
            evc2 = p.run("vector", lambda h, c=c: h.tensor_copy(whid[:, c, :], vtmp[:, :]))
        ev_whid_g = evc2

        ev_epi = None
        ev_xnTi_use = None
        ev_gate_all = None
        ev_acc_all = None
        ev_at = None
        ev_ogt = None
        ev_vt_rd = [None, None]
        ev_rel_par = [None, None]
        ev_sq_par = [None, None]
        ev_md = None
        for ib in range(nib):
            i0 = ib * IB
            for k in range(ntib):
                it = ib * ntib + k
                cx = name + "cxb%d" % k
                p.waits("sync", [ev_epi])
                p.dma("sync", xib[k][:, :], x_in[it * 128:(it + 1) * 128, :], chan=cx)
                p.dwait("scalar", cx)
                p.waits("scalar", [ev_tr_last])
                ev_xn = p.run("scalar", lambda h, k=k, it=it: h.activation(
                    out=xnbf[:], in_=xib[k][:], func=AF.Identity,
                    scale=rstd_all[:, it:it + 1], bias=nbias_all[:, it:it + 1]))
                p.waits("tensor", [ev_xn, ev_ptuse, ev_xnTi_use])
                for dc in range(8):
                    ev_tr = p.run("tensor", lambda h, dc=dc: h.transpose(
                        ptr[:, dc * 128:(dc + 1) * 128], xnbf[:, dc * 128:(dc + 1) * 128],
                        ident[:]), inc=(dc == 7))
                ev_tr_last = ev_tr
                p.waits("scalar", [ev_tr])
                for dc in range(8):
                    ev_ptuse = p.run("scalar", lambda h, dc=dc, k=k: h.activation(
                        out=xnTi[:, dc, k * 128:(k + 1) * 128],
                        in_=ptr[:, dc * 128:(dc + 1) * 128], func=AF.Identity,
                        scale=lns[:, dc:dc + 1], bias=lnb[:, dc:dc + 1]))
            ev_xnTi = ev_ptuse
            for g in range(16):
                pvslice = pv[:, (g % 2) * 512:(g % 2) * 512 + IB]
                p.waits("tensor", [ev_xnTi, ev_gate_all, ev_epi, ev_whid_g])
                for dc in range(8):
                    ev_mg = p.run("tensor", lambda h, dc=dc, g=g, pvslice=pvslice: h.matmul(
                        pvslice, whid[:, dc, g * 128:(g + 1) * 128], xnTi[:, dc, :],
                        start=(dc == 0), stop=(dc == 7)), inc=(dc == 7))
                p.waits("scalar", [ev_mg])
                ev_gate_all = p.run("scalar", lambda h, g=g, pvslice=pvslice: h.activation(
                    out=gateT[:, g, :], in_=pvslice, func=AF.Silu, bias=bh2col[:, g:g + 1]))
            for j in range(nt):
                par = j % 2
                psim = pv[:, par * 512:par * 512 + IB]
                p.waits("tensor", [ev_qt, ev_rel_par[par],
                                   ev_gate_all if j <= 1 else None])
                ev_sim = p.run("tensor", lambda h, j=j, i0=i0, psim=psim: h.matmul(
                    psim, KT[:, j * 128:(j + 1) * 128], QT[:, i0:i0 + IB],
                    start=True, stop=True), inc=1)
                p.waits("vector", [ev_sim, ev_sq_par[par]])
                ev_rel_par[par] = p.run("vector", lambda h, psim=psim, par=par:
                                        h.tensor_scalar_max(stmp2[par][:, :], psim, 0.0))
                p.waits("scalar", [ev_rel_par[par], ev_acc_all])
                ev_at = p.run("scalar", lambda h, j=j, par=par: h.activation(
                    out=attnT[:, j, :], in_=stmp2[par][:, :], func=AF.Square, scale=1.0 / T))
                ev_sq_par[par] = ev_at
            ev_attnT = ev_at
            p.dwait("sync", name + "cvs")
            for quarter in range(4):
                for j0 in range(0, nt, 2):
                    vpar = (j0 // 2) % 2
                    cv = name + "cv%d" % vpar
                    p.waits("sync", [ev_vt_rd[vpar]])
                    p.dma("sync", vt[vpar][:, 0:1024].rearrange("p (jt e) -> p jt e", jt=2),
                          v_scr[j0 * 128:(j0 + 2) * 128,
                                quarter * 512:(quarter + 1) * 512].rearrange(
                              "(jt p) e -> p jt e", p=128), chan=cv)
                    p.dwait("tensor", cv)
                    p.waits("tensor", [ev_attnT, ev_ogt])
                    for jj in range(2):
                        j = j0 + jj
                        for vc in range(4):
                            ev_acc = p.run("tensor", lambda h, j=j, jj=jj, vc=vc, vpar=vpar:
                                           h.matmul(
                                pool[:, vc, 0:IB],
                                vt[vpar][:, jj * 512 + vc * 128:jj * 512 + (vc + 1) * 128],
                                attnT[:, j, :], start=(j == 0), stop=(j == nt - 1)),
                                inc=(vc == 3 and jj == 1))
                    ev_vt_rd[vpar] = ev_acc
                ev_acc_all = ev_acc
                p.waits("vector", [ev_acc, ev_gate_all])
                for vc in range(4):
                    g = quarter * 4 + vc
                    ev_ogt = p.run("vector", lambda h, vc=vc, g=g: h.scalar_tensor_tensor(
                        gateT[:, g, :], pool[:, vc, 0:IB], 1.0, gateT[:, g, :], ALU.bypass, ALU.mult))
            for k in range(ntib):
                it = ib * ntib + k
                p.waits("tensor", [ev_ogt, ev_epi])
                for g in range(16):
                    for nb2 in range(2):
                        ev_md = p.run("tensor", lambda h, g=g, nb2=nb2, k=k: h.matmul(
                            pv[:, nb2 * 512:(nb2 + 1) * 512],
                            gateT[:, g, k * 128:(k + 1) * 128],
                            wout[:, g, nb2 * 512:(nb2 + 1) * 512],
                            start=(g == 0), stop=(g == 15)), inc=(g == 15 and nb2 == 1))
                p.waits("vector", [ev_md])
                p.dwait("vector", name + "cso")
                p.run("vector", lambda h: h.tensor_tensor(vtmp[:, :1024], pv, bout_rep[:, :], ALU.add))
                ev_xo = p.run("vector", lambda h, k=k: h.tensor_tensor(
                    vtmp[:, :1024], vtmp[:, :1024], xib[k][:], ALU.add))
                ev_epi = ev_xo
                p.waits("sync", [ev_xo])
                p.dma("sync", x_out[it * 128:(it + 1) * 128, :], vtmp[:, :1024], chan=name + "cso")
        p.dwait("sync", name + "cso")

    return build_l0attna

build_l0attna = _mk_build_l0attna()


def _mk_build_l0ffn():

    T = 4096
    D = 1024
    NT = T // 128
    MAGIC = 12582912.0


    def build_l0ffn(p, t, cc, x_in, x_out, wq, wscale, wbias, name="ffn", consts=None):
        nc = p.nc
        p.add_channel(name + "cst")
        ident, ones1, ev_ident = consts if consts is not None else build_consts(p, t)

        wq8 = t.sb(name + "_wq8", [128, D], I8)
        wqbf = t.sb(name + "_wqbf", [128, D], BF16)
        wqT = t.sb(name + "_wqT", [128, 8, D], BF16)
        wsrow = t.sb(name + "_wsrow", [1, D], F32)
        brow = t.sb(name + "_brow", [1, D], F32)
        wsrep = t.sb(name + "_wsrep", [128, D], F32)
        brep = t.sb(name + "_brep", [128, D], F32)

        pt = t.ps(name + "_pt", [128, 1024], BF16)
        pb = t.ps(name + "_pb", [128, 1024], F32)
        acc = t.ps(name + "_acc", [128, 1024], F32)

        p.dma("sync", wsrow[:, :], wscale.rearrange("(o d) -> o d", o=1), chan=name+"cwb")
        p.dma("sync", brow[:, :], wbias.rearrange("(o d) -> o d", o=1), chan=name+"cwb")
        p.dwait("tensor", name+"cwb")
        p.waits("tensor", [ev_ident])
        for nb in range(2):
            evt = p.run("tensor", lambda h, nb=nb: h.matmul(pb[:, nb * 512:(nb + 1) * 512],
                                                      ones1[:, :], wsrow[:, nb * 512:(nb + 1) * 512],
                                                      start=True, stop=True), inc=(nb == 1))
        p.wait("vector", evt)
        ev_wsrep = p.run("vector", lambda h: h.tensor_copy(wsrep[:], pb[:, :]), inc=1)
        p.wait("tensor", ev_wsrep)
        for nb in range(2):
            evt = p.run("tensor", lambda h, nb=nb: h.matmul(pb[:, nb * 512:(nb + 1) * 512],
                                                      ones1[:, :], brow[:, nb * 512:(nb + 1) * 512],
                                                      start=True, stop=True), inc=(nb == 1))
        p.wait("vector", evt)
        ev_brep = p.run("vector", lambda h: h.tensor_copy(brep[:], pb[:, :]), inc=1)

        ev_copy = None
        evc = None
        evt = None
        for oc in range(8):
            p.waits("sync", [evc])
            p.dma("sync", wq8[:, :], wq[oc * 128:(oc + 1) * 128, :], chan=name+"cw")
            p.dwait("vector", name+"cw")
            p.waits("vector", [evt])
            evc = p.run("vector", lambda h: h.tensor_copy(wqbf[:], wq8[:]), inc=1)
            p.waits("tensor", [evc, ev_copy, ev_brep])
            for dc in range(8):
                evt = p.run("tensor", lambda h, dc=dc: h.transpose(
                    pt[:, dc * 128:(dc + 1) * 128], wqbf[:, dc * 128:(dc + 1) * 128], ident[:]),
                    inc=(dc == 7))
            p.wait("vector", evt)
            ev_copy = p.run("vector", lambda h, oc=oc: h.tensor_copy(
                wqT[:, :, oc * 128:(oc + 1) * 128].rearrange("p c o -> p c o"),
                pt[:, :].rearrange("p (c o) -> p c o", c=8)), inc=1)

        xb = [t.sb(name + "_x0", [128, D], F32), t.sb(name + "_x1", [128, D], F32)]
        xsqp = [t.sb(name + "_xsq%d" % q, [128, D], F32) for q in range(2)]
        xnp = [t.sb(name + "_xn%d" % q, [128, D], F32) for q in range(2)]
        xqfp = [t.sb(name + "_xqf%d" % q, [128, D], F32) for q in range(2)]
        xqbfp = [t.sb(name + "_xqbf%d" % q, [128, D], BF16) for q in range(2)]
        xqTp = [t.sb(name + "_xqT%d" % q, [128, 8, 128], BF16) for q in range(2)]
        t1p = [t.sb(name + "_t1%d" % q, [128, D], F32) for q in range(2)]
        xop = [t.sb(name + "_xo%d" % q, [128, D], F32) for q in range(2)]
        stp = [t.sb(name + "_st%d" % q, [128, 10], F32) for q in range(2)]
        pt2 = t.ps(name + "_pt2", [128, 1024], BF16)
        ptq = [pt, pt2]
        accq = [acc, pb]

        ev_tile_done = [ev_copy, ev_copy]
        ev_pt_used_par = [ev_copy, ev_copy]
        ev_epi_par = [ev_brep, ev_brep]
        ev_mm_par = [None, None]

        def tile_body(i, par, xt, xsq, xn, xqf, xqbf, xqT, t1, xo, st, ptb, accb):
            nonlocal ev_tile_done, ev_pt_used_par, ev_epi_par, ev_mm_par
            s1 = st[:, 0:1]
            s2 = st[:, 1:2]
            mean = st[:, 2:3]
            m2 = st[:, 3:4]
            var = st[:, 4:5]
            rstd = st[:, 5:6]
            nbias = st[:, 6:7]
            amax = st[:, 7:8]
            xs = st[:, 8:9]
            rxs = st[:, 9:10]
            gd = ev_tile_done[par]
            p.waits("sync", [gd])
            cx = name + "cx%d" % par
            p.dma("sync", xt[:, :], x_in[i * 128:(i + 1) * 128, :], chan=cx)
            p.dwait("scalar", cx)
            p.waits("scalar", [gd])
            ev_sq = p.run("scalar", lambda h: h.square(xsq[:], xt[:]), inc=1)
            p.dwait("vector", cx)
            p.waits("vector", [gd])
            p.run("vector", lambda h: h.reduce_sum(s1, xt[:], axis=mybir.AxisListType.X))
            p.waits("vector", [ev_sq])
            ev_v = p.run("vector", lambda h: h.reduce_sum(s2, xsq[:], axis=mybir.AxisListType.X), inc=1)
            p.wait("vector", ev_v)
            p.run("vector", lambda h: h.tensor_scalar_mul(mean, s1, 1.0 / D))
            p.run("vector", lambda h: h.tensor_scalar_mul(s2, s2, 1.0 / D))
            p.run("vector", lambda h: h.tensor_tensor(m2, mean, mean, ALU.mult))
            ev_v = p.run("vector", lambda h: h.tensor_tensor(var, s2, m2, ALU.subtract), inc=1)
            p.waits("scalar", [ev_v, cc.ev(1e-5)])
            p.run("scalar", lambda h: h.activation(out=rstd, in_=var, func=AF.Ln, bias=cc.ap(1e-5)))
            ev_s = p.run("scalar", lambda h: h.activation(out=rstd, in_=rstd, func=AF.Exp, scale=-0.5), inc=1)
            p.waits("vector", [ev_s])
            ev_nb = p.run("vector", lambda h: h.scalar_tensor_tensor(
                nbias, mean, -1.0, rstd, ALU.mult, ALU.mult), inc=1)
            p.waits("scalar", [ev_nb])
            ev_xn = p.run("scalar", lambda h: h.activation(out=xn[:], in_=xt[:], func=AF.Identity,
                                                           scale=rstd, bias=nbias), inc=1)
            p.waits("vector", [ev_xn])
            ev_am = p.run("vector", lambda h: h.tensor_reduce(
                amax, xn[:], axis=mybir.AxisListType.X, op=ALU.max, apply_absolute_value=True), inc=1)
            p.wait("vector", ev_am)
            p.run("vector", lambda h: h.tensor_scalar(xs, amax, 1e-6, 1.0 / 127.0, ALU.max, ALU.mult))
            ev_rxs = p.run("vector", lambda h: h.reciprocal(rxs, xs), inc=1)
            p.waits("scalar", [ev_rxs])
            ev_q = p.run("scalar", lambda h: h.activation(out=xqf[:], in_=xn[:], func=AF.Copy,
                                                          scale=rxs), inc=1)
            p.waits("vector", [ev_q])
            p.run("vector", lambda h: h.tensor_scalar(xqf[:], xqf[:], MAGIC, MAGIC, ALU.add, ALU.subtract))
            ev_cl = p.run("vector", lambda h: h.tensor_scalar(xqbf[:], xqf[:], 127.0, -128.0, ALU.min, ALU.max), inc=1)
            p.waits("tensor", [ev_cl, ev_pt_used_par[par]])
            for dc in range(8):
                evt = p.run("tensor", lambda h, dc=dc: h.transpose(
                    ptb[:, dc * 128:(dc + 1) * 128], xqbf[:, dc * 128:(dc + 1) * 128], ident[:]),
                    inc=(dc == 7))
            p.wait("vector", evt)
            ev_xqT = p.run("vector", lambda h: h.tensor_copy(
                xqT[:, :, :], ptb[:, :].rearrange("p (c o) -> p c o", c=8)), inc=1)
            ev_pt_used_par[par] = ev_xqT
            p.waits("tensor", [ev_xqT, ev_epi_par[par]])
            for dc in range(8):
                for nb in range(2):
                    ev_mm = p.run("tensor", lambda h, nb=nb, dc=dc: h.matmul(
                        accb[:, nb * 512:(nb + 1) * 512], xqT[:, dc, :],
                        wqT[:, dc, nb * 512:(nb + 1) * 512],
                        start=(dc == 0), stop=(dc == 7)), inc=(dc == 7 and nb == 1))
            ev_mm_par[par] = ev_mm
            p.waits("scalar", [ev_mm])
            ev_epi = p.run("scalar", lambda h: h.activation(out=t1[:], in_=accb[:, :], func=AF.Copy,
                                                            scale=xs), inc=1)
            ev_epi_par[par] = ev_epi
            p.waits("vector", [ev_epi])
            p.run("vector", lambda h: h.tensor_tensor(t1[:], t1[:], wsrep[:], ALU.mult))
            p.run("vector", lambda h: h.tensor_tensor(t1[:], t1[:], brep[:], ALU.add))
            p.dwait("vector", name + "cst%d" % par)
            ev_xo = p.run("vector", lambda h: h.tensor_tensor(xo[:], t1[:], xt[:], ALU.add), inc=1)
            ev_tile_done[par] = ev_xo
            p.waits("sync", [ev_xo])
            p.dma("sync", x_out[i * 128:(i + 1) * 128, :], xo[:, :], chan=name + "cst%d" % par)

        for i in range(NT):
            par = i % 2
            tile_body(i, par, xb[par], xsqp[par], xnp[par], xqfp[par], xqbfp[par],
                      xqTp[par], t1p[par], xop[par], stp[par], ptq[par], accq[par])
        p.dwait("sync", name + "cst0")
        p.dwait("sync", name + "cst1")

    return build_l0ffn

build_l0ffn = _mk_build_l0ffn()


def _mk_build_l0attnb():

    T = 4096
    D = 1024
    NH = 16
    E = 64
    IBB = 1024


    def build_l0attnb(p, t, cc, consts, x_in, x_out, v_scr, w, name="a0b"):
        nc = p.nc
        ident, ones1, ev_ident = consts
        nt = T // 128
        ntb = T // 512
        nib = T // IBB

        xnT = t.sb(name + "_xnT", [128, 8, T], BF16)
        attT = t.sb(name + "_attT", [128, 8, T], BF16)
        QTh = t.sb(name + "_QTh", [64, T], BF16)
        KTh = t.sb(name + "_KTh", [64, T], BF16)
        PT = t.sb(name + "_PT", [128, IBB], BF16)
        PT2 = t.sb(name + "_PT2", [128, IBB], BF16)
        Vh = t.sb(name + "_Vh", [128, nt, 65], BF16)
        Vh2 = t.sb(name + "_Vh2", [128, nt, 65], BF16)
        wqkv = t.sb(name + "_wqkv", [128, 8, 128], BF16)
        wvall = t.sb(name + "_wvall", [128, 8, 1024], BF16)
        bv_rep = t.sb(name + "_bvrep", [128, 1024], BF16)
        bqcol = t.sb(name + "_bqcol", [64, NH], F32)
        bkcol = t.sb(name + "_bkcol", [64, NH], F32)
        rstd_all = t.sb(name + "_rstdall", [128, nt], F32)
        nbias_all = t.sb(name + "_nbiasall", [128, nt], F32)
        recip = t.sb(name + "_recip", [1, IBB], F32)
        pden_sb = t.sb(name + "_pdensb", [64, IBB], F32)
        vstage = t.sb(name + "_vstage", [128, NH, 65], BF16)
        xt2 = [t.sb(name + "_x0", [128, D], F32), t.sb(name + "_x1", [128, D], F32)]
        xsq = t.sb(name + "_xsq", [128, D], F32)
        xnbf = t.sb(name + "_xnbf", [128, D], BF16)
        wtmp = t.sb(name + "_wtmp", [128, 1024], F32)
        s1 = t.sb(name + "_s1", [128, 1], F32)
        s2 = t.sb(name + "_s2", [128, 1], F32)
        mean = t.sb(name + "_mean", [128, 1], F32)
        m2 = t.sb(name + "_m2", [128, 1], F32)
        var = t.sb(name + "_var", [128, 1], F32)

        pool = t.ps(name + "_pool", [128, 6, 512], F32)
        pnorm = t.ps(name + "_pnorm", [128, 512], F32)
        ptr = t.ps(name + "_ptr", [128, 1024], BF16)
        psim = [pool[:, 0:2, :].rearrange("p a b -> p (a b)"),
                pool[:, 2:4, :].rearrange("p a b -> p (a b)")]
        pO = pool[:, 4:6, :].rearrange("p a b -> p (a b)")
        pproj = pool[:, 0, :]
        pproj2 = pool[:, 1, :]
        pdelta = pool[:, 2:4, :].rearrange("p a b -> p (a b)")

        cwb = name + "cwb"
        for h in range(NH):
            p.dma("sync", bqcol[:, h:h + 1], w["bq"][h, 0, :].rearrange("(p o) -> p o", o=1), chan=cwb)
            p.dma("sync", bkcol[:, h:h + 1], w["bk"][h, 0, :].rearrange("(p o) -> p o", o=1), chan=cwb)
        borow = wtmp[0:1, 0:1024]
        bvrow = wtmp[0:1, 0:1024]
        p.dwait("tensor", cwb)
        p.waits("tensor", [ev_ident])
        p.dma("sync", bvrow, w["bv"].rearrange("h o e -> o (h e)"), chan=cwb + "V")
        p.dwait("tensor", cwb + "V")
        for nb in range(2):
            evb = p.run("tensor", lambda h, nb=nb: h.matmul(
                pproj if nb == 0 else pproj2, ones1[:, :],
                bvrow[:, nb * 512:(nb + 1) * 512], start=True, stop=True), inc=1)
            p.wait("vector", evb)
            ev_b = p.run("vector", lambda h, nb=nb: h.tensor_copy(
                bv_rep[:, nb * 512:(nb + 1) * 512], pproj if nb == 0 else pproj2))
            p.wait("tensor", ev_b)
        cwh = name + "cwh"
        evc = evb
        for c in range(8):
            p.waits("sync", [evc])
            for hq in range(4):
                p.dma("sync", wtmp[:, hq * 256:(hq + 1) * 256].rearrange("p (h e) -> p h e", h=4),
                      w["wv"][hq * 4:(hq + 1) * 4, c * 128:(c + 1) * 128, :].rearrange("h p e -> p h e"),
                      chan=cwh)
                p.dwait("sync", cwh)
            p.dwait("vector", cwh)
            p.waits("vector", [evc])
            evc = p.run("vector", lambda h, c=c: h.tensor_copy(wvall[:, c, :], wtmp[:, :1024]))
        ev_wpre = evc
        ev_tr = None

        ev_xuse = [ev_wpre, ev_wpre]
        ev_ptuse = ev_wpre
        ev_tr_last = ev_tr
        ev_pv_eva = None
        ev_vst = None
        for i in range(nt):
            par = i % 2
            xt = xt2[par]
            cx = name + "cxa%d" % par
            p.waits("sync", [ev_xuse[par]])
            p.dma("sync", xt[:, :], x_in[i * 128:(i + 1) * 128, :], chan=cx)
            p.dwait("scalar", cx)
            ev_sq = p.run("scalar", lambda h, xt=xt: h.square(xsq[:], xt[:]))
            p.dwait("vector", cx)
            p.run("vector", lambda h, xt=xt: h.reduce_sum(s1[:], xt[:], axis=mybir.AxisListType.X))
            p.waits("vector", [ev_sq])
            p.run("vector", lambda h: h.reduce_sum(s2[:], xsq[:], axis=mybir.AxisListType.X))
            p.run("vector", lambda h: h.tensor_scalar_mul(mean[:], s1[:], 1.0 / D))
            p.run("vector", lambda h: h.tensor_scalar_mul(s2[:], s2[:], 1.0 / D))
            p.run("vector", lambda h: h.tensor_tensor(m2[:], mean[:], mean[:], ALU.mult))
            ev_v = p.run("vector", lambda h: h.tensor_tensor(var[:], s2[:], m2[:], ALU.subtract))
            p.waits("scalar", [ev_v, cc.ev(1e-5)])
            p.run("scalar", lambda h, i=i: h.activation(out=rstd_all[:, i:i + 1], in_=var[:],
                                                        func=AF.Ln, bias=cc.ap(1e-5)))
            ev_s = p.run("scalar", lambda h, i=i: h.activation(
                out=rstd_all[:, i:i + 1], in_=rstd_all[:, i:i + 1], func=AF.Exp, scale=-0.5))
            p.waits("vector", [ev_s])
            ev_nb = p.run("vector", lambda h, i=i: h.scalar_tensor_tensor(
                nbias_all[:, i:i + 1], mean[:], -1.0, rstd_all[:, i:i + 1], ALU.mult, ALU.mult))
            p.waits("scalar", [ev_nb, ev_tr_last])
            ev_xn = p.run("scalar", lambda h, xt=xt, i=i: h.activation(
                out=xnbf[:], in_=xt[:], func=AF.Identity,
                scale=rstd_all[:, i:i + 1], bias=nbias_all[:, i:i + 1]))
            ev_xuse[par] = ev_xn
            p.waits("tensor", [ev_xn, ev_ptuse])
            for dc in range(8):
                ev_tr = p.run("tensor", lambda h, dc=dc: h.transpose(
                    ptr[:, dc * 128:(dc + 1) * 128], xnbf[:, dc * 128:(dc + 1) * 128], ident[:]),
                    inc=(dc == 7))
            ev_tr_last = ev_tr
            p.waits("scalar", [ev_tr])
            for dc in range(8):
                ev_ptuse = p.run("scalar", lambda h, dc=dc, i=i: h.activation(
                    out=xnT[:, dc, i * 128:(i + 1) * 128], in_=ptr[:, dc * 128:(dc + 1) * 128],
                    func=AF.Copy))
            for nb in range(2):
                pp = pproj if nb == 0 else pproj2
                p.waits("tensor", [ev_ptuse, ev_pv_eva])
                for dc in range(8):
                    ev_mv = p.run("tensor", lambda h, dc=dc, nb=nb, pp=pp, i=i: h.matmul(
                        pp, xnT[:, dc, i * 128:(i + 1) * 128],
                        wvall[:, dc, nb * 512:(nb + 1) * 512],
                        start=(dc == 0), stop=(dc == 7)), inc=(dc == 7))
                p.waits("vector", [ev_mv])
                p.dwait("vector", name + "cvs")
                ev_pv_eva = p.run("vector", lambda h, nb=nb, pp=pp: h.tensor_tensor(
                    vstage[:, nb * 8:(nb + 1) * 8, 0:64],
                    pp.rearrange("p (g e) -> p g e", e=64),
                    bv_rep[:, nb * 512:(nb + 1) * 512].rearrange("p (g e) -> p g e", e=64),
                    ALU.add))
            if i == 0:
                ev_ones = p.run("gpsimd", lambda h: h.memset(vstage[:, :, 64:65], 1.0))
                p.waits("sync", [ev_ones])
            ev_vst = ev_pv_eva
            p.waits("sync", [ev_pv_eva])
            p.dwait("sync", name + "cvs")
            for hh in range(NH):
                p.dma("sync", v_scr[hh, i * 128:(i + 1) * 128, :], vstage[:, hh, :],
                      chan=name + "cvs")

        ev_attT_use = None
        ev_mq = None
        ev_qk_use = None
        ev_at_w = None
        ev_exp_par = [ev_vst, ev_vst]
        ev_acc_par = [None, None]
        ev_vh_use = [None, None]
        ev_pn_ev = None
        PTb = [PT, PT2]
        Vhb = [Vh, Vh2]
        p.dwait("sync", name + "cvs")
        for hh in range(NH):
            cvh = name + "cvh%d" % (hh % 2)
            Vhh = Vhb[hh % 2]
            p.waits("sync", [ev_vh_use[hh % 2]])
            for q4 in range(0, nt, 8):
                qn = min(8, nt - q4)
                p.dma("sync", Vhh[:, q4:q4 + qn, :],
                      v_scr[hh, q4 * 128:(q4 + qn) * 128, :].rearrange("(jt p) e -> p jt e", p=128),
                      chan=cvh)
            ev_qke = None
            for dst, wsrc, bcol in ((QTh, "wq", bqcol), (KTh, "wk", bkcol)):
                p.waits("sync", [p.ev("vector")])
                p.dma("sync", wtmp[:, 0:512].rearrange("p (c e) -> p c e", c=8),
                      w[wsrc][hh, :, :].rearrange("(c p) e -> p c e", p=128), chan=cwh)
                p.dwait("vector", cwh)
                p.waits("vector", [ev_mq])
                ev_wc = p.run("vector", lambda h: h.tensor_copy(
                    wqkv[:, :, 0:64], wtmp[:, 0:512].rearrange("p (c e) -> p c e", c=8)))
                for tb in range(ntb):
                    p.waits("tensor", [ev_wc, ev_qk_use, ev_exp_par[0], ev_qke])
                    for dc in range(8):
                        ev_mq = p.run("tensor", lambda h, dc=dc, tb=tb: h.matmul(
                            pproj[0:64, :], wqkv[:, dc, 0:64],
                            xnT[:, dc, tb * 512:(tb + 1) * 512],
                            start=(dc == 0), stop=(dc == 7)), inc=(dc == 7))
                    p.waits("scalar", [ev_mq])
                    ev_qke = p.run("scalar", lambda h, dst=dst, tb=tb, hh=hh, bcol=bcol:
                                   h.activation(out=dst[:, tb * 512:(tb + 1) * 512],
                                                in_=pproj[0:64, :], func=AF.Identity,
                                                bias=bcol[:, hh:hh + 1]))
            ev_qk = ev_qke
            for ib in range(nib):
                i0 = ib * IBB
                ev_sm_j = [None] * nt
                ev_pT_j = [None] * nt
                nhalf = IBB // 512
                for j in range(nt + 1):
                    if j < nt:
                        ps = psim[j % 2]
                        p.waits("tensor", [ev_qk, ev_exp_par[j % 2]])
                        for half in range(nhalf):
                            ev_sm = p.run("tensor", lambda h, j=j, i0=i0, ps=ps, half=half: h.matmul(
                                ps[:, half * 512:(half + 1) * 512], KTh[:, j * 128:(j + 1) * 128],
                                QTh[:, i0 + half * 512:i0 + (half + 1) * 512],
                                start=True, stop=True), inc=(half == nhalf - 1))
                        ev_sm_j[j] = ev_sm
                        p.waits("scalar", [ev_sm, cc.ev(-15.0), ev_acc_par[j % 2]])
                        ev_pT_j[j] = p.run("scalar", lambda h, ps=ps, j=j: h.activation(
                            out=PTb[j % 2][:, 0:IBB], in_=ps[:, 0:IBB], func=AF.Exp,
                            scale=0.125, bias=cc.ap(-15.0)))
                        ev_exp_par[j % 2] = ev_pT_j[j]
                    if j >= 1:
                        jj = j - 1
                        if jj == 0:
                            p.dwait("tensor", cvh)
                        p.waits("tensor", [ev_pT_j[jj], ev_at_w if jj == 0 else None])
                        for half in range(nhalf):
                            ev_acc = p.run("tensor", lambda h, jj=jj, half=half, Vhh=Vhh: h.matmul(
                                pO[0:65, half * 512:(half + 1) * 512], Vhh[:, jj, :],
                                PTb[jj % 2][:, half * 512:(half + 1) * 512],
                                start=(jj == 0), stop=(jj == nt - 1)), inc=(half == nhalf - 1))
                        ev_acc_par[jj % 2] = ev_acc
                ev_vh_use[hh % 2] = ev_acc
                ev_qk_use = ev_sm_j[nt - 1]
                p.waits("vector", [ev_acc])
                ev_rc = p.run("vector", lambda h: h.reciprocal(recip[:, 0:IBB], pO[64:65, 0:IBB]))
                for nb in range(nhalf):
                    p.waits("tensor", [ev_rc, ev_pn_ev])
                    ev_bc = p.run("tensor", lambda h, nb=nb: h.matmul(
                        pnorm[0:64, :], ones1[:, 0:64],
                        recip[:, nb * 512:(nb + 1) * 512], start=True, stop=True), inc=1)
                    p.waits("scalar", [ev_bc])
                    ev_pn_ev = p.run("scalar", lambda h, nb=nb: h.copy(
                        out=pden_sb[:, nb * 512:(nb + 1) * 512], in_=pnorm[0:64, :]))
                p.waits("vector", [ev_pn_ev, ev_attT_use])
                ev_at_w = p.run("vector", lambda h, hh=hh, i0=i0: h.scalar_tensor_tensor(
                    attT[(hh % 2) * 64:(hh % 2) * 64 + 64, hh // 2, i0:i0 + IBB],
                    pO[0:64, 0:IBB], 1.0, pden_sb[:, 0:IBB], ALU.bypass, ALU.mult))

        p.waits("sync", [p.ev("vector")])
        p.dma("sync", borow, w["bo"].rearrange("(o d) -> o d", o=1), chan=cwb + "L")
        p.dwait("tensor", cwb + "L")
        p.waits("tensor", [ev_at_w])
        for nb in range(2):
            evb = p.run("tensor", lambda h, nb=nb: h.matmul(
                pproj if nb == 0 else pproj2, ones1[:, :],
                borow[:, nb * 512:(nb + 1) * 512], start=True, stop=True), inc=1)
            p.wait("vector", evb)
            ev_b = p.run("vector", lambda h, nb=nb: h.tensor_copy(
                bv_rep[:, nb * 512:(nb + 1) * 512], pproj if nb == 0 else pproj2))
            p.wait("tensor", ev_b)
        evc = ev_b
        for c in range(8):
            p.waits("sync", [evc])
            p.dma("sync", wtmp[:, :1024], w["wo"][c * 128:(c + 1) * 128, :], chan=cwh)
            p.dwait("vector", cwh)
            evc = p.run("vector", lambda h: h.tensor_copy(xnbf[:], wtmp[:, :1024]))
            p.waits("tensor", [evc, ev_ptuse])
            for dc in range(8):
                ev_tr = p.run("tensor", lambda h, dc=dc: h.transpose(
                    ptr[:, dc * 128:(dc + 1) * 128], xnbf[:, dc * 128:(dc + 1) * 128], ident[:]),
                    inc=(dc == 7))
            ev_ptuse = ev_tr
            p.waits("vector", [ev_tr])
            evc = p.run("vector", lambda h, c=c: h.tensor_copy(
                wvall[:, :, c * 128:(c + 1) * 128],
                ptr[:, :].rearrange("p (a b) -> p a b", a=8)))
        ev_wo_ready = evc

        ev_epi = None
        ev_md = None
        for i in range(nt):
            par = i % 2
            xt = xt2[par]
            cx = name + "cxb%d" % par
            p.waits("sync", [ev_epi])
            p.dma("sync", xt[:, :], x_in[i * 128:(i + 1) * 128, :], chan=cx)
            p.waits("tensor", [ev_at_w, ev_epi, ev_md, ev_wo_ready])
            for dc in range(8):
                for nb in range(2):
                    ev_md = p.run("tensor", lambda h, dc=dc, nb=nb, i=i: h.matmul(
                        pdelta[:, nb * 512:(nb + 1) * 512], attT[:, dc, i * 128:(i + 1) * 128],
                        wvall[:, dc, nb * 512:(nb + 1) * 512],
                        start=(dc == 0), stop=(dc == 7)), inc=(dc == 7 and nb == 1))
            p.dwait("vector", cx)
            p.waits("vector", [ev_md])
            p.dwait("vector", name + "cso")
            p.run("vector", lambda h: h.tensor_tensor(xsq[:, :], pdelta, bv_rep[:, :], ALU.add))
            ev_epi = p.run("vector", lambda h, xt=xt: h.tensor_tensor(xsq[:, :], xsq[:, :], xt[:], ALU.add))
            p.waits("sync", [ev_epi])
            p.dma("sync", x_out[i * 128:(i + 1) * 128, :], xsq[:, :], chan=name + "cso")
        p.dwait("sync", name + "cso")

    return build_l0attnb

build_l0attnb = _mk_build_l0attnb()


def _mk_build_l1attna():

    T = 4096
    D = 1024
    N = 1024
    TB = 512
    NSQ = 11


    def build_l1attna(p, t, cc, consts, x_in, x_out, w, name="a1a"):
        nc = p.nc
        ident, ones1, ev_ident = consts
        nt = T // 128
        ntb = T // TB
        identf = t.handles["c_identf"]

        xnT = t.sb(name + "_xnT", [128, 8, T], BF16)
        big = t.sb(name + "_big", [128, 24576], BF16)
        WinT = t.sb(name + "_WinT", [128, 8, N], BF16)
        WoutT = t.sb(name + "_WoutT", [128, 8, D], BF16)
        yhT = t.sb(name + "_yhT", [128, 8, TB], BF16)
        rstd_all = t.sb(name + "_rstdall", [128, nt], F32)
        nbias_all = t.sb(name + "_nbiasall", [128, nt], F32)
        bpcol = t.sb(name + "_bpcol", [128, 24], F32)
        dbcol = t.sb(name + "_dbcol", [128, 8], F32)
        nalpha = t.sb(name + "_nalpha", [128, 8], F32)
        bpd = t.sb(name + "_bpd", [128, 8], F32)
        carry = t.sb(name + "_carry", [128, 8], F32)
        carryprev = t.sb(name + "_cprev", [128, 1], F32)
        ing = t.sb(name + "_ing", [128, 1], F32)
        iso = t.sb(name + "_iso", [128, 1], F32)
        onecol = t.sb(name + "_onecol", [128, 1], F32)
        sc1 = t.sb(name + "_sc1", [1, 4], F32)
        wtmp = t.sb(name + "_wtmp", [128, 1024], F32)
        xnbf = t.sb(name + "_xnbf", [128, D], BF16)
        xt2 = [t.sb(name + "_x0", [128, D], F32), t.sb(name + "_x1", [128, D], F32)]
        xsq = t.sb(name + "_xsq", [128, D], F32)
        s1 = t.sb(name + "_s1", [128, 1], F32)
        s2 = t.sb(name + "_s2", [128, 1], F32)
        mean = t.sb(name + "_mean", [128, 1], F32)
        m2 = t.sb(name + "_m2", [128, 1], F32)
        var = t.sb(name + "_var", [128, 1], F32)
        dtr = t.sb(name + "_dtr", [128, 1], F32)
        dtmp = t.sb(name + "_dtmp", [128, 128], F32)
        uTp = [t.sb(name + "_uT%d" % q, [128, TB], F32) for q in range(2)]
        txp = [t.sb(name + "_tx%d" % q, [128, TB], F32) for q in range(2)]
        taxp = [t.sb(name + "_tax%d" % q, [128, TB], F32) for q in range(2)]
        tdelta = t.sb(name + "_tdelta", [128, TB], F32)
        tap = [t.sb(name + "_ta%d" % q, [128, TB], F32) for q in range(2)]
        tbp_ = [t.sb(name + "_tb%d" % q, [128, TB], F32) for q in range(2)]
        tcp_ = [t.sb(name + "_tc%d" % q, [128, TB], F32) for q in range(2)]
        tpp = [t.sb(name + "_tp%d" % q, [128, TB], F32) for q in range(2)]
        tr_ = xsq[:, 0:TB]
        tdp = [t.sb(name + "_td%d" % q, [128, TB], F32) for q in range(2)]
        trs = xsq[:, TB:2 * TB]
        zT = tdelta
        epi = wtmp[:, 0:D]

        bigv = big.rearrange("p (a c q) -> p a c q", a=3, c=8)
        Wnat = bigv[:, 0, :, :]
        Bm = bigv[:, 1, :, :]
        B2m = bigv[:, 2, :, :]
        WpT = big.rearrange("p (c k) -> p c k", c=8)

        pool = t.ps(name + "_pool", [128, 7, 512], F32)
        ptr = t.ps(name + "_ptr", [128, 1024], BF16)
        pa = pool[:, 0, :]
        pb = pool[:, 1, :]
        praw = [pool[:, 2, :], pool[:, 3, :], pool[:, 4, :]]
        pu = pool[:, 5, :]
        ps1 = pool[:, 6, 0:128]

        cwb = name + "cwb"

        def load_col(dst, src1d, ncol):
            for c in range(ncol):
                p.dma("sync", dst[:, c:c + 1],
                      src1d[c * 128:(c + 1) * 128].rearrange("(p o) -> p o", o=1), chan=cwb)
        load_col(bpcol, w["bp"], 24)
        load_col(dbcol, w["delta_bias"], 8)
        load_col(nalpha, w["alpha_log"], 8)
        p.dma("sync", sc1[0:1, 0:1], w["log_gamma"].rearrange("(o k) -> o k", o=1), chan=cwb)
        p.dwait("vector", cwb)
        p.dwait("scalar", cwb)
        p.dwait("gpsimd", cwb)
        ev_cz = p.run("gpsimd", lambda h: h.memset(carry[:, :], 0.0))
        ev_oc = p.run("gpsimd", lambda h: h.memset(onecol[:, :], 1.0))
        p.run("vector", lambda h: h.tensor_tensor(bpd[:, :], bpcol[:, 0:8], dbcol[:, :], ALU.add))
        p.run("scalar", lambda h: h.activation(out=txp[0][:, 0:8], in_=nalpha[:, :], func=AF.Abs))
        p.run("scalar", lambda h: h.activation(out=txp[0][:, 8:16], in_=txp[0][:, 0:8], func=AF.Exp, scale=-1.0))
        p.waits("scalar", [cc.ev(1.0), cc.ev(1e-12)])
        ev_s = p.run("scalar", lambda h: h.activation(out=txp[0][:, 8:16], in_=txp[0][:, 8:16], func=AF.Ln,
                                                      bias=cc.ap(1.0)))
        p.waits("vector", [ev_s])
        p.run("vector", lambda h: h.tensor_scalar_max(txp[0][:, 0:8], nalpha[:, :], 0.0))
        p.run("vector", lambda h: h.tensor_tensor(txp[0][:, 0:8], txp[0][:, 0:8], txp[0][:, 8:16], ALU.add))
        ev_na = p.run("vector", lambda h: h.tensor_scalar_mul(nalpha[:, :], txp[0][:, 0:8], -1.0))

        def sigma_phase(wap, transposed_dst, inv_dst, tag, extra_scale_ev=None, extra_scale=None):
            cw = name + "cw" + tag
            evc = p.ev("vector")
            ev_tr = None
            for c in range(8):
                p.waits("sync", [evc])
                p.dma("sync", wtmp[:, :], wap[c * 128:(c + 1) * 128, :], chan=cw)
                p.dwait("vector", cw)
                evc = p.run("vector", lambda h, c=c: h.tensor_copy(Wnat[:, c, :], wtmp[:, :]))
            for c in range(8):
                p.waits("tensor", [evc])
                for dc in range(8):
                    ev_tr = p.run("tensor", lambda h, c=c, dc=dc: h.transpose(
                        ptr[:, dc * 128:(dc + 1) * 128], Wnat[:, c, dc * 128:(dc + 1) * 128],
                        ident[:]), inc=(dc == 7))
                p.waits("vector", [ev_tr])
                evc = p.run("vector", lambda h, c=c: h.tensor_copy(
                    transposed_dst[:, :, c * 128:(c + 1) * 128],
                    ptr[:, :].rearrange("p (a b) -> p a b", a=8)))
            ev_ev = evc
            for mt in range(8):
                for nb2 in range(2):
                    pp = pa if nb2 == 0 else pb
                    p.waits("tensor", [ev_ev])
                    for k in range(8):
                        ev_mm = p.run("tensor", lambda h, mt=mt, nb2=nb2, k=k, pp=pp: h.matmul(
                            pp, Wnat[:, k, mt * 128:(mt + 1) * 128],
                            Wnat[:, k, nb2 * 512:(nb2 + 1) * 512],
                            start=(k == 0), stop=(k == 7)), inc=(k == 7))
                    p.waits("scalar", [ev_mm])
                    ev_ev = p.run("scalar", lambda h, mt=mt, nb2=nb2, pp=pp: h.copy(
                        out=Bm[:, mt, nb2 * 512:(nb2 + 1) * 512], in_=pp))
            src, dst = Bm, B2m
            for it in range(NSQ):
                for mt in range(8):
                    for nb2 in range(2):
                        pp = pa if nb2 == 0 else pb
                        p.waits("tensor", [ev_ev])
                        for k in range(8):
                            ev_mm = p.run("tensor", lambda h, mt=mt, nb2=nb2, k=k, pp=pp,
                                          src=src: h.matmul(
                                pp, src[:, k, mt * 128:(mt + 1) * 128],
                                src[:, k, nb2 * 512:(nb2 + 1) * 512],
                                start=(k == 0), stop=(k == 7)), inc=(k == 7))
                        p.waits("scalar", [ev_mm])
                        ev_ev = p.run("scalar", lambda h, mt=mt, nb2=nb2, pp=pp, dst=dst: h.copy(
                            out=dst[:, mt, nb2 * 512:(nb2 + 1) * 512], in_=pp))
                p.waits("vector", [ev_ev, ev_oc])
                p.run("vector", lambda h: h.memset(dtr[:, :], 0.0))
                for c in range(8):
                    p.run("vector", lambda h, c=c, dst=dst: h.tensor_tensor(
                        dtmp[:, :], dst[:, c, c * 128:(c + 1) * 128], identf[:, :], ALU.mult))
                    p.run("vector", lambda h: h.reduce_sum(s1[:], dtmp[:, :], axis=mybir.AxisListType.X))
                    p.run("vector", lambda h: h.tensor_tensor(dtr[:, :], dtr[:, :], s1[:], ALU.add))
                ev_dt = p.ev("vector")
                p.waits("tensor", [ev_dt])
                ev_tm = p.run("tensor", lambda h: h.matmul(ps1[0:1, 0:1], dtr[:, 0:1], onecol[:, 0:1],
                                                           start=True, stop=True), inc=1)
                p.waits("vector", [ev_tm])
                ev_rc = p.run("vector", lambda h: h.reciprocal(sc1[0:1, 1:2], ps1[0:1, 0:1]))
                p.waits("tensor", [ev_rc])
                ev_bc = p.run("tensor", lambda h: h.matmul(ps1[:, 0:1], ones1[:, :], sc1[0:1, 1:2],
                                                           start=True, stop=True), inc=1)
                p.waits("scalar", [ev_bc])
                ev_sc = p.run("scalar", lambda h: h.copy(out=s2[:, :], in_=ps1[:, 0:1]))
                p.waits("vector", [ev_sc])
                for c in range(8):
                    ev_ev = p.run("vector", lambda h, c=c, dst=dst: h.tensor_scalar_mul(
                        dst[:, c, :], dst[:, c, :], s2[:, 0:1]))
                src, dst = dst, src
            Bfin = src
            p.waits("vector", [ev_ev])
            p.run("vector", lambda h: h.memset(dtr[:, :], 0.0))
            ev_z = p.run("vector", lambda h: h.memset(s2[:, :], 0.0))
            ev_ev2 = ev_z
            for mt in range(8):
                for nb2 in range(2):
                    pp = pa if nb2 == 0 else pb
                    p.waits("tensor", [ev_ev2, ev_ev])
                    for k in range(8):
                        ev_mm = p.run("tensor", lambda h, mt=mt, nb2=nb2, k=k, pp=pp, Bfin=Bfin:
                                      h.matmul(
                            pp, transposed_dst[:, k, mt * 128:(mt + 1) * 128],
                            Bfin[:, k, nb2 * 512:(nb2 + 1) * 512],
                            start=(k == 0), stop=(k == 7)), inc=(k == 7))
                    p.waits("vector", [ev_mm])
                    p.run("vector", lambda h, mt=mt, nb2=nb2, pp=pp: h.tensor_tensor(
                        uTp[0][:, :], pp, Wnat[:, mt, nb2 * 512:(nb2 + 1) * 512], ALU.mult))
                    p.run("vector", lambda h: h.reduce_sum(s1[:], uTp[0][:, 0:512], axis=mybir.AxisListType.X))
                    ev_ev2 = p.run("vector", lambda h: h.tensor_tensor(dtr[:, :], dtr[:, :], s1[:], ALU.add))
            p.waits("vector", [ev_ev2])
            p.run("vector", lambda h: h.memset(uTp[0][:, 0:1], 0.0))
            for c in range(8):
                p.run("vector", lambda h, c=c, Bfin=Bfin: h.tensor_tensor(
                    dtmp[:, :], Bfin[:, c, c * 128:(c + 1) * 128], identf[:, :], ALU.mult))
                p.run("vector", lambda h: h.reduce_sum(s1[:], dtmp[:, :], axis=mybir.AxisListType.X))
                p.run("vector", lambda h: h.tensor_tensor(uTp[0][:, 0:1], uTp[0][:, 0:1], s1[:], ALU.add))
            ev_tb = p.ev("vector")
            p.waits("tensor", [ev_tb])
            p.run("tensor", lambda h: h.matmul(ps1[0:1, 1:2], dtr[:, 0:1], onecol[:, 0:1],
                                               start=True, stop=True))
            ev_tm = p.run("tensor", lambda h: h.matmul(ps1[0:1, 2:3], uTp[0][:, 0:1], onecol[:, 0:1],
                                                       start=True, stop=True), inc=1)
            p.waits("vector", [ev_tm])
            p.run("vector", lambda h: h.reciprocal(sc1[0:1, 3:4], ps1[0:1, 2:3]))
            ev_l = p.run("vector", lambda h: h.tensor_tensor(sc1[0:1, 0:1] if False else sc1[0:1, 1:2],
                                                             ps1[0:1, 1:2], sc1[0:1, 3:4], ALU.mult))
            p.waits("scalar", [ev_l])
            ev_sg = p.run("scalar", lambda h: h.activation(out=sc1[0:1, 1:2], in_=sc1[0:1, 1:2],
                                                           func=AF.Sqrt))
            p.waits("vector", [ev_sg, extra_scale_ev])
            p.run("vector", lambda h: h.tensor_scalar_max(sc1[0:1, 1:2], sc1[0:1, 1:2], 1.0))
            ev_iv = p.run("vector", lambda h: h.reciprocal(sc1[0:1, 1:2], sc1[0:1, 1:2]))
            if extra_scale is not None:
                ev_iv = p.run("vector", lambda h: h.tensor_tensor(
                    sc1[0:1, 1:2], sc1[0:1, 1:2], extra_scale, ALU.mult))
            p.waits("tensor", [ev_iv])
            ev_bc = p.run("tensor", lambda h: h.matmul(ps1[:, 0:1], ones1[:, :], sc1[0:1, 1:2],
                                                       start=True, stop=True), inc=1)
            p.waits("scalar", [ev_bc])
            ev_fin = p.run("scalar", lambda h: h.copy(out=inv_dst[:, :], in_=ps1[:, 0:1]))
            return ev_fin

        ev_eg = p.run("scalar", lambda h: h.activation(out=sc1[0:1, 0:1], in_=sc1[0:1, 0:1],
                                                       func=AF.Exp))
        ev_si = sigma_phase(w["W_in"], WinT, ing, "i", extra_scale_ev=ev_eg, extra_scale=sc1[0:1, 0:1])
        ev_so = sigma_phase(w["W_out"], WoutT, iso, "o")

        cwp = name + "cwp"
        evc = p.ev("vector")
        for r in range(24):
            p.waits("sync", [evc])
            p.dma("sync", wtmp[:, :], w["Wp"][r * 128:(r + 1) * 128, :], chan=cwp)
            p.dwait("vector", cwp)
            evc = p.run("vector", lambda h: h.tensor_copy(xnbf[:], wtmp[:, :]))
            p.waits("tensor", [evc, ev_si, ev_so])
            for dc in range(8):
                ev_tr = p.run("tensor", lambda h, dc=dc: h.transpose(
                    ptr[:, dc * 128:(dc + 1) * 128], xnbf[:, dc * 128:(dc + 1) * 128], ident[:]),
                    inc=(dc == 7))
            p.waits("vector", [ev_tr])
            evc = p.run("vector", lambda h, r=r: h.tensor_copy(
                WpT[:, :, r * 128:(r + 1) * 128],
                ptr[:, :].rearrange("p (a b) -> p a b", a=8)))
        ev_wp = evc

        ev_xuse = [ev_wp, ev_wp]
        ev_ptuse = ev_wp
        ev_tr_last = ev_tr
        for i in range(nt):
            par = i % 2
            xt = xt2[par]
            cx = name + "cxa%d" % par
            p.waits("sync", [ev_xuse[par]])
            p.dma("sync", xt[:, :], x_in[i * 128:(i + 1) * 128, :], chan=cx)
            p.dwait("scalar", cx)
            ev_sq = p.run("scalar", lambda h, xt=xt: h.square(xsq[:], xt[:]))
            p.dwait("vector", cx)
            p.run("vector", lambda h, xt=xt: h.reduce_sum(s1[:], xt[:], axis=mybir.AxisListType.X))
            p.waits("vector", [ev_sq])
            p.run("vector", lambda h: h.reduce_sum(s2[:], xsq[:], axis=mybir.AxisListType.X))
            p.run("vector", lambda h: h.tensor_scalar_mul(mean[:], s1[:], 1.0 / D))
            p.run("vector", lambda h: h.tensor_scalar_mul(s2[:], s2[:], 1.0 / D))
            p.run("vector", lambda h: h.tensor_tensor(m2[:], mean[:], mean[:], ALU.mult))
            ev_v = p.run("vector", lambda h: h.tensor_tensor(var[:], s2[:], m2[:], ALU.subtract))
            p.waits("scalar", [ev_v, cc.ev(1e-5)])
            p.run("scalar", lambda h, i=i: h.activation(out=rstd_all[:, i:i + 1], in_=var[:],
                                                        func=AF.Ln, bias=cc.ap(1e-5)))
            ev_s = p.run("scalar", lambda h, i=i: h.activation(
                out=rstd_all[:, i:i + 1], in_=rstd_all[:, i:i + 1], func=AF.Exp, scale=-0.5))
            p.waits("vector", [ev_s])
            ev_nb = p.run("vector", lambda h, i=i: h.scalar_tensor_tensor(
                nbias_all[:, i:i + 1], mean[:], -1.0, rstd_all[:, i:i + 1], ALU.mult, ALU.mult))
            p.waits("scalar", [ev_nb, ev_tr_last])
            ev_xn = p.run("scalar", lambda h, xt=xt, i=i: h.activation(
                out=xnbf[:], in_=xt[:], func=AF.Identity,
                scale=rstd_all[:, i:i + 1], bias=nbias_all[:, i:i + 1]))
            ev_xuse[par] = ev_xn
            p.waits("tensor", [ev_xn, ev_ptuse])
            for dc in range(8):
                ev_tr = p.run("tensor", lambda h, dc=dc: h.transpose(
                    ptr[:, dc * 128:(dc + 1) * 128], xnbf[:, dc * 128:(dc + 1) * 128], ident[:]),
                    inc=(dc == 7))
            ev_tr_last = ev_tr
            p.waits("scalar", [ev_tr])
            for dc in range(8):
                ev_ptuse = p.run("scalar", lambda h, dc=dc, i=i: h.activation(
                    out=xnT[:, dc, i * 128:(i + 1) * 128], in_=ptr[:, dc * 128:(dc + 1) * 128],
                    func=AF.Copy))

        ev_praw_used = None
        ev_pu_used = None
        ev_yh_used = None
        ev_yh = None
        ev_sa_prev = None
        ev_chain_v = [None, None]
        ev_epi = None
        ev_md = None
        for tbi in range(ntb):
            t0 = tbi * TB
            def g_body(g, t0, par, tx, tax, ta, tb_, tc_, tp, td, uT):
                nonlocal ev_praw_used, ev_pu_used, ev_yh, ev_sa_prev
                p.waits("tensor", [ev_praw_used, ev_ptuse, ev_wp])
                for r3 in range(3):
                    kk = r3 * 8 + g
                    for dc in range(8):
                        ev_mm = p.run("tensor", lambda h, r3=r3, kk=kk, dc=dc, t0=t0: h.matmul(
                            praw[r3], WpT[:, dc, kk * 128:(kk + 1) * 128],
                            xnT[:, dc, t0:t0 + TB],
                            start=(dc == 0), stop=(dc == 7)), inc=(dc == 7 and r3 == 2))
                p.waits("tensor", [ev_pu_used, ev_si])
                for dc in range(8):
                    ev_mu = p.run("tensor", lambda h, g=g, dc=dc, t0=t0: h.matmul(
                        pu, WinT[:, dc, g * 128:(g + 1) * 128], xnT[:, dc, t0:t0 + TB],
                        start=(dc == 0), stop=(dc == 7)), inc=(dc == 7))
                p.waits("scalar", [ev_mu])
                ev_ue = p.run("scalar", lambda h: h.activation(out=uT[:, :], in_=pu,
                                                               func=AF.Identity, scale=ing[:, :]))
                ev_pu_used = ev_ue
                p.waits("scalar", [ev_mm, ev_na])
                p.run("scalar", lambda h, g=g: h.activation(out=tx[:, :], in_=praw[0],
                                                            func=AF.Identity, bias=bpd[:, g:g + 1]))
                ev_praw_used = p.run("scalar", lambda h, g=g: h.activation(
                    out=tb_[:, :], in_=praw[1], func=AF.Tanh, bias=bpcol[:, 8 + g:9 + g]))
                ev_praw_used = p.run("scalar", lambda h, g=g: h.activation(
                    out=tc_[:, :], in_=praw[2], func=AF.Tanh, bias=bpcol[:, 16 + g:17 + g]))
                p.run("scalar", lambda h: h.activation(out=tax[:, :], in_=tx[:, :], func=AF.Abs))
                p.run("scalar", lambda h: h.activation(out=tax[:, :], in_=tax[:, :], func=AF.Exp,
                                                       scale=-1.0))
                ev_sp = p.run("scalar", lambda h: h.activation(out=tax[:, :], in_=tax[:, :],
                                                               func=AF.Ln, bias=cc.ap(1.0)))
                p.waits("vector", [ev_sp, ev_sa_prev])
                p.run("vector", lambda h: h.tensor_scalar_max(tdelta[:, :], tx[:, :], 0.0))
                ev_vd = p.run("vector", lambda h: h.tensor_tensor(tdelta[:, :], tdelta[:, :],
                                                                  tax[:, :], ALU.add))
                p.waits("scalar", [ev_vd])
                ev_sa = p.run("scalar", lambda h, g=g: h.activation(
                    out=ta[:, :], in_=tdelta[:, :], func=AF.Exp, scale=nalpha[:, g:g + 1]))
                ev_sa_prev = ev_sa
                p.waits("vector", [ev_sa])
                p.run("vector", lambda h: h.tensor_scalar_min(ta[:, :], ta[:, :], 1.0 - 1e-4))
                p.run("vector", lambda h: h.tensor_tensor(tp[:, :], ta[:, :], ta[:, :], ALU.mult))
                p.run("vector", lambda h: h.tensor_tensor(trs[:, :], tc_[:, :], tc_[:, :], ALU.mult))
                p.run("vector", lambda h: h.tensor_tensor(tp[:, :], tp[:, :], trs[:, :], ALU.add))
                p.run("vector", lambda h: h.tensor_tensor(tr_[:, :], tb_[:, :], tb_[:, :], ALU.mult))
                p.run("vector", lambda h: h.tensor_tensor(tax[:, :], ta[:, :], tb_[:, :], ALU.mult))
                p.run("vector", lambda h: h.tensor_tensor(td[:, :], tp[:, :], tr_[:, :], ALU.subtract))
                p.run("vector", lambda h: h.tensor_tensor(td[:, :], td[:, :], td[:, :], ALU.mult))
                p.run("vector", lambda h: h.scalar_tensor_tensor(tax[:, :], tax[:, :], 4.0, tax[:, :],
                                                                 ALU.mult, ALU.mult))
                ev_vq = p.run("vector", lambda h: h.tensor_tensor(td[:, :], td[:, :], tax[:, :], ALU.add))
                p.waits("scalar", [ev_vq])
                ev_sd = p.run("scalar", lambda h: h.activation(out=td[:, :], in_=td[:, :],
                                                               func=AF.Sqrt, bias=cc.ap(1e-12)))
                p.waits("vector", [ev_sd])
                p.run("vector", lambda h: h.tensor_tensor(tp[:, :], tp[:, :], tr_[:, :], ALU.add))
                ev_vl = p.run("vector", lambda h: h.tensor_tensor(tp[:, :], tp[:, :], td[:, :], ALU.add))
                p.waits("scalar", [ev_vl])
                ev_ss = p.run("scalar", lambda h: h.activation(out=tp[:, :], in_=tp[:, :],
                                                               func=AF.Sqrt, scale=0.5,
                                                               bias=cc.ap(1e-12)))
                p.waits("vector", [ev_ss, ev_ue])
                p.run("vector", lambda h: h.tensor_scalar_max(tp[:, :], tp[:, :], 1.0))
                p.run("vector", lambda h: h.reciprocal(trs[:, :], tp[:, :]))
                p.run("vector", lambda h: h.tensor_tensor(ta[:, :], ta[:, :], trs[:, :], ALU.mult))
                p.run("vector", lambda h: h.tensor_tensor(tc_[:, :], tc_[:, :], trs[:, :], ALU.mult))
                p.run("vector", lambda h: h.tensor_tensor(tb_[:, :], tb_[:, :], trs[:, :], ALU.mult))
                p.run("vector", lambda h: h.tensor_tensor(tr_[:, :], tb_[:, :], uT[:, :], ALU.mult))
                p.run("vector", lambda h, g=g: h.tensor_copy(carryprev[:, :], carry[:, g:g + 1]))
                p.run("vector", lambda h, g=g: h.tensor_tensor_scan(
                    zT[:, :], ta[:, :], tr_[:, :], carry[:, g:g + 1], ALU.mult, ALU.add))
                p.run("vector", lambda h, g=g: h.tensor_copy(carry[:, g:g + 1], zT[:, TB - 1:TB]))
                p.waits("vector", [ev_yh_used])
                p.run("vector", lambda h, g=g: h.tensor_tensor(
                    yhT[:, g, 1:TB], tc_[:, 1:TB], zT[:, 0:TB - 1], ALU.mult))
                ev_yh = p.run("vector", lambda h, g=g: h.tensor_tensor(
                    yhT[:, g, 0:1], tc_[:, 0:1], carryprev[:, :], ALU.mult))
            for g in range(8):
                par = g % 2
                p.waits("scalar", [ev_chain_v[par]])
                g_body(g, t0, par, txp[par], taxp[par], tap[par],
                       tbp_[par], tcp_[par], tpp[par], tdp[par], uTp[par])
                ev_chain_v[par] = ev_yh
            for k in range(TB // 128):
                it = tbi * (TB // 128) + k
                par = it % 2
                xt = xt2[par]
                cx = name + "cxb%d" % par
                p.waits("sync", [ev_epi, ev_xuse[par]])
                p.dma("sync", xt[:, :], x_in[it * 128:(it + 1) * 128, :], chan=cx)
                p.waits("tensor", [ev_yh, ev_md, ev_epi, ev_so])
                for g in range(8):
                    for nb2 in range(2):
                        ev_md = p.run("tensor", lambda h, g=g, nb2=nb2, k=k: h.matmul(
                            pool[:, nb2, :], yhT[:, g, k * 128:(k + 1) * 128],
                            WoutT[:, g, nb2 * 512:(nb2 + 1) * 512],
                            start=(g == 0), stop=(g == 7)), inc=(g == 7 and nb2 == 1))
                ev_yh_used = ev_md
                p.waits("scalar", [ev_md, ev_so])
                p.dwait("scalar", name + "cso")
                ev_ep = p.run("scalar", lambda h: h.activation(
                    out=epi[:, :], in_=pool[:, 0:2, :].rearrange("p a b -> p (a b)"),
                    func=AF.Identity, scale=iso[:, :]))
                p.dwait("vector", cx)
                p.dwait("vector", name + "cso")
                p.waits("vector", [ev_ep])
                ev_epi = p.run("vector", lambda h, xt=xt: h.tensor_tensor(
                    epi[:, :], epi[:, :], xt[:], ALU.add))
                ev_xuse[par] = ev_epi
                p.waits("sync", [ev_epi])
                p.dma("sync", x_out[it * 128:(it + 1) * 128, :], epi[:, :], chan=name + "cso")
        p.dwait("sync", name + "cso")

    return build_l1attna

build_l1attna = _mk_build_l1attna()


def _mk_build_l1ffn():
    from concourse.bass_core_rust import IndirectOffsetOnAxis

    T = 4096
    D = 1024
    NHE = 4
    DQ = 256
    NSUB = 128
    TOPK = 8
    KC = 16
    NE = 16384
    DL = 256
    DH = 512
    NTOK_GLOBAL = 32768.0
    NCORES = 8


    def build_l1ffn(p, t, cc, consts, x_in, x_out, h_scr, q_scr, xl_scr, cc_in, cc_out, w, name="a1f",
                    h_ready=None, skip_hall=False):
        nc = p.nc
        ident, ones1, ev_ident = consts
        nt = T // 128
        ntb = T // 512

        xnT = t.sb(name + "_xnT", [128, 8, T], BF16)
        qtmp = t.sb(name + "_qtmp", [128, 512], F32)
        qf = t.sb(name + "_qf", [128, 8, 128], F32)
        hg = t.sb(name + "_hg", [128, 32, DH], BF16)
        hg2 = t.sb(name + "_hg2", [128, 32, DH], BF16)
        wqT = hg2[:, 0:16, :].rearrange("p a b -> p (a b)").rearrange(
            "p (c q) -> p c q", c=8)
        xnf = t.sb(name + "_xnf", [128, 1024], F32)
        xlbf = hg2[:, 20:22, :].rearrange("p a b -> p (a b)")
        xlT = hg[:, 18:20, :].rearrange("p a (b c) -> p (a b) c", b=4)
        xlblk = t.sb(name + "_xlblk", [128, 8, 512], BF16)
        w2aT = t.sb(name + "_w2aT", [128, 8, 512], BF16)
        wv_sb = t.sb(name + "_wv", [128, 4, 1024], BF16)
        w1_sb = hg[:, 16:18, :]
        sk1T = t.sb(name + "_sk1T", [128, 128], F32)
        sk2T = t.sb(name + "_sk2T", [128, 128], F32)
        bqcol = t.sb(name + "_bqcol", [128, 8], F32)
        gamcol = t.sb(name + "_gamcol", [128, 8], F32)
        betcol = t.sb(name + "_betcol", [128, 8], F32)
        qsum = t.sb(name + "_qsum", [128, 16], F32)
        qsc = t.sb(name + "_qsc", [128, 8], F32)
        qbs = t.sb(name + "_qbs", [128, 8], F32)
        iota16 = t.sb(name + "_iota16", [128, 16], F32)
        rstd_all = t.sb(name + "_rstdall", [128, nt], F32)
        nbias_all = t.sb(name + "_nbiasall", [128, nt], F32)
        wqTl = hg[:, 0:16, :].rearrange("p a b -> p (a b)").rearrange("p (c q) -> p c q", c=8)
        wtmp = t.sb(name + "_wtmp", [128, 1024], F32)
        xnbf = t.sb(name + "_xnbf", [128, D], BF16)
        xt2 = [t.sb(name + "_x0", [128, D], F32), t.sb(name + "_x1", [128, D], F32)]
        xsq = t.sb(name + "_xsq", [128, D], F32)
        s1t = t.sb(name + "_s1t", [128, 1], F32)
        s2t = t.sb(name + "_s2t", [128, 1], F32)
        mean = t.sb(name + "_mean", [128, 1], F32)
        m2 = t.sb(name + "_m2", [128, 1], F32)
        var = t.sb(name + "_var", [128, 1], F32)
        qnT = t.sb(name + "_qnT", [128, 8, 128], F32)
        ss = t.sb(name + "_ss", [128, 8, 128], F32)
        ts1 = t.sb(name + "_ts1", [128, 4, 16], F32)
        ts2 = t.sb(name + "_ts2", [128, 4, 16], F32)
        ti1 = t.sb(name + "_ti1", [128, 4, 16], U32)
        ti2 = t.sb(name + "_ti2", [128, 4, 16], U32)
        ti1f = t.sb(name + "_ti1f", [128, 4, 16], F32)
        ti2f = t.sb(name + "_ti2f", [128, 4, 16], F32)
        comb = t.sb(name + "_comb", [128, 4, 256], F32)
        fs = t.sb(name + "_fs", [128, 4, 8], F32)
        tci = t.sb(name + "_tci", [128, 4, 8], U32)
        tcif = t.sb(name + "_tcif", [128, 4, 8], F32)
        tmod = t.sb(name + "_tmod", [128, 4, 8], F32)
        idiv = t.sb(name + "_idiv", [128, 4, 8], F32)
        eoh = comb[:, :, 0:128].rearrange("p h (a b) -> p h a b", a=8)
        i1v = t.sb(name + "_i1v", [128, 4, 8], F32)
        i2v = t.sb(name + "_i2v", [128, 4, 8], F32)
        probs = t.sb(name + "_probs", [128, 4, 8], F32)
        probs2 = t.sb(name + "_probs2", [128, 4, 8], F32)
        psum8 = t.sb(name + "_psum8", [128, 4, 1], F32)
        eidx = t.sb(name + "_eidx", [128, 32], I32)
        xp = t.sb(name + "_xp", [128, DH], BF16)
        xp2 = t.sb(name + "_xp2", [128, DH], BF16)
        dots = t.sb(name + "_dots", [128, 32], F32)
        act = t.sb(name + "_act", [128, 32], F32)
        dump = t.sb(name + "_dump", [128, DH], BF16)
        accb = t.sb(name + "_accb", [128, DH], F32)
        accbf = t.sb(name + "_accbf", [128, DH], BF16)
        accT = t.sb(name + "_accT", [128, 4, 128], BF16)
        lat_t = t.sb(name + "_latt", [128, 2, 128], BF16)
        hge = t.sb(name + "_hge", [128, DH], BF16)

        pool = t.ps(name + "_pool", [128, 7, 512], F32)
        ptr = t.ps(name + "_ptr", [128, 1024], BF16)
        pa = pool[:, 0, :]
        pb = pool[:, 1, :]
        psc = [pool[:, 2, :], pool[:, 3, :]]
        pxp = pool[:, 4, :]
        pfin = pool[:, 5:7, :].rearrange("p a b -> p (a b)")

        cwb = name + "cwb"

        def load_col(dst, src1d, ncol):
            for c in range(ncol):
                p.dma("sync", dst[:, c:c + 1],
                      src1d[c * 128:(c + 1) * 128].rearrange("(p o) -> p o", o=1), chan=cwb)
        load_col(bqcol, w["bq"].rearrange("h q -> (h q)"), 8)
        load_col(gamcol, w["gamma"].rearrange("h q -> (h q)"), 8)
        load_col(betcol, w["beta"].rearrange("h q -> (h q)"), 8)
        p.dwait("vector", cwb)
        p.dwait("gpsimd", cwb)
        ev_io = p.run("gpsimd", lambda h: h.iota(iota16[:, :], pattern=[[1, 16]], base=0,
                                                 channel_multiplier=0,
                                                 allow_small_or_imprecise_dtypes=True))

        cwh = name + "cwh"
        evc = p.ev("vector")
        p.waits("sync", [evc])
        p.dma("sync", wtmp[:, 0:1024].rearrange("p (c f) -> p c f", c=2),
              w["W1"].rearrange("(c p) f -> p c f", p=128), chan=cwh)
        p.dwait("vector", cwh)
        evc = p.run("vector", lambda h: h.tensor_copy(
            w1_sb[:, :, :], wtmp[:, 0:1024].rearrange("p (c f) -> p c f", c=2)))
        for c in range(4):
            p.waits("sync", [evc])
            p.dma("sync", wtmp[:, :], w["W2"][c * 128:(c + 1) * 128, D:], chan=cwh)
            p.dwait("vector", cwh)
            evc = p.run("vector", lambda h, c=c: h.tensor_copy(wv_sb[:, c, :], wtmp[:, :]))
        for c in range(4):
            p.waits("sync", [evc])
            p.dma("sync", wtmp[:, :], w["W2"][c * 128:(c + 1) * 128, 0:D], chan=cwh)
            p.dwait("vector", cwh)
            evc = p.run("vector", lambda h: h.tensor_copy(xnbf[:], wtmp[:, :]))
            p.waits("tensor", [evc])
            for dc in range(8):
                ev_tr = p.run("tensor", lambda h, dc=dc: h.transpose(
                    ptr[:, dc * 128:(dc + 1) * 128], xnbf[:, dc * 128:(dc + 1) * 128], ident[:]),
                    inc=(dc == 7))
            p.waits("vector", [ev_tr])
            evc = p.run("vector", lambda h, c=c: h.tensor_copy(
                w2aT[:, :, c * 128:(c + 1) * 128],
                ptr[:, :].rearrange("p (a b) -> p a b", a=8)))
        identf = t.handles["c_identf"]
        pq32 = pool[:, 2:4, :].rearrange("p a b -> p (a b)")
        for hh in range(NHE):
            for qc in range(2):
                p.waits("sync", [evc])
                p.dma("sync", wtmp[:, :], w["Wq"][hh, qc * 128:(qc + 1) * 128, :], chan=cwh)
                p.dwait("tensor", cwh)
                p.waits("tensor", [evc])
                for dc in range(8):
                    ev_tr = p.run("tensor", lambda h, dc=dc: h.transpose(
                        pq32[:, dc * 128:(dc + 1) * 128], wtmp[:, dc * 128:(dc + 1) * 128],
                        identf[:, :]), inc=(dc == 7))
                p.waits("vector", [ev_tr])
                col = slice(hh * 256 + qc * 128, hh * 256 + (qc + 1) * 128)
                p.run("vector", lambda h, col=col: h.tensor_copy(
                    wqT[:, :, col], pq32.rearrange("p (a b) -> p a b", a=8)))
                evc = p.run("vector", lambda h, col=col: h.tensor_tensor(
                    wqTl[:, :, col], pq32.rearrange("p (a b) -> p a b", a=8),
                    wqT[:, :, col], ALU.subtract))
        for skap, dst in ((w["sk1"], sk1T), (w["sk2"], sk2T)):
            p.waits("sync", [evc])
            p.dma("sync", wtmp[:, 0:128], skap[:, :], chan=cwh)
            p.dwait("tensor", cwh)
            p.waits("tensor", [evc])
            ev_tr = p.run("tensor", lambda h: h.transpose(pool[:, 2, 0:128], wtmp[:, 0:128],
                                                          identf[:, :]), inc=1)
            p.waits("vector", [ev_tr])
            evc = p.run("vector", lambda h, dst=dst: h.tensor_copy(dst[:, :], pool[:, 2, 0:128]))
        ev_wpre = evc

        if not skip_hall:
            ev_ge = None
            for et in range(NE // 128):
                p.waits("sync", [evc])
                p.dma("sync", wtmp[:, 0:256], w["latents"][et * 128:(et + 1) * 128, :], chan=cwh)
                p.dwait("vector", cwh)
                evc = p.run("vector", lambda h: h.tensor_copy(xnbf[:, 0:256], wtmp[:, 0:256]))
                p.waits("tensor", [evc])
                for dc in range(2):
                    ev_tr = p.run("tensor", lambda h, dc=dc: h.transpose(
                        ptr[:, dc * 128:(dc + 1) * 128], xnbf[:, dc * 128:(dc + 1) * 128],
                        ident[:]), inc=(dc == 1))
                p.waits("vector", [ev_tr])
                evc2 = p.run("vector", lambda h: h.tensor_copy(
                    lat_t[:, :, :], ptr[:, 0:256].rearrange("p (a b) -> p a b", a=2)))
                p.waits("tensor", [evc2, ev_ge])
                for k in range(2):
                    ev_hm = p.run("tensor", lambda h, k=k: h.matmul(
                        pa, lat_t[:, k, :], w1_sb[:, k, :],
                        start=(k == 0), stop=(k == 1)), inc=(k == 1))
                p.waits("scalar", [ev_hm])
                p.dwait("scalar", name + "chs")
                ev_ge = p.run("scalar", lambda h: h.activation(out=hge[:, :], in_=pa, func=AF.Gelu))
                p.waits("sync", [ev_ge])
                p.dma("sync", h_scr[et * 128:(et + 1) * 128, :], hge[:, :], chan=name + "chs")
            ev_hall = ev_ge
        else:
            ev_hall = h_ready

        ev_xuse = [ev_wpre, ev_wpre]
        ev_ptuse = ev_wpre
        ev_tr_last = ev_tr
        for i in range(nt):
            par = i % 2
            xt = xt2[par]
            cx = name + "cxa%d" % par
            p.waits("sync", [ev_xuse[par]])
            p.dma("sync", xt[:, :], x_in[i * 128:(i + 1) * 128, :], chan=cx)
            p.dwait("scalar", cx)
            ev_sq = p.run("scalar", lambda h, xt=xt: h.square(xsq[:], xt[:]))
            p.dwait("vector", cx)
            p.run("vector", lambda h, xt=xt: h.reduce_sum(s1t[:], xt[:], axis=mybir.AxisListType.X))
            p.waits("vector", [ev_sq])
            p.run("vector", lambda h: h.reduce_sum(s2t[:], xsq[:], axis=mybir.AxisListType.X))
            p.run("vector", lambda h: h.tensor_scalar_mul(mean[:], s1t[:], 1.0 / D))
            p.run("vector", lambda h: h.tensor_scalar_mul(s2t[:], s2t[:], 1.0 / D))
            p.run("vector", lambda h: h.tensor_tensor(m2[:], mean[:], mean[:], ALU.mult))
            ev_v = p.run("vector", lambda h: h.tensor_tensor(var[:], s2t[:], m2[:], ALU.subtract))
            p.waits("scalar", [ev_v, cc.ev(1e-5)])
            p.run("scalar", lambda h, i=i: h.activation(out=rstd_all[:, i:i + 1], in_=var[:],
                                                        func=AF.Ln, bias=cc.ap(1e-5)))
            ev_s = p.run("scalar", lambda h, i=i: h.activation(
                out=rstd_all[:, i:i + 1], in_=rstd_all[:, i:i + 1], func=AF.Exp, scale=-0.5))
            p.waits("vector", [ev_s])
            ev_nb = p.run("vector", lambda h, i=i: h.scalar_tensor_tensor(
                nbias_all[:, i:i + 1], mean[:], -1.0, rstd_all[:, i:i + 1], ALU.mult, ALU.mult))
            p.waits("scalar", [ev_nb, ev_tr_last])
            ev_xn = p.run("scalar", lambda h, xt=xt, i=i: h.activation(
                out=xnf[:], in_=xt[:], func=AF.Identity,
                scale=rstd_all[:, i:i + 1], bias=nbias_all[:, i:i + 1]))
            ev_xuse[par] = ev_xn
            p.waits("vector", [ev_xn])
            ev_xh = p.run("vector", lambda h: h.tensor_copy(xnbf[:], xnf[:]))
            ev_xl = p.run("vector", lambda h: h.tensor_tensor(xlbf[:], xnf[:], xnbf[:], ALU.subtract))
            p.waits("tensor", [ev_xh, ev_ptuse])
            for dc in range(8):
                ev_tr = p.run("tensor", lambda h, dc=dc: h.transpose(
                    ptr[:, dc * 128:(dc + 1) * 128], xnbf[:, dc * 128:(dc + 1) * 128], ident[:]),
                    inc=(dc == 7))
            p.waits("scalar", [ev_tr])
            for dc in range(8):
                ev_ptuse = p.run("scalar", lambda h, dc=dc, i=i: h.activation(
                    out=xnT[:, dc, i * 128:(i + 1) * 128], in_=ptr[:, dc * 128:(dc + 1) * 128],
                    func=AF.Copy))
            p.waits("tensor", [ev_xl, ev_ptuse])
            for dc in range(8):
                ev_tr = p.run("tensor", lambda h, dc=dc: h.transpose(
                    ptr[:, dc * 128:(dc + 1) * 128], xlbf[:, dc * 128:(dc + 1) * 128], ident[:]),
                    inc=(dc == 7))
            ev_tr_last = ev_tr
            p.dwait("scalar", name + "cxl")
            p.waits("scalar", [ev_tr])
            for dc in range(8):
                ev_ptuse = p.run("scalar", lambda h, dc=dc: h.activation(
                    out=xlT[:, dc, :], in_=ptr[:, dc * 128:(dc + 1) * 128], func=AF.Copy))
            p.waits("sync", [ev_ptuse])
            for c2 in range(8):
                p.dma("sync", xl_scr[c2, :, i * 128:(i + 1) * 128], xlT[:, c2, :],
                      chan=name + "cxl")

        ev_z = p.run("vector", lambda h: h.memset(qsum[:, :], 0.0))
        ev_qe = None
        ev_xlb = None
        for tbi in range(ntb):
            p.waits("sync", [ev_xlb])
            p.dwait("sync", name + "cxl")
            for c2 in range(8):
                p.dma("sync", xlblk[:, c2, :], xl_scr[c2, :, tbi * 512:(tbi + 1) * 512],
                      chan=name + "cxb")
            for c in range(8):
                pp = psc[c % 2]
                p.waits("tensor", [ev_ptuse, ev_qe])
                p.dwait("tensor", name + "cxb")
                for dc in range(8):
                    p.run("tensor", lambda h, c=c, tbi=tbi, dc=dc, pp=pp: h.matmul(
                        pp, wqT[:, dc, c * 128:(c + 1) * 128],
                        xnT[:, dc, tbi * 512:(tbi + 1) * 512],
                        start=(dc == 0), stop=False))
                for dc in range(8):
                    p.run("tensor", lambda h, c=c, dc=dc, pp=pp: h.matmul(
                        pp, wqT[:, dc, c * 128:(c + 1) * 128], xlblk[:, dc, :],
                        start=False, stop=False))
                for dc in range(8):
                    ev_qm = p.run("tensor", lambda h, c=c, tbi=tbi, dc=dc, pp=pp: h.matmul(
                        pp, wqTl[:, dc, c * 128:(c + 1) * 128],
                        xnT[:, dc, tbi * 512:(tbi + 1) * 512],
                        start=False, stop=(dc == 7)), inc=(dc == 7))
                ev_xlb = ev_qm
                p.waits("scalar", [ev_qm, ev_z])
                p.dwait("scalar", name + "cqs")
                ev_qe = p.run("scalar", lambda h, c=c, tbi=tbi, pp=pp: h.activation(
                    out=qtmp[:, :], in_=pp, func=AF.Identity, bias=bqcol[:, c:c + 1]))
                p.waits("sync", [ev_qe])
                p.dma("sync", q_scr[c, :, tbi * 512:(tbi + 1) * 512], qtmp[:, :],
                      chan=name + "cqs")
                p.waits("vector", [ev_qe, ev_z])
                p.run("vector", lambda h: h.reduce_sum(s1t[:], qtmp[:, :], axis=mybir.AxisListType.X))
                p.run("vector", lambda h, c=c: h.tensor_tensor(
                    qsum[:, c:c + 1], qsum[:, c:c + 1], s1t[:], ALU.add))
                p.waits("scalar", [ev_z])
                ev_sqs = p.run("scalar", lambda h: h.square(xsq[:, 0:512], qtmp[:, :]))
                p.waits("vector", [ev_sqs])
                p.run("vector", lambda h: h.reduce_sum(s2t[:], xsq[:, 0:512], axis=mybir.AxisListType.X))
                ev_z = p.run("vector", lambda h, c=c: h.tensor_tensor(
                    qsum[:, 8 + c:9 + c], qsum[:, 8 + c:9 + c], s2t[:], ALU.add))

        p.waits("gpsimd", [ev_z])
        ev_st = p.cinc("gpsimd", lambda h: h.dma_start(out=cc_in[:, :], in_=qsum[:, :]),
                       name + "ccs")
        p.wait("gpsimd", ev_st)
        ev_cc = p.cinc("gpsimd", lambda h: h.collective_compute(
            "AllReduce", ALU.add, replica_groups=[list(range(NCORES))],
            ins=[cc_in.opt()], outs=[cc_out.opt()]), name + "ccol", inc=1)
        p.wait("gpsimd", ev_cc)
        ev_ld = p.cinc("gpsimd", lambda h: h.dma_start(out=qsum[:, :], in_=cc_out[:, :]),
                       name + "ccl")
        p.waits("vector", [ev_ld])
        p.run("vector", lambda h: h.tensor_scalar_mul(qsum[:, 0:8], qsum[:, 0:8], 1.0 / NTOK_GLOBAL))
        ev_vv = p.run("vector", lambda h: h.tensor_scalar_mul(qsum[:, 8:16], qsum[:, 8:16], 1.0 / NTOK_GLOBAL))
        p.run("vector", lambda h: h.tensor_tensor(qsc[:, :], qsum[:, 0:8], qsum[:, 0:8], ALU.mult))
        ev_vv = p.run("vector", lambda h: h.tensor_tensor(qsc[:, :], qsum[:, 8:16], qsc[:, :], ALU.subtract))
        p.waits("scalar", [ev_vv, cc.ev(1e-5)])
        p.run("scalar", lambda h: h.activation(out=qbs[:, :], in_=qsc[:, :], func=AF.Ln, bias=cc.ap(1e-5)))
        ev_ss2 = p.run("scalar", lambda h: h.activation(out=qbs[:, :], in_=qbs[:, :], func=AF.Exp, scale=-0.5))
        p.waits("vector", [ev_ss2])
        p.run("vector", lambda h: h.tensor_tensor(qsc[:, :], qbs[:, :], gamcol[:, :], ALU.mult))
        p.run("vector", lambda h: h.tensor_tensor(qbs[:, :], qsum[:, 0:8], qsc[:, :], ALU.mult))
        ev_bn = p.run("vector", lambda h: h.tensor_tensor(qbs[:, :], betcol[:, :], qbs[:, :], ALU.subtract))

        ev_qnT_use = None
        ev_ss_use = None
        ev_epi = None
        ev_fm = None
        ev_ei = None
        ev_hg_use_par = [None, None]
        ev_xp_par = [None, None]
        ev_pr_par = [None, None]
        hgb = [hg, hg2]
        xpb = [xp, xp2]
        probsb = [probs, probs2]

        def partA(i):
            nonlocal ev_qnT_use, ev_ss_use, ev_ei
            par = i % 2
            xt = xt2[par]
            cx = name + "cxb%d" % par
            probs = probsb[par]
            p.waits("sync", [ev_epi, ev_xuse[par]])
            p.dma("sync", xt[:, :], x_in[i * 128:(i + 1) * 128, :], chan=cx)
            cq = name + "cq%d" % (i % 2)
            p.waits("sync", [ev_qnT_use])
            p.dwait("sync", name + "cqs")
            for c in range(8):
                p.dma("sync", qf[:, c, :], q_scr[c, :, i * 128:(i + 1) * 128], chan=cq)
            p.dwait("scalar", cq)
            p.waits("scalar", [ev_bn, ev_qnT_use])
            for c in range(8):
                ev_qn = p.run("scalar", lambda h, c=c: h.activation(
                    out=qnT[:, c, :], in_=qf[:, c, :], func=AF.Identity,
                    scale=qsc[:, c:c + 1], bias=qbs[:, c:c + 1]))
            p.waits("tensor", [ev_qn])
            for hh in range(NHE):
                for side in range(2):
                    p.waits("tensor", [ev_ss_use])
                    ev_sm = p.run("tensor", lambda h, hh=hh, side=side: h.matmul(
                        pool[:, 2 + side, 0:128],
                        qnT[:, 2 * hh + side, :], sk1T[:, :] if side == 0 else sk2T[:, :],
                        start=True, stop=True), inc=1)
                    p.waits("vector", [ev_sm])
                    ev_ss_use = p.run("vector", lambda h, hh=hh, side=side: h.tensor_copy(
                        ss[:, (0 if side == 0 else 4) + hh, :], pool[:, 2 + side, 0:128]))
            ev_qnT_use = ev_sm
            for side, tsv, tiv, tivf in ((0, ts1, ti1, ti1f), (1, ts2, ti2, ti2f)):
                for hh in range(NHE):
                    src = ss[:, (0 if side == 0 else 4) + hh, :]
                    p.run("vector", lambda h, src=src, tsv=tsv, hh=hh: h.max(
                        out=tsv[:, hh, 0:8], in_=src))
                    p.run("vector", lambda h, src=src, tsv=tsv, hh=hh: h.match_replace(
                        out=xsq[:, 0:128], in_to_replace=tsv[:, hh, 0:8], in_values=src,
                        imm_value=-1e30))
                    p.run("vector", lambda h, tsv=tsv, hh=hh: h.max(
                        out=tsv[:, hh, 8:16], in_=xsq[:, 0:128]))
                    p.run("vector", lambda h, src=src, tsv=tsv, tiv=tiv, hh=hh: h.max_index(
                        out=tiv[:, hh, 0:8], in_max=tsv[:, hh, 0:8], in_values=src))
                    ev_tk = p.run("vector", lambda h, src=src, tsv=tsv, tiv=tiv, hh=hh: h.max_index(
                        out=tiv[:, hh, 8:16], in_max=tsv[:, hh, 8:16], in_values=src))
            for hh in range(NHE):
                p.run("vector", lambda h, hh=hh: h.tensor_tensor(
                    comb[:, hh, :].rearrange("p (a b) -> p a b", a=16),
                    ts1[:, hh, :].to_broadcast([128, 16, 16]),
                    ts2[:, hh, :].rearrange("p (o k) -> p o k", o=1).to_broadcast([128, 16, 16]),
                    ALU.add))
                p.run("vector", lambda h, hh=hh: h.max(out=fs[:, hh, :], in_=comb[:, hh, :]))
                ev_tk = p.run("vector", lambda h, hh=hh: h.max_index(
                    out=tci[:, hh, :], in_max=fs[:, hh, :], in_values=comb[:, hh, :]))
            p.waits("scalar", [ev_tk, ev_pr_par[par]])
            ev_ex = p.run("scalar", lambda h, probs=probs: h.activation(
                out=probs[:, :, :], in_=fs[:, :, :], func=AF.Exp))
            p.waits("vector", [ev_ex])
            p.run("vector", lambda h: h.tensor_tensor(eoh[:, :, 0:4, 0], probs[:, :, 0:4],
                                                      probs[:, :, 4:8], ALU.add))
            p.run("vector", lambda h: h.tensor_tensor(eoh[:, :, 0:2, 0], eoh[:, :, 0:2, 0],
                                                      eoh[:, :, 2:4, 0], ALU.add))
            p.run("vector", lambda h: h.tensor_tensor(psum8[:, :, 0], eoh[:, :, 0:1, 0],
                                                      eoh[:, :, 1:2, 0], ALU.add))
            p.run("vector", lambda h: h.reciprocal(psum8[:, :, :], psum8[:, :, :]))
            p.run("vector", lambda h, probs=probs: h.tensor_tensor(
                probs[:, :, :], probs[:, :, :],
                psum8[:, :, 0:1].to_broadcast([128, 4, 8]), ALU.mult))
            p.dwait("vector", name + "cg%d" % (1 - par))
            p.run("vector", lambda h: h.tensor_copy(tcif[:, :, :], tci[:, :, :]))
            p.run("vector", lambda h: h.tensor_copy(ti1f[:, :, :], ti1[:, :, :]))
            p.run("vector", lambda h: h.tensor_copy(ti2f[:, :, :], ti2[:, :, :]))
            MAGIC = 12582912.0
            p.run("vector", lambda h: h.tensor_scalar(idiv[:, :, :], tcif[:, :, :], -7.5, 1.0 / 16.0,
                                                      ALU.add, ALU.mult))
            p.run("vector", lambda h: h.tensor_scalar(idiv[:, :, :], idiv[:, :, :], MAGIC, MAGIC,
                                                      ALU.add, ALU.subtract))
            p.run("vector", lambda h: h.scalar_tensor_tensor(tmod[:, :, :], idiv[:, :, :], -16.0,
                                                             tcif[:, :, :], ALU.mult, ALU.add))
            for (idxv, tivf, outv) in ((idiv, ti1f, i1v), (tmod, ti2f, i2v)):
                p.run("vector", lambda h, idxv=idxv: h.tensor_tensor(
                    eoh[:, :, :, :],
                    idxv[:, :, :].to_broadcast([128, 4, 8, 16]),
                    iota16[:, :].rearrange("p (a b k) -> p a b k", a=1, b=1).to_broadcast([128, 4, 8, 16]),
                    ALU.is_equal))
                p.run("vector", lambda h, tivf=tivf: h.tensor_tensor(
                    eoh[:, :, :, :], eoh[:, :, :, :],
                    tivf[:, :, :].rearrange("p h (o k) -> p h o k", o=1).to_broadcast([128, 4, 8, 16]),
                    ALU.mult))
                p.run("vector", lambda h: h.tensor_tensor(eoh[:, :, :, 0:8], eoh[:, :, :, 0:8],
                                                          eoh[:, :, :, 8:16], ALU.add))
                p.run("vector", lambda h: h.tensor_tensor(eoh[:, :, :, 0:4], eoh[:, :, :, 0:4],
                                                          eoh[:, :, :, 4:8], ALU.add))
                p.run("vector", lambda h: h.tensor_tensor(eoh[:, :, :, 0:2], eoh[:, :, :, 0:2],
                                                          eoh[:, :, :, 2:4], ALU.add))
                p.run("vector", lambda h, outv=outv: h.tensor_tensor(
                    outv[:, :, :], eoh[:, :, :, 0], eoh[:, :, :, 1], ALU.add))
            p.run("vector", lambda h: h.scalar_tensor_tensor(
                i1v[:, :, :], i1v[:, :, :], 128.0, i2v[:, :, :], ALU.mult, ALU.add))
            ev_ei = p.run("vector", lambda h: h.tensor_copy(
                eidx[:, :], i1v[:, :, :].rearrange("p h k -> p (h k)")))
            p.dwait("gpsimd", name + "chs")
            p.waits("gpsimd", [ev_ei, ev_hg_use_par[par], ev_hall])
            for k in range(32):
                p.cinc("gpsimd", lambda h, k=k, par=par: h.indirect_dma_start(
                    out=hgb[par][:, k, :], out_offset=None,
                    in_=h_scr.rearrange("(p g) f -> p g f", p=128),
                    in_offset=IndirectOffsetOnAxis(eidx[:, k:k + 1], 1)), name + "cg%d" % par)
            p.waits("tensor", [ev_xp_par[par], ev_ptuse])
            for dc in range(8):
                ev_xm = p.run("tensor", lambda h, dc=dc, i=i: h.matmul(
                    pxp, xnT[:, dc, i * 128:(i + 1) * 128], w2aT[:, dc, :],
                    start=(dc == 0), stop=(dc == 7)), inc=(dc == 7))
            p.waits("vector", [ev_xm])
            p.run("vector", lambda h, par=par: h.tensor_copy(xpb[par][:, :], pxp))

        def partB(i):
            nonlocal ev_epi, ev_fm
            par = i % 2
            xt = xt2[par]
            cx = name + "cxb%d" % par
            hg = hgb[par]
            probs = probsb[par]
            p.dwait("vector", name + "cg%d" % par)
            for k in range(32):
                ev_dot = p.run("vector", lambda h, k=k, hg=hg, par=par: h.scalar_tensor_tensor(
                    dump[:, :], hg[:, k, :], 1.0, xpb[par][:, :], ALU.bypass, ALU.mult,
                    accum_out=dots[:, k:k + 1]))
            ev_xp_par[par] = ev_dot
            p.waits("scalar", [ev_dot])
            ev_gl = p.run("scalar", lambda h: h.activation(out=act[:, :], in_=dots[:, :], func=AF.Gelu))
            p.waits("vector", [ev_gl])
            ev_act = p.run("vector", lambda h, probs=probs: h.tensor_tensor(
                act[:, :], act[:, :], probs[:, :, :].rearrange("p h k -> p (h k)"), ALU.mult))
            ev_pr_par[par] = ev_act
            ev_acc = p.run("vector", lambda h, hg=hg: h.tensor_scalar(
                accb[:, :], hg[:, 0, :], act[:, 0:1], None, ALU.mult))
            for k in range(1, 32):
                ev_acc = p.run("vector", lambda h, k=k, hg=hg: h.scalar_tensor_tensor(
                    accb[:, :], hg[:, k, :], act[:, k:k + 1], accb[:, :], ALU.mult, ALU.add))
            ev_hg_use_par[par] = ev_acc
            ev_accbf = p.run("vector", lambda h: h.tensor_copy(accbf[:, :], accb[:, :]))
            p.waits("tensor", [ev_accbf, ev_fm])
            for dc in range(4):
                ev_tra = p.run("tensor", lambda h, dc=dc: h.transpose(
                    ptr[:, dc * 128:(dc + 1) * 128], accbf[:, dc * 128:(dc + 1) * 128], ident[:]),
                    inc=(dc == 3))
            p.waits("vector", [ev_tra])
            ev_accT = p.run("vector", lambda h: h.tensor_copy(
                accT[:, :, :], ptr[:, 0:512].rearrange("p (a b) -> p a b", a=4)))
            p.waits("tensor", [ev_accT, ev_epi])
            for fc in range(4):
                for nb2 in range(2):
                    ev_fm = p.run("tensor", lambda h, fc=fc, nb2=nb2: h.matmul(
                        pfin[:, nb2 * 512:(nb2 + 1) * 512], accT[:, fc, :],
                        wv_sb[:, fc, nb2 * 512:(nb2 + 1) * 512],
                        start=(fc == 0), stop=(fc == 3)), inc=(fc == 3 and nb2 == 1))
            p.waits("scalar", [ev_fm])
            p.dwait("scalar", name + "cso")
            ev_ep = p.run("scalar", lambda h: h.activation(out=xnf[:, :], in_=pfin,
                                                           func=AF.Identity, scale=0.25))
            p.dwait("vector", cx)
            p.waits("vector", [ev_ep])
            ev_epi = p.run("vector", lambda h, xt=xt: h.tensor_tensor(
                xnf[:, :], xnf[:, :], xt[:], ALU.add))
            ev_xuse[par] = ev_epi
            p.waits("sync", [ev_epi])
            p.dma("sync", x_out[i * 128:(i + 1) * 128, :], xnf[:, :], chan=name + "cso")

        for i in range(nt + 1):
            if i < nt:
                partA(i)
            if i >= 1:
                partB(i - 1)
        p.dwait("sync", name + "cso")
        return ev_hall

    return build_l1ffn

build_l1ffn = _mk_build_l1ffn()


def _mk_build_l1attnb():

    T = 4096
    D = 1024
    NH = 8
    HD = 64
    SEG = 1024
    C = 32
    CLAMP = 35.0
    MAG = 12582912.0


    def build_l1attnb(p, t, cc, consts, x_in, x_out, w, name="a1b"):
        STOP = 0
        nc = p.nc
        ident, ones1, ev_ident = consts
        identf = t.handles["c_identf"]
        iota_r = t.handles["c_iota_r"]
        iota_c = t.handles["c_iota_c"]
        nt = T // 128
        nseg = T // SEG
        ntt = SEG // 128
        nck = SEG // C

        xnT = t.sb(name + "_xnT", [128, 8, SEG], BF16)
        wsel = t.sb(name + "_wsel", [128, 8, 160], BF16)
        dtw = t.sb(name + "_dtw", [32, 512], BF16)
        wk4 = t.sb(name + "_wk4", [128, 8, 2048], BF16)
        wo_sb = t.sb(name + "_wo", [128, 4, 1024], BF16)
        dbrep = t.sb(name + "_dbrep", [128, 512], F32)
        bdrep = t.sb(name + "_bdrep", [128, 512], F32)
        tfrep = t.sb(name + "_tfrep", [128, 512], F32)
        rstd_all = t.sb(name + "_rstdall", [128, nt], F32)
        nbias_all = t.sb(name + "_nbiasall", [128, nt], F32)
        TRI4 = t.sb(name + "_TRI4", [128, 32], F32)
        ONES32 = t.sb(name + "_ONES32", [128, 32], F32)
        mask2r = t.sb(name + "_mask2r", [128, 32], F32)
        IDR32 = t.sb(name + "_IDR32", [128, 32], F32)
        S = t.sb(name + "_S", [128, 4, HD], F32)
        kT = t.sb(name + "_kT", [128, 4, SEG], BF16)
        knat = t.sb(name + "_knat", [128, ntt, 512], BF16)
        vnat = t.sb(name + "_vnat", [128, ntt, 512], BF16)
        bvnat = t.sb(name + "_bvnat", [128, ntt, 512], BF16)
        rsig = t.sb(name + "_rsig", [128, ntt, 512], BF16)
        gsig = t.sb(name + "_gsig", [128, ntt, 512], BF16)
        ysn = t.sb(name + "_ysn", [128, ntt, 512], BF16)
        ysT = t.sb(name + "_ysT", [128, 4, SEG], BF16)
        logdn = t.sb(name + "_logdn", [128, ntt, 512], F32)
        dtrawT = t.sb(name + "_dtrawT", [32, SEG], BF16)
        bcol = t.sb(name + "_bcol", [128, ntt], F32)
        Lcm = t.sb(name + "_Lcm", [128, 4, C], F32)
        Ltok = t.sb(name + "_Ltok", [128, 512], F32)
        Qt = t.sb(name + "_Qt", [128, 4, C], F32)
        Q2 = t.sb(name + "_Q2", [128, 4, C], BF16)
        K2 = t.sb(name + "_K2", [128, 4, C], BF16)
        Q3 = t.sb(name + "_Q3", [128, 4, 16], BF16)
        K3 = t.sb(name + "_K3", [128, 4, 16], BF16)
        khat = t.sb(name + "_khat", [128, 512], BF16)
        Asb = t.sb(name + "_Asb", [128, 8, C], BF16)
        Dend = t.sb(name + "_Dend", [128, 4], F32)
        e2 = t.sb(name + "_e2", [128, 4, C], F32)
        e3 = t.sb(name + "_e3", [128, 4, C], F32)
        st1 = t.sb(name + "_st1", [128, 512], F32)
        wtmp = t.sb(name + "_wtmp", [128, 2048], F32)
        xnbf = t.sb(name + "_xnbf", [128, D], BF16)
        xt2 = [t.sb(name + "_x0", [128, D], F32), t.sb(name + "_x1", [128, D], F32)]
        xsq = t.sb(name + "_xsq", [128, D], F32)
        s1 = t.sb(name + "_s1", [128, 1], F32)
        s2 = t.sb(name + "_s2", [128, 1], F32)
        mean = t.sb(name + "_mean", [128, 1], F32)
        m2 = t.sb(name + "_m2", [128, 1], F32)
        var = t.sb(name + "_var", [128, 1], F32)

        pool = t.ps(name + "_pool", [128, 7, 512], F32)
        ptr = t.ps(name + "_ptr", [128, 1024], BF16)
        pW = pool[:, 0, :]
        pU = pool[:, 1, :]
        pA = pool[:, 2, :]
        pL = pool[:, 3, :]
        pS2 = pool[:, 4, :]
        pTr = pool[:, 5, :]
        pP = pool[:, 6, :]
        pP2 = pool[:, 5, :]

        cwb = name + "cwb"

        rowtmp = wtmp[0:1, 0:512]
        ev_rd = None
        for idx, (src, dst) in enumerate(((w["dt_b"], dbrep),
                                          (w["base_decay"].rearrange("h k -> (h k)"), bdrep),
                                          (w["time_first"].rearrange("h k -> (h k)"), tfrep))):
            ch = cwb + str(idx)
            p.waits("sync", [ev_rd])
            p.dma("sync", rowtmp, src.rearrange("(o k) -> o k", o=1), chan=ch)
            p.dwait("tensor", ch)
            p.waits("tensor", [ev_ident, ev_rd])
            ev_b = p.run("tensor", lambda h: h.matmul(pP, ones1[:, :], rowtmp,
                                                      start=True, stop=True), inc=1)
            p.waits("vector", [ev_b])
            ev_rd = p.run("vector", lambda h, dst=dst: h.tensor_copy(dst[:, :], pP))

        ev_g1 = p.run("gpsimd", lambda h: h.memset(ONES32[:, :], 1.0))
        ev_sz = p.run("gpsimd", lambda h: h.memset(S[:, :, :], 0.0))
        p.waits("vector", [ev_g1, ev_rd])
        p.run("vector", lambda h: h.tensor_scalar(st1[:, 0:1], iota_r[:, 0:1], -15.5, 1.0 / 32.0,
                                                  ALU.add, ALU.mult))
        p.run("vector", lambda h: h.tensor_scalar(st1[:, 0:1], st1[:, 0:1], MAG, MAG,
                                                  ALU.add, ALU.subtract))
        p.run("vector", lambda h: h.scalar_tensor_tensor(st1[:, 1:2], st1[:, 0:1], -32.0,
                                                         iota_r[:, 0:1], ALU.mult, ALU.add))
        ev_tri = p.run("vector", lambda h: h.tensor_tensor(
            TRI4[:, :], st1[:, 1:2].to_broadcast([128, 32]), iota_c[:, 0:32], ALU.is_le))
        p.run("vector", lambda h: h.tensor_scalar(st1[:, 2:3], st1[:, 1:2], -7.5, 1.0 / 16.0,
                                                  ALU.add, ALU.mult))
        p.run("vector", lambda h: h.tensor_scalar(st1[:, 2:3], st1[:, 2:3], MAG, MAG,
                                                  ALU.add, ALU.subtract))
        p.run("vector", lambda h: h.tensor_scalar(st1[:, 3:35], iota_c[:, 0:32], -7.5, 1.0 / 16.0,
                                                  ALU.add, ALU.mult))
        p.run("vector", lambda h: h.tensor_scalar(st1[:, 3:35], st1[:, 3:35], MAG, MAG,
                                                  ALU.add, ALU.subtract))
        p.run("vector", lambda h: h.tensor_tensor(
            st1[:, 3:35], st1[:, 3:35], st1[:, 2:3].to_broadcast([128, 32]), ALU.is_equal))
        p.run("vector", lambda h: h.tensor_tensor(
            mask2r[:, :], st1[:, 1:2].to_broadcast([128, 32]), iota_c[:, 0:32], ALU.is_le))
        p.run("vector", lambda h: h.tensor_tensor(
            mask2r[:, :], mask2r[:, :], st1[:, 3:35], ALU.mult))
        ev_msk = p.run("vector", lambda h: h.tensor_tensor(
            IDR32[:, :], st1[:, 1:2].to_broadcast([128, 32]), iota_c[:, 0:32], ALU.is_equal))

        cwh = name + "cwh"
        evc = p.ev("vector")
        p.waits("sync", [evc])
        p.dma("sync", wtmp[0:32, 0:512], w["dt_W"][:, :], chan=cwh)
        p.dwait("vector", cwh)
        evc = p.run("vector", lambda h: h.tensor_copy(dtw[:, :], wtmp[0:32, 0:512]))
        for c in range(8):
            p.waits("sync", [evc])
            p.dma("sync", wtmp[:, 0:160], w["W_sel"][c * 128:(c + 1) * 128, :], chan=cwh)
            p.dwait("vector", cwh)
            evc = p.run("vector", lambda h, c=c: h.tensor_copy(wsel[:, c, :], wtmp[:, 0:160]))
        for c in range(8):
            for j, wn in enumerate(("Wk", "Wv", "Wr", "Wg")):
                p.waits("sync", [evc])
                p.dma("sync", wtmp[:, j * 512:(j + 1) * 512], w[wn][c * 128:(c + 1) * 128, :],
                      chan=cwh)
                p.dwait("vector", cwh)
                evc = p.run("vector", lambda h, c=c, j=j: h.tensor_copy(
                    wk4[:, c, j * 512:(j + 1) * 512], wtmp[:, j * 512:(j + 1) * 512]))
        for c in range(4):
            p.waits("sync", [evc])
            p.dma("sync", wtmp[:, 0:1024], w["Wo"][c * 128:(c + 1) * 128, :], chan=cwh)
            p.dwait("vector", cwh)
            evc = p.run("vector", lambda h, c=c: h.tensor_copy(wo_sb[:, c, :], wtmp[:, 0:1024]))
        ev_wpre = evc

        ev_xuse = [ev_wpre, ev_wpre]
        ev_ptuse = ev_wpre
        ev_tr_last = None
        ev_pP = None
        ev_pP2 = ev_msk
        ev_ys_ev = None
        ev_S_upd = ev_sz
        ev_U_mm = None
        ev_intra = None
        ev_Aev = ev_msk
        ev_Lev = None
        ev_kh = None
        ev_fac = None
        ev_Lcm_ev = None
        ev_epi = None
        ev_md = None
        ev_ysT_ev = None
        for sg in range(nseg):
            tg0 = sg * SEG
            p.waits("tensor", [ev_ys_ev, ev_Aev, ev_kh, ev_fac, ev_S_upd, ev_md, ev_epi])
            for tt0 in range(ntt):
                i = tg0 // 128 + tt0
                par = i % 2
                xt = xt2[par]
                cx = name + "cxa%d" % par
                p.waits("sync", [ev_xuse[par]])
                p.dma("sync", xt[:, :], x_in[i * 128:(i + 1) * 128, :], chan=cx)
                p.dwait("scalar", cx)
                ev_sq = p.run("scalar", lambda h, xt=xt: h.square(xsq[:], xt[:]))
                p.dwait("vector", cx)
                p.run("vector", lambda h, xt=xt: h.reduce_sum(s1[:], xt[:], axis=mybir.AxisListType.X))
                p.waits("vector", [ev_sq])
                p.run("vector", lambda h: h.reduce_sum(s2[:], xsq[:], axis=mybir.AxisListType.X))
                p.run("vector", lambda h: h.tensor_scalar_mul(mean[:], s1[:], 1.0 / D))
                p.run("vector", lambda h: h.tensor_scalar_mul(s2[:], s2[:], 1.0 / D))
                p.run("vector", lambda h: h.tensor_tensor(m2[:], mean[:], mean[:], ALU.mult))
                ev_v = p.run("vector", lambda h: h.tensor_tensor(var[:], s2[:], m2[:], ALU.subtract))
                p.waits("scalar", [ev_v, cc.ev(1e-5)])
                p.run("scalar", lambda h, i=i: h.activation(out=rstd_all[:, i:i + 1], in_=var[:],
                                                            func=AF.Ln, bias=cc.ap(1e-5)))
                ev_s = p.run("scalar", lambda h, i=i: h.activation(
                    out=rstd_all[:, i:i + 1], in_=rstd_all[:, i:i + 1], func=AF.Exp, scale=-0.5))
                p.waits("vector", [ev_s])
                ev_nb = p.run("vector", lambda h, i=i: h.scalar_tensor_tensor(
                    nbias_all[:, i:i + 1], mean[:], -1.0, rstd_all[:, i:i + 1], ALU.mult, ALU.mult))
                p.waits("scalar", [ev_nb, ev_tr_last])
                ev_xn = p.run("scalar", lambda h, xt=xt, i=i: h.activation(
                    out=xnbf[:], in_=xt[:], func=AF.Identity,
                    scale=rstd_all[:, i:i + 1], bias=nbias_all[:, i:i + 1]))
                ev_xuse[par] = ev_xn
                p.waits("tensor", [ev_xn, ev_ptuse])
                for dc in range(8):
                    ev_tr = p.run("tensor", lambda h, dc=dc: h.transpose(
                        ptr[:, dc * 128:(dc + 1) * 128], xnbf[:, dc * 128:(dc + 1) * 128], ident[:]),
                        inc=(dc == 7))
                ev_tr_last = ev_tr
                p.waits("scalar", [ev_tr])
                for dc in range(8):
                    ev_ptuse = p.run("scalar", lambda h, dc=dc, tt0=tt0: h.activation(
                        out=xnT[:, dc, tt0 * 128:(tt0 + 1) * 128], in_=ptr[:, dc * 128:(dc + 1) * 128],
                        func=AF.Copy))


            for g in range(4):
                for tb2 in range(SEG // 512):
                    p.waits("tensor", [ev_pP, ev_ptuse])
                    for dc in range(8):
                        ev_m = p.run("tensor", lambda h, g=g, tb2=tb2, dc=dc, tg0=tg0: h.matmul(
                            pP, wk4[:, dc, g * 128:(g + 1) * 128],
                            xnT[:, dc, tb2 * 512:(tb2 + 1) * 512],
                            start=(dc == 0), stop=(dc == 7)), inc=(dc == 7))
                    p.waits("scalar", [ev_m])
                    ev_pP = p.run("scalar", lambda h, g=g, tb2=tb2: h.activation(
                        out=kT[:, g, tb2 * 512:(tb2 + 1) * 512], in_=pP, func=AF.Copy))
            for tt in range(ntt):
                it = tg0 // 128 + tt
                p.waits("tensor", [ev_pP, ev_ptuse])
                for dc in range(8):
                    ev_m = p.run("tensor", lambda h, dc=dc, tt=tt: h.matmul(
                        pP[:, 0:64], xnT[:, dc, tt * 128:(tt + 1) * 128], wsel[:, dc, 32:96],
                        start=(dc == 0), stop=(dc == 7)), inc=(dc == 7))
                p.waits("vector", [ev_m])
                p.run("vector", lambda h: h.reduce_sum(s1[:], pP[:, 0:64], axis=mybir.AxisListType.X))
                ev_pP = p.run("vector", lambda h, tt=tt: h.tensor_scalar_mul(
                    bcol[:, tt:tt + 1], s1[:], 1.0 / 64.0))
                p.waits("tensor", [ev_pP])
                for dc in range(8):
                    ev_m = p.run("tensor", lambda h, dc=dc, tt=tt: h.matmul(
                        pP[0:32, 0:128], wsel[:, dc, 0:32], xnT[:, dc, tt * 128:(tt + 1) * 128],
                        start=(dc == 0), stop=(dc == 7)), inc=(dc == 7))
                p.waits("scalar", [ev_m])
                ev_pP = p.run("scalar", lambda h, tt=tt: h.copy(
                    out=dtrawT[:, tt * 128:(tt + 1) * 128], in_=pP[0:32, 0:128]))
                for j in range(4):
                    p.waits("tensor", [ev_pP, ev_pP2])
                    for dc in range(8):
                        ev_m = p.run("tensor", lambda h, j=j, dc=dc, tt=tt: h.matmul(
                            pP, xnT[:, dc, tt * 128:(tt + 1) * 128],
                            wk4[:, dc, j * 512:(j + 1) * 512],
                            start=(dc == 0), stop=(dc == 7)), inc=(dc == 7))
                    p.waits("scalar", [ev_m])
                    if j == 0:
                        ev_pP = p.run("scalar", lambda h, tt=tt: h.copy(out=knat[:, tt, :], in_=pP))
                    elif j == 1:
                        p.run("scalar", lambda h, tt=tt: h.copy(out=vnat[:, tt, :], in_=pP))
                        ev_pP = p.run("scalar", lambda h, tt=tt: h.activation(
                            out=bvnat[:, tt, :], in_=pP, func=AF.Identity,
                            scale=bcol[:, tt:tt + 1]))
                    elif j == 2:
                        ev_pP = p.run("scalar", lambda h, tt=tt: h.activation(
                            out=rsig[:, tt, :], in_=pP, func=AF.Sigmoid))
                    else:
                        ev_pP = p.run("scalar", lambda h, tt=tt: h.activation(
                            out=gsig[:, tt, :], in_=pP, func=AF.Sigmoid))
                p.waits("tensor", [ev_pP, ev_pP2])
                ev_m = p.run("tensor", lambda h, tt=tt: h.matmul(
                    pP2, dtrawT[:, tt * 128:(tt + 1) * 128], dtw[:, :],
                    start=True, stop=True), inc=1)
                p.waits("vector", [ev_m])
                ev_vv = p.run("vector", lambda h: h.tensor_tensor(st1[:, :], pP2, dbrep[:, :], ALU.add))
                p.waits("scalar", [ev_vv])
                p.dwait("scalar", name + "cso")
                p.run("scalar", lambda h: h.activation(out=wtmp[:, 0:512], in_=st1[:, :], func=AF.Abs))
                p.run("scalar", lambda h: h.activation(out=wtmp[:, 0:512], in_=wtmp[:, 0:512],
                                                       func=AF.Exp, scale=-1.0))
                p.waits("scalar", [cc.ev(1.0)])
                ev_sp = p.run("scalar", lambda h: h.activation(out=wtmp[:, 0:512], in_=wtmp[:, 0:512],
                                                               func=AF.Ln, bias=cc.ap(1.0)))
                p.waits("vector", [ev_sp])
                p.run("vector", lambda h: h.tensor_scalar_max(st1[:, :], st1[:, :], 0.0))
                p.run("vector", lambda h: h.tensor_tensor(st1[:, :], st1[:, :], wtmp[:, 0:512], ALU.add))
                ev_pP2 = p.run("vector", lambda h, tt=tt: h.scalar_tensor_tensor(
                    logdn[:, tt, :], st1[:, :], -1.0, bdrep[:, :], ALU.mult, ALU.mult))

            for ck in range(nck if STOP != 1 else 0):
                tt = ck // 4
                pb = (ck % 4) * 32
                t0 = ck * C
                rows = slice(pb, pb + 32)
                p.waits("tensor", [ev_Lev, ev_kh, ev_tri, ev_pP2, ev_Aev])
                ev_mL = p.run("tensor", lambda h, rows=rows, tt=tt, pb=pb: h.matmul(
                    pL[rows, :], TRI4[rows, :], logdn[rows, tt, :],
                    start=True, stop=True, tile_position=(pb, pb)), inc=1)
                ev_mS = p.run("tensor", lambda h, rows=rows, tt=tt, pb=pb: h.matmul(
                    pS2[rows, :], ONES32[rows, :], logdn[rows, tt, :],
                    start=True, stop=True, tile_position=(pb, pb)), inc=1)
                p.waits("scalar", [ev_mL])
                ev_Lev = p.run("scalar", lambda h, rows=rows: h.copy(out=Ltok[rows, :],
                                                                     in_=pL[rows, :]))
                p.waits("vector", [ev_Lev, ev_mS, ev_U_mm])
                ev_k1 = p.run("vector", lambda h, rows=rows: h.tensor_tensor(
                    st1[rows, :], pS2[rows, :], Ltok[rows, :], ALU.subtract))
                p.waits("scalar", [ev_k1])
                ev_k2 = p.run("scalar", lambda h, rows=rows: h.activation(
                    out=st1[rows, :], in_=st1[rows, :], func=AF.Exp))
                p.waits("vector", [ev_k2])
                ev_kh = p.run("vector", lambda h, rows=rows, tt=tt: h.tensor_tensor(
                    khat[rows, :], st1[rows, :], knat[rows, tt, :], ALU.mult))
                p.waits("tensor", [ev_Lcm_ev, ev_Lev, ev_msk])
                for blk in range(4):
                    ev_tb = p.run("tensor", lambda h, blk=blk, rows=rows, pb=pb: h.matmul(
                        pTr[:, blk * 32:(blk + 1) * 32], Ltok[rows, blk * 128:(blk + 1) * 128],
                        IDR32[rows, :], start=True, stop=True, tile_position=(pb, 0)),
                        inc=(blk == 3))
                p.waits("scalar", [ev_tb, ev_fac])
                ev_Lcm_ev = p.run("scalar", lambda h: h.copy(
                    out=Lcm[:, :, :], in_=pTr[:, 0:128].rearrange("p (a b) -> p a b", a=4)))
                p.waits("scalar", [ev_Lcm_ev])
                ev_de = p.run("scalar", lambda h: h.activation(out=Dend[:, :],
                                                               in_=Lcm[:, :, C - 1], func=AF.Exp))
                ev_e1 = p.run("scalar", lambda h: h.activation(
                    out=e2[:, :, :], in_=Lcm[:, :, :], func=AF.Exp))
                p.waits("vector", [ev_e1, ev_U_mm, ev_intra])
                ev_qt = p.run("vector", lambda h, t0=t0: h.tensor_tensor(
                    Qt[:, :, :], e2[:, :, :], kT[:, :, t0:t0 + C], ALU.mult))
                ev_q2 = None
                for sub in range(2):
                    mid = sub * 16 + 7
                    ssl = slice(sub * 16, (sub + 1) * 16)
                    p.waits("vector", [ev_q2] if ev_q2 else [])
                    p.run("vector", lambda h, ssl=ssl, mid=mid: h.tensor_tensor(
                        e2[:, :, ssl], Lcm[:, :, ssl],
                        Lcm[:, :, mid:mid + 1].to_broadcast([128, 4, 16]), ALU.subtract))
                    p.run("vector", lambda h, ssl=ssl: h.tensor_scalar(
                        e2[:, :, ssl], e2[:, :, ssl], CLAMP, -CLAMP, ALU.min, ALU.max))
                    ev_q2 = p.run("vector", lambda h, ssl=ssl: h.tensor_scalar_mul(
                        e3[:, :, ssl], e2[:, :, ssl], -1.0))
                p.waits("scalar", [ev_q2])
                p.run("scalar", lambda h: h.activation(out=e2[:, :, :], in_=e2[:, :, :], func=AF.Exp))
                ev_se = p.run("scalar", lambda h: h.activation(out=e3[:, :, :], in_=e3[:, :, :],
                                                               func=AF.Exp))
                p.waits("vector", [ev_se])
                p.run("vector", lambda h, t0=t0: h.tensor_tensor(
                    Q2[:, :, :], e2[:, :, :], kT[:, :, t0:t0 + C], ALU.mult))
                ev_f1 = p.run("vector", lambda h, t0=t0: h.tensor_tensor(
                    K2[:, :, :], e3[:, :, :], kT[:, :, t0:t0 + C], ALU.mult))
                p.run("vector", lambda h: h.tensor_tensor(
                    e2[:, :, 0:16], Lcm[:, :, 16:32],
                    Lcm[:, :, 15:16].to_broadcast([128, 4, 16]), ALU.subtract))
                ev_f2 = p.run("vector", lambda h: h.tensor_tensor(
                    e2[:, :, 16:32], Lcm[:, :, 15:16].to_broadcast([128, 4, 16]),
                    Lcm[:, :, 0:16], ALU.subtract))
                p.waits("scalar", [ev_f2])
                ev_f3 = p.run("scalar", lambda h: h.activation(out=e2[:, :, :], in_=e2[:, :, :],
                                                               func=AF.Exp))
                p.waits("vector", [ev_f3])
                p.run("vector", lambda h, t0=t0: h.tensor_tensor(
                    Q3[:, :, :], e2[:, :, 0:16], kT[:, :, t0 + 16:t0 + C], ALU.mult))
                ev_fac = p.run("vector", lambda h, t0=t0: h.tensor_tensor(
                    K3[:, :, :], e2[:, :, 16:32], kT[:, :, t0:t0 + 16], ALU.mult))
                p.waits("tensor", [ev_qt, ev_ys_ev, ev_S_upd, ev_fac, ev_pP])
                for hh in range(NH if STOP != 4 else 0):
                    rb = (hh % 2) * 64
                    bank = pW if rb == 0 else pP
                    p.run("tensor", lambda h, hh=hh, rb=rb, rows=rows, pb=pb, bank=bank: h.matmul(
                        bank[rows, hh * 64:(hh + 1) * 64],
                        Qt[rb:rb + 64, hh // 2, :], S[rb:rb + 64, hh // 2, :],
                        start=(hh < 2), stop=(hh >= NH - 2), tile_position=(rb, pb)))
                for hh in range(NH if STOP != 2 else 0):
                    rb = (hh % 2) * 64
                    p.waits("tensor", [ev_Aev, ev_ys_ev])
                    p.run("tensor", lambda h, hh=hh, rb=rb, rows=rows, pb=pb: h.matmul(
                        pA[rows, 0:32], K2[rb:rb + 64, hh // 2, :], Q2[rb:rb + 64, hh // 2, :],
                        start=True, stop=True, tile_position=(rb, pb)))
                    ev_am = p.run("tensor", lambda h, hh=hh, rb=rb, pb=pb: h.matmul(
                        pL[pb:pb + 16, 448:464], K3[rb:rb + 64, hh // 2, :],
                        Q3[rb:rb + 64, hh // 2, :],
                        start=True, stop=True, tile_position=(rb, pb)), inc=1)
                    p.waits("vector", [ev_am])
                    p.run("vector", lambda h, hh=hh, rows=rows: h.scalar_tensor_tensor(
                        Asb[rows, hh, :], pA[rows, 0:32], 1.0, mask2r[rows, :],
                        ALU.bypass, ALU.mult))
                    ev_Aev = p.run("vector", lambda h, hh=hh, pb=pb: h.tensor_copy(
                        Asb[pb:pb + 16, hh, 16:32], pL[pb:pb + 16, 448:464]))
                p.waits("tensor", [ev_Aev])
                for hh in range(NH if STOP not in (2, 5) else 0):
                    ev_intra = p.run("tensor", lambda h, hh=hh, rows=rows, tt=tt, pb=pb: h.matmul(
                        pA[rows, hh * 64:(hh + 1) * 64], Asb[rows, hh, :],
                        bvnat[rows, tt, hh * 64:(hh + 1) * 64],
                        start=(hh == 0), stop=(hh == NH - 1), tile_position=(pb, pb)),
                        inc=(hh == NH - 1))
                p.waits("tensor", [ev_kh, ev_S_upd])
                for hh in range(NH if STOP != 3 else 0):
                    rb = (hh % 2) * 64
                    ev_U_mm = p.run("tensor", lambda h, hh=hh, rb=rb, rows=rows, tt=tt, pb=pb:
                                    h.matmul(
                        pU[rb:rb + 64, (hh // 2) * 64:(hh // 2) * 64 + 64],
                        khat[rows, hh * 64:(hh + 1) * 64],
                        bvnat[rows, tt, hh * 64:(hh + 1) * 64],
                        start=True, stop=True, tile_position=(pb, rb)),
                        inc=(hh == NH - 1))
                p.waits("vector", [ev_U_mm, ev_de])
                for g in range(4):
                    ev_S_upd = p.run("vector", lambda h, g=g: h.scalar_tensor_tensor(
                        S[:, g, :], S[:, g, :], Dend[:, g:g + 1], pU[:, g * 64:(g + 1) * 64],
                        ALU.mult, ALU.add))
                p.waits("vector", [ev_intra, ev_ysT_ev])
                p.run("vector", lambda h, rows=rows, tt=tt: h.tensor_tensor(
                    st1[rows, :], knat[rows, tt, :], vnat[rows, tt, :], ALU.mult))
                p.run("vector", lambda h, rows=rows: h.tensor_tensor(
                    st1[rows, :], st1[rows, :], tfrep[rows, :], ALU.mult))
                st1v = st1[rows, :].rearrange("p (a b) -> p a b", a=4)
                pWv = pW[rows, :].rearrange("p (a b) -> p a b", a=4)
                pPv = pP[rows, :].rearrange("p (a b) -> p a b", a=4)
                p.run("vector", lambda h, st1v=st1v, pWv=pWv: h.tensor_tensor(
                    st1v[:, :, 0:64], pWv[:, :, 0:64], st1v[:, :, 0:64], ALU.add))
                p.run("vector", lambda h, st1v=st1v, pPv=pPv: h.tensor_tensor(
                    st1v[:, :, 64:128], pPv[:, :, 64:128], st1v[:, :, 64:128], ALU.add))
                p.run("vector", lambda h, rows=rows: h.tensor_tensor(
                    st1[rows, :], pA[rows, :], st1[rows, :], ALU.add))
                p.run("vector", lambda h, rows=rows, tt=tt: h.tensor_tensor(
                    st1[rows, :], st1[rows, :], rsig[rows, tt, :], ALU.mult))
                ev_ys_ev = p.run("vector", lambda h, rows=rows, tt=tt: h.tensor_tensor(
                    ysn[rows, tt, :], st1[rows, :], gsig[rows, tt, :], ALU.mult))

            for tt in range(ntt):
                it = tg0 // 128 + tt
                p.waits("tensor", [ev_ys_ev, ev_ysT_ev, ev_ptuse])
                for vb in range(4):
                    ev_tb = p.run("tensor", lambda h, tt=tt, vb=vb: h.transpose(
                        ptr[:, vb * 128:(vb + 1) * 128], ysn[:, tt, vb * 128:(vb + 1) * 128],
                        ident[:]), inc=(vb == 3))
                p.waits("scalar", [ev_tb])
                ev_ysT_ev = p.run("scalar", lambda h, tt=tt, tg0=tg0: h.copy(
                    out=ysT[:, :, tg0 % SEG + tt * 128:tg0 % SEG + (tt + 1) * 128]
                    if False else ysT[:, :, tt * 128:(tt + 1) * 128],
                    in_=ptr[:, 0:512].rearrange("p (a b) -> p a b", a=4)))
            for tt in range(ntt):
                it = tg0 // 128 + tt
                par = tt % 2
                xt = xt2[par]
                cx = name + "cxb%d" % par
                p.waits("sync", [ev_epi, ev_xuse[par]])
                p.dma("sync", xt[:, :], x_in[it * 128:(it + 1) * 128, :], chan=cx)
                p.waits("tensor", [ev_ysT_ev, ev_md, ev_epi])
                for kc in range(4):
                    for nb2 in range(2):
                        ev_md = p.run("tensor", lambda h, kc=kc, nb2=nb2, tt=tt: h.matmul(
                            pP if nb2 == 0 else pP2, ysT[:, kc, tt * 128:(tt + 1) * 128],
                            wo_sb[:, kc, nb2 * 512:(nb2 + 1) * 512],
                            start=(kc == 0), stop=(kc == 3)), inc=(kc == 3 and nb2 == 1))
                p.dwait("scalar", name + "cso")
                p.dwait("scalar", cx)
                p.waits("scalar", [ev_md])
                p.run("scalar", lambda h: h.copy(out=wtmp[:, 0:512], in_=pP))
                ev_ep = p.run("scalar", lambda h: h.copy(out=wtmp[:, 512:1024], in_=pP2))
                p.waits("vector", [ev_ep])
                ev_epi = p.run("vector", lambda h, xt=xt: h.tensor_tensor(
                    wtmp[:, 0:1024], wtmp[:, 0:1024], xt[:], ALU.add))
                ev_xuse[par] = ev_epi
                p.waits("sync", [ev_epi])
                p.dma("sync", x_out[it * 128:(it + 1) * 128, :], wtmp[:, 0:1024], chan=name + "cso")
            ev_pP = ev_ep
            ev_pP2 = ev_epi
        p.dwait("sync", name + "cso")

    return build_l1attnb

build_l1attnb = _mk_build_l1attnb()


from concourse.bass_utils import run_bass_kernel_spmd
from concourse.replica_groups import maybe_share_collective_output_space

_T = 4096
_D = 1024
_NC = 8


def _prog_l0attna():
    nc = bass.Bass("TRN2", target_bir_lowering=False)
    shapes = {"ln_scale": [_D], "ln_bias": [_D], "W_hidden": [_D, 4096], "b_hidden": [4096],
              "W_qk": [_D, 128], "b_qk": [128], "os_weight": [2, 128], "os_bias": [2, 128],
              "W_out": [2048, _D], "b_out": [_D]}
    x_in = nc.dram_tensor("x_in", [_T, _D], F32, kind="ExternalInput").ap()
    waps = {k: nc.dram_tensor(k, s, F32, kind="ExternalInput").ap() for k, s in shapes.items()}
    x_out = nc.dram_tensor("x_out", [_T, _D], F32, kind="ExternalOutput").ap()
    v_scr = nc.dram_tensor("v_scr", [_T, 2048], BF16, kind="Internal").ap()
    p = Prog(nc)
    t = Tiles(nc)
    cc = ConstCache(p, t)
    consts = build_consts(p, t)
    build_l0attna(p, t, cc, consts, x_in, x_out, v_scr, waps)
    p.emit()
    t.close()
    return nc


def _prog_l0ffn():
    nc = bass.Bass("TRN2", target_bir_lowering=False)
    x_in = nc.dram_tensor("x_in", [_T, _D], F32, kind="ExternalInput").ap()
    wq = nc.dram_tensor("wq", [_D, _D], I8, kind="ExternalInput").ap()
    wscale = nc.dram_tensor("wscale", [_D], F32, kind="ExternalInput").ap()
    wbias = nc.dram_tensor("wbias", [_D], F32, kind="ExternalInput").ap()
    x_out = nc.dram_tensor("x_out", [_T, _D], F32, kind="ExternalOutput").ap()
    p = Prog(nc)
    t = Tiles(nc)
    cc = ConstCache(p, t)
    build_l0ffn(p, t, cc, x_in, x_out, wq, wscale, wbias)
    p.emit()
    t.close()
    return nc


def _prog_l0attnb():
    nc = bass.Bass("TRN2", target_bir_lowering=False)
    shapes = {"wq": [16, _D, 64], "wk": [16, _D, 64], "wv": [16, _D, 64],
              "bq": [16, 1, 64], "bk": [16, 1, 64], "bv": [16, 1, 64],
              "wo": [_D, _D], "bo": [_D]}
    x_in = nc.dram_tensor("x_in", [_T, _D], F32, kind="ExternalInput").ap()
    waps = {k: nc.dram_tensor(k, s, F32, kind="ExternalInput").ap() for k, s in shapes.items()}
    x_out = nc.dram_tensor("x_out", [_T, _D], F32, kind="ExternalOutput").ap()
    v_scr = nc.dram_tensor("v_scr", [16, _T, 65], BF16, kind="Internal").ap()
    p = Prog(nc)
    t = Tiles(nc)
    cc = ConstCache(p, t)
    consts = build_consts(p, t)
    build_l0attnb(p, t, cc, consts, x_in, x_out, v_scr, waps)
    p.emit()
    t.close()
    return nc


def _prog_l1attna():
    nc = bass.Bass("TRN2", target_bir_lowering=False)
    shapes = {"W_in": [_D, _D], "W_out": [_D, _D], "Wp": [3 * _D, _D], "bp": [3 * _D],
              "alpha_log": [_D], "delta_bias": [_D], "log_gamma": [1]}
    x_in = nc.dram_tensor("x_in", [_T, _D], F32, kind="ExternalInput").ap()
    waps = {k: nc.dram_tensor(k, s, F32, kind="ExternalInput").ap() for k, s in shapes.items()}
    x_out = nc.dram_tensor("x_out", [_T, _D], F32, kind="ExternalOutput").ap()
    p = Prog(nc)
    t = Tiles(nc)
    cc = ConstCache(p, t)
    consts = build_consts(p, t)
    build_l1attna(p, t, cc, consts, x_in, x_out, waps)
    p.emit()
    t.close()
    return nc


def _prog_l1ffn():
    nc = bass.Bass("TRN2", target_bir_lowering=False, num_devices=_NC)
    shapes = {"Wq": [4, 256, _D], "bq": [4, 256], "gamma": [4, 256], "beta": [4, 256],
              "sk1": [128, 128], "sk2": [128, 128], "latents": [16384, 256],
              "W1": [256, 512], "W2": [512, 2048]}
    x_in = nc.dram_tensor("x_in", [_T, _D], F32, kind="ExternalInput").ap()
    waps = {k: nc.dram_tensor(k, s, F32, kind="ExternalInput").ap() for k, s in shapes.items()}
    x_out = nc.dram_tensor("x_out", [_T, _D], F32, kind="ExternalOutput").ap()
    h_scr = nc.dram_tensor("h_scr", [16384, 512], BF16, kind="Internal").ap()
    q_scr = nc.dram_tensor("q_scr", [8, 128, _T], F32, kind="Internal").ap()
    xl_scr = nc.dram_tensor("xl_scr", [8, 128, _T], BF16, kind="Internal").ap()
    cc_in = nc.dram_tensor("cc_in", [128, 16], F32, kind="Internal").ap()
    aspace = maybe_share_collective_output_space("AllReduce", [list(range(_NC))])
    cc_out = nc.dram_tensor("cc_out", [128, 16], F32, kind="Internal",
                            addr_space=aspace).ap()
    p = Prog(nc)
    t = Tiles(nc)
    cc = ConstCache(p, t)
    consts = build_consts(p, t)
    build_l1ffn(p, t, cc, consts, x_in, x_out, h_scr, q_scr, xl_scr, cc_in, cc_out, waps)
    p.emit()
    t.close()
    return nc


def _prog_l1attnb():
    nc = bass.Bass("TRN2", target_bir_lowering=False)
    shapes = {"W_sel": [_D, 160], "dt_W": [32, 512], "dt_b": [512],
              "Wk": [_D, 512], "Wv": [_D, 512], "Wr": [_D, 512], "Wg": [_D, 512],
              "Wo": [512, _D], "base_decay": [8, 64], "time_first": [8, 64]}
    x_in = nc.dram_tensor("x_in", [_T, _D], F32, kind="ExternalInput").ap()
    waps = {k: nc.dram_tensor(k, s, F32, kind="ExternalInput").ap() for k, s in shapes.items()}
    x_out = nc.dram_tensor("x_out", [_T, _D], F32, kind="ExternalOutput").ap()
    p = Prog(nc)
    t = Tiles(nc)
    cc = ConstCache(p, t)
    consts = build_consts(p, t)
    build_l1attnb(p, t, cc, consts, x_in, x_out, waps)
    p.emit()
    t.close()
    return nc


_A0A_SHAPES = {"ln_scale": [_D], "ln_bias": [_D], "W_hidden": [_D, 4096], "b_hidden": [4096],
               "W_qk": [_D, 128], "b_qk": [128], "os_weight": [2, 128], "os_bias": [2, 128],
               "W_out": [2048, _D], "b_out": [_D]}
_A0B_SHAPES = {"wq": [16, _D, 64], "wk": [16, _D, 64], "wv": [16, _D, 64],
               "bq": [16, 1, 64], "bk": [16, 1, 64], "bv": [16, 1, 64],
               "wo": [_D, _D], "bo": [_D]}
_A1A_SHAPES = {"W_in": [_D, _D], "W_out": [_D, _D], "Wp": [3 * _D, _D], "bp": [3 * _D],
               "alpha_log": [_D], "delta_bias": [_D], "log_gamma": [1]}
_F1_SHAPES = {"Wq": [4, 256, _D], "bq": [4, 256], "gamma": [4, 256], "beta": [4, 256],
              "sk1": [128, 128], "sk2": [128, 128], "latents": [16384, 256],
              "W1": [256, 512], "W2": [512, 2048]}
_A1B_SHAPES = {"W_sel": [_D, 160], "dt_W": [32, 512], "dt_b": [512],
               "Wk": [_D, 512], "Wv": [_D, 512], "Wr": [_D, 512], "Wg": [_D, 512],
               "Wo": [512, _D], "base_decay": [8, 64], "time_first": [8, 64]}


def _build_fused():
    nc = bass.Bass("TRN2", target_bir_lowering=False, num_devices=_NC)

    def dr(n, s, dt=F32):
        return nc.dram_tensor(n, s, dt, kind="ExternalInput").ap()

    x_in = dr("x_in", [_T, _D])
    x_out = nc.dram_tensor("x_out", [_T, _D], F32, kind="ExternalOutput").ap()
    chain = [x_in] + [nc.dram_tensor("xc%d" % i, [_T, _D], F32, kind="Internal").ap()
                      for i in range(7)] + [x_out]
    wa0a = {k: dr("a0a_" + k, s) for k, s in _A0A_SHAPES.items()}
    f0_wq = dr("f0_wq", [_D, _D], I8)
    f0_ws = dr("f0_wscale", [_D])
    f0_wb = dr("f0_wbias", [_D])
    wa0b = {k: dr("a0b_" + k, s) for k, s in _A0B_SHAPES.items()}
    wa1a = {k: dr("a1a_" + k, s) for k, s in _A1A_SHAPES.items()}
    wf1 = {k: dr("f1_" + k, s) for k, s in _F1_SHAPES.items()}
    wa1b = {k: dr("a1b_" + k, s) for k, s in _A1B_SHAPES.items()}

    v_scr_a = nc.dram_tensor("v_scr_a", [_T, 2048], BF16, kind="Internal").ap()
    v_scr_b = nc.dram_tensor("v_scr_b", [16, _T, 65], BF16, kind="Internal").ap()
    h_scr = nc.dram_tensor("h_scr", [16384, 512], BF16, kind="Internal").ap()
    q_scr = nc.dram_tensor("q_scr", [8, 128, _T], F32, kind="Internal").ap()
    xl_scr = nc.dram_tensor("xl_scr", [8, 128, _T], BF16, kind="Internal").ap()
    cc_in = nc.dram_tensor("cc_in", [128, 16], F32, kind="Internal").ap()
    aspace = maybe_share_collective_output_space("AllReduce", [list(range(_NC))])
    cc_out = nc.dram_tensor("cc_out", [128, 16], F32, kind="Internal",
                            addr_space=aspace).ap()
    cc_in2 = nc.dram_tensor("cc_in2", [128, 16], F32, kind="Internal").ap()
    cc_out2 = nc.dram_tensor("cc_out2", [128, 16], F32, kind="Internal",
                             addr_space=aspace).ap()

    p = Prog(nc)
    t = Tiles(nc)
    cc = ConstCache(p, t)
    consts = build_consts(p, t)
    for v in (-15.0, 1.0, 1e-10, 1e-12, 1e-5):
        cc.get(v)
    cc.freeze()

    phases = [
        lambda t2, xi, xo: build_l0attna(p, t2, cc, consts, xi, xo, v_scr_a, wa0a,
                                         name="a0a"),
        lambda t2, xi, xo: build_l0ffn(p, t2, cc, xi, xo, f0_wq, f0_ws, f0_wb,
                                       name="f0a", consts=consts),
        lambda t2, xi, xo: build_l0attnb(p, t2, cc, consts, xi, xo, v_scr_b, wa0b,
                                         name="a0b"),
        lambda t2, xi, xo: build_l0ffn(p, t2, cc, xi, xo, f0_wq, f0_ws, f0_wb,
                                       name="f0b", consts=consts),
        lambda t2, xi, xo: build_l1attna(p, t2, cc, consts, xi, xo, wa1a, name="a1a"),
        lambda t2, xi, xo: build_l1ffn(p, t2, cc, consts, xi, xo, h_scr, q_scr, xl_scr,
                                       cc_in, cc_out, wf1, name="f1a"),
        lambda t2, xi, xo: build_l1attnb(p, t2, cc, consts, xi, xo, wa1b, name="a1b"),
        lambda t2, xi, xo: build_l1ffn(p, t2, cc, consts, xi, xo, h_scr, q_scr, xl_scr,
                                       cc_in2, cc_out2, wf1, name="f1b",
                                       skip_hall=True),
    ]
    for i, ph in enumerate(phases):
        t2 = PhaseTiles(t)
        ph(t2, chain[i], chain[i + 1])
        barrier(p)
        t2.close()
    p.emit()
    t.close()
    return nc


_FUSED = None


def _get_fused():
    global _FUSED
    if _FUSED is None:
        _FUSED = _build_fused()
    return _FUSED


_PROGS = None


def _get_progs():
    global _PROGS
    if _PROGS is None:
        _PROGS = {
            "a0a": _prog_l0attna(),
            "f0": _prog_l0ffn(),
            "a0b": _prog_l0attnb(),
            "a1a": _prog_l1attna(),
            "f1": _prog_l1ffn(),
            "a1b": _prog_l1attnb(),
        }
    return _PROGS


def _f32(a):
    return np.ascontiguousarray(np.asarray(a, dtype=np.float32))


def kernel(**inputs):
    progs = _get_fused()
    x = _f32(inputs["L0_attn_a_x"])
    xs = [np.ascontiguousarray(x[c]) for c in range(_NC)]

    w_a0a = {
        "ln_scale": _f32(inputs["L0_attn_a_ln_scale"]),
        "ln_bias": _f32(inputs["L0_attn_a_ln_bias"]),
        "W_hidden": _f32(inputs["L0_attn_a_W_hidden"]),
        "b_hidden": _f32(inputs["L0_attn_a_b_hidden"]),
        "W_qk": _f32(inputs["L0_attn_a_W_qk"]),
        "b_qk": _f32(inputs["L0_attn_a_b_qk"]),
        "os_weight": _f32(inputs["L0_attn_a_os_weight"]),
        "os_bias": _f32(inputs["L0_attn_a_os_bias"]),
        "W_out": _f32(inputs["L0_attn_a_W_out"]),
        "b_out": _f32(inputs["L0_attn_a_b_out"]),
    }
    w_f0 = {
        "wq": np.ascontiguousarray(np.asarray(inputs["L0_ffn_weight_q"], dtype=np.int8)),
        "wscale": _f32(inputs["L0_ffn_weight_scale"]),
        "wbias": _f32(inputs["L0_ffn_bias"]),
    }
    w_a0b = {
        "wq": _f32(inputs["L0_attn_b_wq"]),
        "bq": _f32(inputs["L0_attn_b_bq"]),
        "wk": _f32(inputs["L0_attn_b_wk"]),
        "bk": _f32(inputs["L0_attn_b_bk"]),
        "wv": _f32(inputs["L0_attn_b_wv"]),
        "bv": _f32(inputs["L0_attn_b_bv"]),
        "wo": _f32(inputs["L0_attn_b_wo"]),
        "bo": _f32(inputs["L0_attn_b_bo"]),
    }
    w_a1a = {
        "W_in": _f32(inputs["L1_attn_a_W_in"]),
        "W_out": _f32(inputs["L1_attn_a_W_out"]),
        "Wp": _f32(inputs["L1_attn_a_Wp"]),
        "bp": _f32(inputs["L1_attn_a_bp"]),
        "alpha_log": _f32(inputs["L1_attn_a_alpha_log"]),
        "delta_bias": _f32(inputs["L1_attn_a_delta_bias"]),
        "log_gamma": _f32(inputs["L1_attn_a_log_gamma"]).reshape(1),
    }
    w_f1 = {
        "Wq": _f32(inputs["L1_ffn_Wq"]),
        "bq": _f32(inputs["L1_ffn_bq"]),
        "gamma": _f32(inputs["L1_ffn_gamma"]),
        "beta": _f32(inputs["L1_ffn_beta"]),
        "sk1": _f32(inputs["L1_ffn_sk1"]),
        "sk2": _f32(inputs["L1_ffn_sk2"]),
        "latents": _f32(inputs["L1_ffn_latents"]),
        "W1": _f32(inputs["L1_ffn_W1"]),
        "W2": _f32(inputs["L1_ffn_W2"]),
    }
    w_a1b = {
        "W_sel": _f32(inputs["L1_attn_b_W_sel"]),
        "dt_W": _f32(inputs["L1_attn_b_dt_W"]),
        "dt_b": _f32(inputs["L1_attn_b_dt_b"]),
        "Wk": _f32(inputs["L1_attn_b_Wk"]),
        "Wv": _f32(inputs["L1_attn_b_Wv"]),
        "Wr": _f32(inputs["L1_attn_b_Wr"]),
        "Wg": _f32(inputs["L1_attn_b_Wg"]),
        "Wo": _f32(inputs["L1_attn_b_Wo"]),
        "base_decay": _f32(inputs["L1_attn_b_base_decay"]),
        "time_first": _f32(inputs["L1_attn_b_time_first"]),
    }

    wall = {}
    for pre, wmap in (("a0a_", w_a0a), ("f0_", {"wq": w_f0["wq"], "wscale": w_f0["wscale"],
                                                "wbias": w_f0["wbias"]}),
                      ("a0b_", w_a0b), ("a1a_", w_a1a), ("f1_", w_f1), ("a1b_", w_a1b)):
        for k, v in wmap.items():
            wall[pre + k] = v
    in_maps = [dict(wall, x_in=xs[c]) for c in range(_NC)]
    res = run_bass_kernel_spmd(progs, in_maps, core_ids=list(range(_NC)))
    return np.stack([np.asarray(res.results[c]["x_out"], np.float32)
                     for c in range(_NC)], 0)
